# version 06
# baseline (speedup 1.0000x reference)
import numpy as np
import ml_dtypes
from contextlib import ExitStack

import concourse.bass as bass
import concourse.mybir as mybir
from concourse.bass_utils import run_bass_kernel_spmd

F32 = mybir.dt.float32
BF16 = mybir.dt.bfloat16
AF = mybir.ActivationFunctionType
ALU = mybir.AluOpType
AX = mybir.AxisListType.X
BF = ml_dtypes.bfloat16

T = 2048
D = 1024
DC = 8
HID = 1536
GC = 12
FH = 4096
FC = 32
NH = 16
QT = 512
NT = 16
EPS = 1e-5
QZEPS = 1e-6
NCORES = 8

LNQ, KFM, VTOK, TNRM, KTOK = 0, 4096, 8192, 12288, 13312
SNP = 14336
KCH, VCH, QCH, SSB, AMSK, VALS, VFM = 0, 2048, 4096, 6144, 8224, 10272, 11296
F_GB, F_TNRM, F_LNX, F_GA = 0, 11264, 12288, 16384
B_N1, B_GG, B_UC, B_SIG, B_SRG, B_TN = 0, 4096, 10240, 16384, 22528, 28672
WVO, WOO, WCOL = 0, 8192, 16384
FW_SL = [0, 8192]
BW_SL = [0, 3072]
BW_SL2 = [0, 3072]
BW_ROWS = 6144
UEXT, S1, S2, S3, S4, S5 = 0, 528, 1040, 1552, 2064, 2576
XA, XB = 0, 1024
ESC1, ESC2 = 2048, 2560
SQSC = 3072


class Sem:
    def __init__(self, ctx, nc, name):
        self.h = ctx.enter_context(nc.semaphore(name))
        self.n = 0

    def inc(self, instr, k=1):
        self.n += k
        instr.then_inc(self.h, k)
        return instr

    def wait(self, eng, n=None):
        eng.wait_ge(self.h, self.n if n is None else n)


_WRAP_OPS = {
    "memset", "memzero", "tensor_copy", "tensor_tensor", "tensor_add",
    "tensor_sub", "tensor_mul", "tensor_max", "tensor_scalar",
    "tensor_single_scalar", "tensor_scalar_add", "tensor_scalar_sub",
    "tensor_scalar_mul", "tensor_scalar_max", "tensor_scalar_min",
    "tensor_relu", "tensor_reduce", "reduce_max", "reduce_sum",
    "scalar_tensor_tensor", "tensor_tensor_scan", "reciprocal", "select",
    "transpose", "bn_stats", "bn_aggr", "activation", "square", "sqrt",
    "mul", "copy", "add", "sign",
}


class _EngProxy:
    def __init__(self, eng, sem):
        self._e = eng
        self._s = sem

    def __getattr__(self, name):
        f = getattr(self._e, name)
        if name in _WRAP_OPS:
            def g(*a, **kw):
                if self._s.n:
                    self._e.wait_ge(self._s.h, self._s.n)
                ins = f(*a, **kw)
                self._s.n += 1
                ins.then_inc(self._s.h, 1)
                return ins
            return g
        return f


class KB:
    def __init__(self, phases=("ffn1", "attn_b", "ffn2")):
        self.phases = phases
        self.nc = bass.Bass("TRN2", target_bir_lowering=False)
        self.xp_free = [0, 0]
        self.wcol_free = [0, 0]

    dbg_lvl = 99

    def load_lnexp(self):
        def f_ac(s):
            s.add_instruction(mybir.InstLoadActFuncSet(
                name=self.nc.get_next_instruction_name(),
                act_func_set_id=6, ins=[], outs=[]))
        self.sec("scalar", f_ac)

    def sec(self, eng, fn):
        if eng == "vector":
            getattr(self.blk, eng)(lambda e: fn(_EngProxy(e, self.dv)))
        elif eng == "scalar":
            getattr(self.blk, eng)(lambda e: fn(_EngProxy(e, self.ac)))
        elif eng == "gpsimd":
            getattr(self.blk, eng)(lambda e: fn(_EngProxy(e, self.gp)))
        else:
            getattr(self.blk, eng)(fn)

    def build(self):
        nc = self.nc
        dt = nc.dram_tensor
        self.xin = dt("xin", [T, D], F32, kind="ExternalInput")
        self.xprev = dt("xprev", [T, D], F32, kind="ExternalInput")
        self.flag = dt("flag", [128, 1], F32, kind="ExternalInput")
        self.sel = dt("sel", [128, 8], F32, kind="ExternalInput")
        self.ident = dt("ident", [128, 128], BF16, kind="ExternalInput")
        self.triu = dt("triu", [128, 128], F32, kind="ExternalInput")
        self.wq = dt("wq", [D, D], F32, kind="ExternalInput")
        self.wk = dt("wk", [D, D], F32, kind="ExternalInput")
        self.wv = dt("wv", [D, D], F32, kind="ExternalInput")
        self.wo = dt("wo", [D, D], F32, kind="ExternalInput")
        self.bq = dt("bq", [128, DC], F32, kind="ExternalInput")
        self.bk = dt("bk", [128, DC], F32, kind="ExternalInput")
        self.bv_r = dt("bv_r", [1, D], F32, kind="ExternalInput")
        self.bo_r = dt("bo_r", [1, D], F32, kind="ExternalInput")
        self.w1t = dt("w1t", [D, FH], F32, kind="ExternalInput")
        self.w2t = dt("w2t", [D, FH], F32, kind="ExternalInput")
        self.h1t = dt("h1t", [FH, FH], F32, kind="ExternalInput")
        self.h2t = dt("h2t", [FH, FH], F32, kind="ExternalInput")
        self.w3t = dt("w3t", [FH, D], F32, kind="ExternalInput")
        self.w1s = dt("w1s", [128, FC], F32, kind="ExternalInput")
        self.w2s = dt("w2s", [128, FC], F32, kind="ExternalInput")
        self.h1s = dt("h1s", [128, FC], F32, kind="ExternalInput")
        self.h2s = dt("h2s", [128, FC], F32, kind="ExternalInput")
        self.w3s_r = dt("w3s_r", [1, D], F32, kind="ExternalInput")
        self.wint = dt("wint", [D, HID], F32, kind="ExternalInput")
        self.wgatet = dt("wgatet", [D, HID], F32, kind="ExternalInput")
        self.rgint = dt("rgint", [HID, HID], F32, kind="ExternalInput")
        self.rggatet = dt("rggatet", [HID, HID], F32, kind="ExternalInput")
        self.woutt = dt("woutt", [HID, D], F32, kind="ExternalInput")
        self.mgatet = dt("mgatet", [D, HID], F32, kind="ExternalInput")
        self.mupt = dt("mupt", [D, HID], F32, kind="ExternalInput")
        self.mdownt = dt("mdownt", [HID, D], F32, kind="ExternalInput")
        self.convw = dt("convw", [128, GC * 4], F32, kind="ExternalInput")
        self.convb = dt("convb", [128, GC], F32, kind="ExternalInput")
        self.rginb = dt("rginb", [128, GC], F32, kind="ExternalInput")
        self.rggateb = dt("rggateb", [128, GC], F32, kind="ExternalInput")
        self.lam = dt("lam", [128, GC], F32, kind="ExternalInput")
        self.n1w_r = dt("n1w_r", [1, D], F32, kind="ExternalInput")
        self.n2w_r = dt("n2w_r", [1, D], F32, kind="ExternalInput")
        self.out = dt("out", [T, D], F32, kind="ExternalOutput")
        self.x2sp = dt("x2sp", [T, D], F32)
        self.ksp = dt("ksp", [16, 128, 1024], BF16)
        self.vsp = dt("vsp", [16, 128, 1024], BF16)
        self.qsp = dt("qsp", [16, 128, 1024], BF16)
        self.p2sp = dt("p2sp", [4, 128, GC * QT], BF16)
        self.c_w1 = dt("c_w1", [FC, 128, DC * 128], BF16)
        self.c_w2 = dt("c_w2", [FC, 128, DC * 128], BF16)
        self.c_h1 = dt("c_h1", [FC, 128, FC * 128], BF16)
        self.c_h2 = dt("c_h2", [FC, 128, FC * 128], BF16)
        self.c_w3 = dt("c_w3", [FC, 128, D], BF16)
        self.cc_ut_in = dt("cc_ut_in", [128, GC * 3], F32)
        self.cc_ut_out = dt("cc_ut_out", [128 * NCORES, GC * 3], F32, addr_space="Shared")
        self.cc_h_in = dt("cc_h_in", [128, GC], F32)
        self.cc_h_out = dt("cc_h_out", [128 * NCORES, GC], F32, addr_space="Shared")

        with ExitStack() as ctx:
            en = ctx.enter_context
            st = nc.sbuf_tensor
            self.xres = en(st("xres", [128, NT * D], F32))
            self.abf = en(st("abf", [128, 32768], BF16))
            self.wbuf = en(st("wbuf", [128, 18432], BF16))
            self.fscr = en(st("fscr", [128, 4096], F32))
            self.cid = en(st("cid", [128, 128], BF16))
            self.ctri = en(st("ctri", [128, 128], F32))
            self.cones = en(st("cones", [128, 1], BF16))
            self.fones = en(st("fones", [128, 1], BF16))
            self.cflag = en(st("cflag", [128, 1], F32))
            self.csel = en(st("csel", [128, 8], F32))
            self.cbq = en(st("cbq", [128, DC], F32))
            self.cbk = en(st("cbk", [128, DC], F32))
            self.cb0 = en(st("cb0", [128, D], F32))
            self.cb1 = en(st("cb1", [128, D], F32))
            self.cw1s = en(st("cw1s", [128, FC], F32))
            self.cw2s = en(st("cw2s", [128, FC], F32))
            self.ch1s = en(st("ch1s", [128, FC], F32))
            self.ch2s = en(st("ch2s", [128, FC], F32))
            self.cconvw = en(st("cconvw", [128, GC * 4], F32))
            self.cconvb = en(st("cconvb", [128, GC], F32))
            self.crginb = en(st("crginb", [128, GC], F32))
            self.crggateb = en(st("crggateb", [128, GC], F32))
            self.clam = en(st("clam", [128, GC], F32))
            self.clamp = en(st("clamp", [128, GC], F32))
            self.czero = en(st("czero", [128, QT], F32))
            self.stats = en(st("stats", [128, 16], F32))
            self.zsb = en(st("zsb", [128, NH], BF16))
            self.ssf = en(st("ssf", [64, 1024], F32))
            self.zsf = en(st("zsf", [64, NH], F32))
            self.uhalo = en(st("uhalo", [128, GC * 3], F32))
            self.utail = en(st("utail", [128, GC * 3], F32))
            self.utg = en(st("utg", [128, 4 * GC * 3], F32))
            self.urecv = en(st("urecv", [128, GC * 3], F32))
            self.hcol = en(st("hcol", [128, GC], F32))
            self.pcol = en(st("pcol", [128, GC], F32))
            self.hg = en(st("hg", [128, 4 * GC], F32))
            self.hrecv = en(st("hrecv", [128, GC], F32))
            self.qzr = en(st("qzr", [128, NH], F32))
            self.cvals = en(st("cvals", [128, 2], F32))
            nc.const_aps.register(F32, EPS, self.cvals[:, 0:1])
            nc.const_aps.register(F32, 1.0, self.cvals[:, 1:2])
            pt = nc.psum_tensor
            self.G = [en(pt(f"G{i}", [128, 512], F32)) for i in range(4)]
            self.PT = en(pt("PTr", [128, 1024], BF16))
            self.PZ = en(pt("PZ", [128, 512], F32))
            self.PSA = en(pt("PSA", [128, 512], F32))
            self.PSB = en(pt("PSB", [128, 512], F32))

            self.pe = Sem(ctx, nc, "pe")
            self.gp = Sem(ctx, nc, "gp")
            self.dv = Sem(ctx, nc, "dv")
            self.ac = Sem(ctx, nc, "ac")
            self.s_x0 = Sem(ctx, nc, "s_x0")
            self.s_const = Sem(ctx, nc, "s_const")
            self.s_wvo = Sem(ctx, nc, "s_wvo")
            self.s_xp = [Sem(ctx, nc, f"s_xp{i}") for i in range(2)]
            self.s_wc = [Sem(ctx, nc, f"s_wc{i}") for i in range(2)]
            self.s_wf = [Sem(ctx, nc, f"s_wf{i}") for i in range(2)]
            self.s_wfc = [Sem(ctx, nc, f"s_wfc{i}") for i in range(2)]
            self.s_w3c = [Sem(ctx, nc, f"s_w3c{i}") for i in range(2)]
            self.s_w3 = [Sem(ctx, nc, f"s_w3{i}") for i in range(2)]
            self.s_wb = [Sem(ctx, nc, f"s_wb{i}") for i in range(2)]
            self.s_rows = Sem(ctx, nc, "s_rows")
            self.s_x2w = Sem(ctx, nc, "s_x2w")
            self.s_x2r = [Sem(ctx, nc, f"s_x2r{i}") for i in range(2)]
            self.s_p2 = Sem(ctx, nc, "s_p2")
            self.s_cc = Sem(ctx, nc, "s_cc")
            self.s_stage = Sem(ctx, nc, "s_stage")
            self.s_gth = Sem(ctx, nc, "s_gth")
            self.s_out = Sem(ctx, nc, "s_out")
            self.s_ksp = Sem(ctx, nc, "s_ksp")
            self.s_x15 = Sem(ctx, nc, "s_x15")
            self.s_kv = [Sem(ctx, nc, f"s_kv{i}") for i in range(2)]

            self.blk = en(nc.Block())
            self.emit_setup()
            self.emit_attn_a()
            if "ffn1" in self.phases:
                self.emit_ffn(first=True)
            if "attn_b" in self.phases:
                self.emit_attn_b()
            if "ffn2" in self.phases:
                self.emit_ffn(first=False)
            self.emit_store()
        return nc

    def emit_setup(self):
        k = self

        def f_sync(sy):
            for t in range(NT):
                k.s_x0.inc(sy.dma_start(out=k.xres[:, t * D:(t + 1) * D],
                                        in_=k.xin[t * 128:(t + 1) * 128, :]), 16)
            for dst, src in [(k.cid, k.ident), (k.ctri, k.triu), (k.cflag, k.flag),
                             (k.csel, k.sel), (k.cbq, k.bq), (k.cbk, k.bk),
                             (k.cw1s, k.w1s), (k.cw2s, k.w2s), (k.ch1s, k.h1s),
                             (k.ch2s, k.h2s), (k.cconvw, k.convw), (k.cconvb, k.convb),
                             (k.crginb, k.rginb), (k.crggateb, k.rggateb),
                             (k.clam, k.lam)]:
                k.s_const.inc(sy.dma_start(out=dst[:], in_=src[:, :]), 16)
            for dst, src in [(k.cb0, k.bv_r), (k.cb1, k.bo_r)]:
                k.s_const.inc(sy.dma_start(out=dst[:], in_=src.ap().broadcast_to([128, D])), 16)
        k.sec("sync", f_sync)

        def f_gp(g):
            for d in range(DC):
                k.s_wvo.inc(g.dma_start(out=k.wbuf[:, WVO + d * D:WVO + (d + 1) * D],
                                        in_=k.wv[d * 128:(d + 1) * 128, :]), 16)
            for v in range(DC):
                k.s_wvo.inc(g.dma_start(out=k.wbuf[:, WOO + v * D:WOO + (v + 1) * D],
                                        in_=k.wo[v * 128:(v + 1) * 128, :]), 16)
        k.sec("gpsimd", f_gp)

        def f_dv(v):
            k.s_const.wait(v)
            v.memset(k.cvals[:, 0:1], EPS)
            v.memset(k.cvals[:, 1:2], 1.0)
            v.memset(k.czero[:], 0.0)
            v.memset(k.cones[:], 1.0)
            v.memset(k.ssf[:], 0.0)
            v.memset(k.zsf[:], 0.0)
            v.memset(k.uhalo[:], 0.0)
            v.memset(k.hcol[:], 0.0)
            v.memset(k.pcol[:], 1.0)
            v.tensor_scalar(k.fones[:], k.cones[:], k.cflag[:, 0:1], None, ALU.mult)
        k.sec("vector", f_dv)

        def f_ac(s):
            k.dv.wait(s)
            s.activation(k.fscr[:, ESC1:ESC1 + GC], k.clam[:], AF.Abs)
            s.activation(k.fscr[:, ESC2:ESC2 + GC], k.fscr[:, ESC1:ESC1 + GC], AF.Exp, scale=-1.0)
        k.sec("scalar", f_ac)

        def f_dv2(v):
            k.ac.wait(v)
            v.tensor_scalar_max(k.fscr[:, ESC1:ESC1 + GC], k.clam[:], 0.0)
        k.sec("vector", f_dv2)

        def f_ac2(s):
            k.dv.wait(s)
            s.activation(k.fscr[:, ESC2:ESC2 + GC], k.fscr[:, ESC2:ESC2 + GC], AF.Ln, bias=1.0)
        k.sec("scalar", f_ac2)

        def f_dv3(v):
            k.ac.wait(v)
            v.tensor_add(k.clamp[:], k.fscr[:, ESC2:ESC2 + GC], k.fscr[:, ESC1:ESC1 + GC])
            v.tensor_scalar_mul(k.clamp[:], k.clamp[:], -8.0)
        k.sec("vector", f_dv3)

    def ln_group(self, items, rms=False, gamma=None):
        k = self
        n = len(items)

        def f_dv(v):
            for i, (src_, pw, _) in enumerate(items):
                if pw is not None:
                    pw[0].wait(v, pw[1])
                if not rms:
                    v.reduce_sum(k.stats[:, i:i + 1], src_, axis=AX)
                v.scalar_tensor_tensor(k.abf[:, self.tnrm:self.tnrm + D], src_, 1.0, src_,
                                       ALU.mult, ALU.mult,
                                       accum_out=k.stats[:, 4 + i:5 + i])
            if rms:
                v.tensor_scalar_mul(k.stats[:, 12:12 + n], k.stats[:, 4:4 + n], 1.0 / D)
            else:
                v.tensor_scalar_mul(k.stats[:, 8:8 + n], k.stats[:, 0:n], 1.0 / D)
                v.tensor_scalar_mul(k.stats[:, 12:12 + n], k.stats[:, 4:4 + n], 1.0 / D)
                v.tensor_tensor(k.stats[:, 0:n], k.stats[:, 8:8 + n],
                                k.stats[:, 8:8 + n], ALU.mult)
                v.tensor_tensor(k.stats[:, 12:12 + n], k.stats[:, 12:12 + n],
                                k.stats[:, 0:n], ALU.subtract)
        k.sec("vector", f_dv)

        def f_ac(s):
            k.dv.wait(s)
            s.activation(k.stats[:, 12:12 + n], k.stats[:, 12:12 + n], AF.Sqrt, bias=EPS)
        k.sec("scalar", f_ac)

        def f_dv2(v):
            k.ac.wait(v)
            v.reciprocal(k.stats[:, 12:12 + n], k.stats[:, 12:12 + n])
        k.sec("vector", f_dv2)

        for i, (src_, _, dst) in enumerate(items):
            def f_dv3(v, i=i, src_=src_):
                if rms:
                    v.tensor_scalar(k.fscr[:, SQSC:SQSC + D], src_,
                                    k.stats[:, 12 + i:13 + i], None, ALU.mult)
                    v.tensor_tensor(k.abf[:, k.tnrm:k.tnrm + D],
                                    k.fscr[:, SQSC:SQSC + D], gamma, ALU.mult)
                else:
                    v.tensor_scalar(k.abf[:, k.tnrm:k.tnrm + D], src_,
                                    k.stats[:, 8 + i:9 + i], k.stats[:, 12 + i:13 + i],
                                    ALU.subtract, ALU.mult)
            k.sec("vector", f_dv3)
            self.tr_tile(self.tnrm, *dst)

    def ln_group_f32(self, items):
        k = self
        n = len(items)

        def f_dv(v):
            for i, (src_, pw, _) in enumerate(items):
                if pw is not None:
                    pw[0].wait(v, pw[1])
                v.reduce_sum(k.stats[:, i:i + 1], src_, axis=AX)
                v.scalar_tensor_tensor(k.abf[:, self.tnrm:self.tnrm + D], src_, 1.0, src_,
                                       ALU.mult, ALU.mult,
                                       accum_out=k.stats[:, 4 + i:5 + i])
            v.tensor_scalar_mul(k.stats[:, 8:8 + n], k.stats[:, 0:n], 1.0 / D)
            v.tensor_scalar_mul(k.stats[:, 12:12 + n], k.stats[:, 4:4 + n], 1.0 / D)
            v.tensor_tensor(k.stats[:, 0:n], k.stats[:, 8:8 + n],
                            k.stats[:, 8:8 + n], ALU.mult)
            v.tensor_tensor(k.stats[:, 12:12 + n], k.stats[:, 12:12 + n],
                            k.stats[:, 0:n], ALU.subtract)
        k.sec("vector", f_dv)

        def f_ac(s):
            k.dv.wait(s)
            s.activation(k.stats[:, 12:12 + n], k.stats[:, 12:12 + n], AF.Sqrt, bias=EPS)
        k.sec("scalar", f_ac)

        def f_dv2(v):
            k.ac.wait(v)
            v.reciprocal(k.stats[:, 12:12 + n], k.stats[:, 12:12 + n])
            for i, (src_, _, dst_) in enumerate(items):
                v.tensor_scalar(dst_, src_, k.stats[:, 8 + i:9 + i],
                                k.stats[:, 12 + i:13 + i], ALU.subtract, ALU.mult)
        k.sec("vector", f_dv2)

    def ln_tile(self, src, out_view, rms=False, gamma=None, pre_wait=None):
        k = self

        def f_dv(v):
            if pre_wait is not None:
                pre_wait[0].wait(v, pre_wait[1])
            if not rms:
                v.reduce_sum(k.stats[:, 0:1], src, axis=AX)
            v.scalar_tensor_tensor(k.fscr[:, SQSC:SQSC + D], src, 1.0, src,
                                            ALU.mult, ALU.mult,
                                            accum_out=k.stats[:, 1:2])
        k.sec("vector", f_dv)

        def f_dv2(v):
            if rms:
                v.tensor_scalar_mul(k.stats[:, 4:5], k.stats[:, 1:2], 1.0 / D)
            else:
                v.tensor_scalar_mul(k.stats[:, 2:3], k.stats[:, 0:1], 1.0 / D)
                v.tensor_scalar_mul(k.stats[:, 3:4], k.stats[:, 1:2], 1.0 / D)
                v.tensor_tensor(k.stats[:, 5:6], k.stats[:, 2:3], k.stats[:, 2:3], ALU.mult)
                v.tensor_tensor(k.stats[:, 4:5], k.stats[:, 3:4],
                                         k.stats[:, 5:6], ALU.subtract)
        k.sec("vector", f_dv2)

        def f_ac(s):
            k.dv.wait(s)
            s.activation(k.stats[:, 6:7], k.stats[:, 4:5], AF.Sqrt, bias=EPS)
        k.sec("scalar", f_ac)

        def f_dv3(v):
            k.ac.wait(v)
            v.reciprocal(k.stats[:, 6:7], k.stats[:, 6:7])
            if rms:
                v.tensor_scalar(k.fscr[:, SQSC:SQSC + D], src, k.stats[:, 6:7], None, ALU.mult)
                v.tensor_tensor(out_view, k.fscr[:, SQSC:SQSC + D], gamma, ALU.mult)
            else:
                v.tensor_scalar(out_view, src, k.stats[:, 2:3], k.stats[:, 6:7],
                                         ALU.subtract, ALU.mult)
        k.sec("vector", f_dv3)

    def tr_tile(self, src_off, fm_off, cs, t_off, nch=DC):
        k = self

        def f_pe(te):
            k.dv.wait(te)
            for d in range(nch):
                ins = te.transpose(k.PT[:, d * 128:(d + 1) * 128],
                                   k.abf[:, src_off + d * 128:src_off + (d + 1) * 128],
                                   k.cid[:])
            k.pe.inc(ins)
        k.sec("tensor", f_pe)

        def f_dv(v):
            k.pe.wait(v)
            dst = k.abf[:, fm_off:fm_off + nch * cs].rearrange(
                "p (c q) -> p c q", q=cs)[:, :, t_off:t_off + 128]
            v.tensor_copy(dst, k.PT[:, 0:nch * 128].rearrange(
                "p (c q) -> p c q", q=128))
        k.sec("vector", f_dv)

    def emit_attn_a(self):
        for q in range(8):
            self._a2_pass1(q)
        self.kv_free = [0, 0]
        for q in range(4):
            self._a2_pass2(q)

    def _a2_pass1(self, q):
        k = self
        own = q >= 4
        qq = q % 4

        k.tnrm = TNRM
        if own:
            items = [(k.xres[:, (qq * 4 + tl) * D:(qq * 4 + tl + 1) * D],
                      (k.s_x0, 16 * NT), (LNQ, QT, tl * 128)) for tl in range(4)]
            self.ln_group(items)
        else:
            for tp in range(2):
                items = []
                for tl in (tp * 2, tp * 2 + 1):
                    t_glob = qq * 4 + tl
                    slot = tl % 2

                    def f_sy(sy, t_glob=t_glob, slot=slot):
                        if k.xp_free[slot] > 0:
                            k.dv.wait(sy, k.xp_free[slot])
                        k.s_xp[slot].inc(sy.dma_start(
                            out=k.fscr[:, slot * 1024:(slot + 1) * 1024],
                            in_=k.xprev[t_glob * 128:(t_glob + 1) * 128, :]), 16)
                    k.sec("sync", f_sy)
                    items.append((k.fscr[:, slot * 1024:(slot + 1) * 1024],
                                  (k.s_xp[slot], k.s_xp[slot].n), (LNQ, QT, tl * 128)))
                self.ln_group(items)
                k.xp_free[0] = k.dv.n
                k.xp_free[1] = k.dv.n

        def f_dv(v):
            k.s_ksp.wait(v)
        k.sec("vector", f_dv)
        self._proj_elu(k.wk, k.cbk, KFM)

        for cl in range(4):
            self._a2_state_chunk(q, own, qq, cl)

        if own:
            def f_dv(v):
                k.s_ksp.wait(v)
            k.sec("vector", f_dv)
            self._proj_elu(k.wq, k.cbq, KFM)
            for cl in range(4):
                idx = qq * 4 + cl

                def f_sy(sy, cl=cl, idx=idx):
                    k.dv.wait(sy)
                    k.s_ksp.inc(sy.dma_start(
                        out=k.qsp[idx],
                        in_=k.abf[:, KFM:KFM + 4096].rearrange(
                            "p (d c) -> p d c", c=QT)[:, :, cl * 128:(cl + 1) * 128]), 16)
                k.sec("sync", f_sy)

    def _a2_state_chunk(self, q, own, qq, cl):
        k = self
        c_off = cl * 128
        g_c = q * 4 + cl

        def f_pe(te, cl=cl):
            k.s_wvo.wait(te)
            k.dv.wait(te)
            for half in range(2):
                ps = k.G[2 + half]
                for d in range(DC):
                    ins = te.matmul(ps[:, :],
                                    k.abf[:, LNQ + d * QT + c_off:LNQ + d * QT + c_off + 128],
                                    k.wbuf[:, WVO + d * D + half * 512:WVO + d * D + (half + 1) * 512],
                                    start=(d == 0), stop=(d == DC - 1))
            k.pe.inc(ins)
        k.sec("tensor", f_pe)
        vproj_done = k.pe.n

        def f_dv(v, cl=cl, own=own, vproj_done=vproj_done):
            k.pe.wait(v, vproj_done)
            for half in range(2):
                ps = k.G[2 + half]
                dst = k.abf[:, VTOK + cl * 1024 + half * 512:VTOK + cl * 1024 + (half + 1) * 512]
                if own:
                    v.tensor_tensor(dst, ps[:, :], k.cb0[:, half * 512:(half + 1) * 512], ALU.add)
                else:
                    v.tensor_tensor(k.fscr[:, ESC1:ESC1 + 512], ps[:, :],
                                    k.cb0[:, half * 512:(half + 1) * 512], ALU.add)
                    v.tensor_scalar(dst, k.fscr[:, ESC1:ESC1 + 512],
                                    k.cflag[:, 0:1], None, ALU.mult)
        k.sec("vector", f_dv)
        v_done = k.dv.n

        def f_pe(te, cl=cl):
            k.dv.wait(te)
            for d in range(DC):
                ins = te.transpose(k.PT[:, d * 128:(d + 1) * 128],
                                   k.abf[:, KFM + d * QT + c_off:KFM + d * QT + c_off + 128],
                                   k.cid[:])
            k.pe.inc(ins)
        k.sec("tensor", f_pe)
        kt_pe = k.pe.n

        def f_dv(v, kt_pe=kt_pe):
            k.pe.wait(v, kt_pe)
            v.tensor_copy(k.abf[:, KTOK:KTOK + 1024], k.PT.ap())
        k.sec("vector", f_dv)
        state_gate = k.dv.n

        def f_pe(te, cl=cl, own=own, state_gate=state_gate):
            k.dv.wait(te, state_gate)
            ones = k.cones if own else k.fones
            for h in range(NH):
                ps = k.PSA if h < 8 else k.PSB
                col = (h % 8) * 64
                te.matmul(ps[0:64, col:col + 64],
                          k.abf[:, KTOK + h * 64:KTOK + (h + 1) * 64],
                          k.abf[:, VTOK + cl * 1024 + h * 64:VTOK + cl * 1024 + (h + 1) * 64],
                          start=(h in (0, 8)), stop=(h in (7, 15)))
                ins = te.matmul(k.PZ[0:64, h:h + 1],
                                k.abf[:, KTOK + h * 64:KTOK + (h + 1) * 64],
                                ones[:], start=(h == 0), stop=(h == 15))
            k.pe.inc(ins)
        k.sec("tensor", f_pe)
        st_pe = k.pe.n

        def f_dv(v, st_pe=st_pe, g_c=g_c):
            k.pe.wait(v, st_pe)
            v.tensor_add(k.ssf[:, 0:512], k.ssf[:, 0:512], k.PSA[0:64, :])
            v.tensor_add(k.ssf[:, 512:1024], k.ssf[:, 512:1024], k.PSB[0:64, :])
            v.tensor_add(k.zsf[:, :], k.zsf[:, :], k.PZ[0:64, 0:NH])
            j = g_c - 15
            if 0 <= j <= 15:
                sv = k.abf[0:64, SNP + j * 1040:SNP + (j + 1) * 1040].rearrange(
                    "p (h c) -> p h c", c=65)
                v.tensor_copy(sv[:, :, 0:64], k.ssf.ap().rearrange("p (h c) -> p h c", c=64))
                v.tensor_copy(sv[:, :, 64:65], k.zsf.ap().rearrange("p (h c) -> p h c", c=1))
        k.sec("vector", f_dv)

        if own:
            idx = qq * 4 + cl

            def f_sy(sy, cl=cl, idx=idx):
                k.dv.wait(sy)
                k.s_ksp.inc(sy.dma_start(
                    out=k.ksp[idx],
                    in_=k.abf[:, KFM:KFM + 4096].rearrange(
                        "p (d c) -> p d c", c=QT)[:, :, cl * 128:(cl + 1) * 128]), 16)
                k.s_ksp.inc(sy.dma_start(
                    out=k.vsp[idx],
                    in_=k.abf[:, VTOK + cl * 1024:VTOK + (cl + 1) * 1024]), 16)
            k.sec("sync", f_sy)

    def _a2_pass2(self, q):
        k = self
        ksp_all = k.s_ksp.n
        kv_free = self.kv_free
        for cl in range(4):
            j = q * 4 + cl
            slot = cl % 2
            t_glob = q * 4 + cl

            def f_sy(sy, j=j, slot=slot):
                if q == 0 and cl < 2:
                    k.s_ksp.wait(sy, ksp_all)
                if kv_free[slot] > 0:
                    k.pe.wait(sy, kv_free[slot])
                k.s_kv[slot].inc(sy.dma_start(
                    out=k.abf[:, KCH + slot * 1024:KCH + (slot + 1) * 1024], in_=k.ksp[j]), 16)
                k.s_kv[slot].inc(sy.dma_start(
                    out=k.abf[:, VCH + slot * 1024:VCH + (slot + 1) * 1024], in_=k.vsp[j]), 16)
                k.s_kv[slot].inc(sy.dma_start(
                    out=k.abf[:, QCH + slot * 1024:QCH + (slot + 1) * 1024], in_=k.qsp[j]), 16)
            k.sec("sync", f_sy)
            kvn = k.s_kv[slot].n

            def f_dv(v, j=j):
                if q == 0 and cl == 0:
                    k.s_ksp.wait(v, ksp_all)
                v.tensor_copy(k.abf[0:64, SSB:SSB + 1040],
                              k.abf[0:64, SNP + j * 1040:SNP + (j + 1) * 1040])
                v.tensor_copy(k.abf[64:128, SSB:SSB + 1040],
                              k.abf[0:64, SNP + j * 1040:SNP + (j + 1) * 1040])
            k.sec("vector", f_dv)
            ssb_done = k.dv.n

            ko = KCH + slot * 1024
            vo = VCH + slot * 1024
            qo = QCH + slot * 1024

            for hb in range(2):
                hs = list(range(hb * 8, hb * 8 + 8))

                def f_pe(te, hs=hs, kvn=kvn, slot=slot):
                    k.s_kv[slot].wait(te, kvn)
                    k.dv.wait(te)
                    for h in hs:
                        pc, row = h // 2, (h % 2) * 64
                        ps = k.G[h % 2]
                        col = ((h % 8) // 2) * 128
                        ins = te.matmul(
                            ps[:, col:col + 128],
                            k.abf[row:row + 64, ko + pc * 128:ko + (pc + 1) * 128],
                            k.abf[row:row + 64, qo + pc * 128:qo + (pc + 1) * 128],
                            start=(h in (hs[0], hs[1])), stop=(h in (hs[6], hs[7])))
                    k.pe.inc(ins)
                k.sec("tensor", f_pe)
                a_pe = k.pe.n

                def f_dv(v, hs=hs, a_pe=a_pe):
                    k.pe.wait(v, a_pe)
                    for h in hs:
                        ps = k.G[h % 2]
                        col = ((h % 8) // 2) * 128
                        v.tensor_tensor(k.abf[:, AMSK + h * 128:AMSK + (h + 1) * 128],
                                        ps[:, col:col + 128], k.ctri[:], ALU.mult)
                k.sec("vector", f_dv)
                amask_done = k.dv.n

                def f_pe(te, hs=hs, amask_done=amask_done, ssb_done=ssb_done, slot=slot):
                    k.dv.wait(te, amask_done)
                    for h in hs:
                        ps = k.G[2 + h % 2]
                        col = ((h % 8) // 2) * 65
                        te.matmul(ps[:, col:col + 64],
                                  k.abf[:, AMSK + h * 128:AMSK + (h + 1) * 128],
                                  k.abf[:, vo + h * 64:vo + (h + 1) * 64],
                                  start=(h in (hs[0], hs[1])), stop=False)
                        te.matmul(ps[:, col + 64:col + 65],
                                  k.abf[:, AMSK + h * 128:AMSK + (h + 1) * 128],
                                  k.cones[:], start=False, stop=False)
                    for h in hs:
                        if h % 2 == 0:
                            pc, row = h // 2, 0
                            col = ((h % 8) // 2) * 65
                            te.matmul(k.G[2][:, col:col + 65],
                                      k.abf[row:row + 64, qo + pc * 128:qo + (pc + 1) * 128],
                                      k.abf[row:row + 64, SSB + h * 65:SSB + (h + 1) * 65],
                                      start=False, stop=(h == hs[6]))
                    for h in hs:
                        if h % 2 == 1:
                            pc, row = h // 2, 64
                            col = ((h % 8) // 2) * 65
                            ins = te.matmul(k.G[3][:, col:col + 65],
                                            k.abf[row:row + 64, qo + pc * 128:qo + (pc + 1) * 128],
                                            k.abf[row:row + 64, SSB + h * 65:SSB + (h + 1) * 65],
                                            start=False, stop=(h == hs[7]))
                    k.pe.inc(ins)
                k.sec("tensor", f_pe)
                o_pe = k.pe.n

                def f_dv(v, hs=hs, o_pe=o_pe):
                    k.pe.wait(v, o_pe)
                    for par in range(2):
                        ps = k.G[2 + par]
                        v.tensor_copy(k.qzr[:, par * 4:par * 4 + 4],
                                      ps[:, 0:260].rearrange("p (h c) -> p h c", c=65)[:, :, 64:65])
                    v.tensor_scalar_add(k.qzr[:, 0:8], k.qzr[:, 0:8], QZEPS)
                    v.reciprocal(k.qzr[:, 0:8], k.qzr[:, 0:8])
                    for h in hs:
                        ps = k.G[2 + h % 2]
                        col = ((h % 8) // 2) * 65
                        qi = (h % 2) * 4 + (h % 8) // 2
                        v.tensor_scalar(k.abf[:, VALS + h * 64:VALS + (h + 1) * 64],
                                        ps[:, col:col + 64], k.qzr[:, qi:qi + 1],
                                        None, ALU.mult)
                k.sec("vector", f_dv)

            self.tr_tile(VALS, VFM, 128, 0)
            vfm_done = k.dv.n

            def f_pe(te, vfm_done=vfm_done):
                k.dv.wait(te, vfm_done)
                for half in range(2):
                    ps = k.G[3] if half == 0 else k.G[0]
                    for vc in range(DC):
                        ins = te.matmul(ps[:, :],
                                        k.abf[:, VFM + vc * 128:VFM + (vc + 1) * 128],
                                        k.wbuf[:, WOO + vc * D + half * 512:WOO + vc * D + (half + 1) * 512],
                                        start=(vc == 0), stop=(vc == DC - 1))
                k.pe.inc(ins)
            k.sec("tensor", f_pe)
            wo_pe = k.pe.n
            kv_free[slot] = k.pe.n

            def f_dv(v, t_glob=t_glob, wo_pe=wo_pe):
                k.pe.wait(v, wo_pe)
                for half in range(2):
                    ps = k.G[3] if half == 0 else k.G[0]
                    xsl = k.xres[:, t_glob * D + half * 512:t_glob * D + (half + 1) * 512]
                    v.tensor_tensor(k.fscr[:, ESC1:ESC1 + 512], ps[:, :],
                                    k.cb1[:, half * 512:(half + 1) * 512], ALU.add)
                    v.tensor_add(xsl, xsl, k.fscr[:, ESC1:ESC1 + 512])
            k.sec("vector", f_dv)

    def _proj_elu(self, wsrc, bias, dstoff):
        k = self
        ln_done = k.dv.n
        pg_free = [0, 0]
        pg_ac = [0, 0]
        for dk in range(DC):
            slot = dk % 2

            def f_gp(g, dk=dk, slot=slot):
                if k.wcol_free[slot] > 0:
                    k.pe.wait(g, k.wcol_free[slot])
                k.s_wc[slot].inc(g.dma_start(
                    out=k.wbuf[:, WCOL + slot * 1024:WCOL + (slot + 1) * 1024].rearrange(
                        "p (i o) -> p i o", o=128),
                    in_=wsrc[:, dk * 128:(dk + 1) * 128].rearrange(
                        "(i p) o -> p i o", p=128)), 16)
            k.sec("gpsimd", f_gp)

            def f_pe(te, dk=dk, slot=slot):
                k.s_wc[slot].wait(te)
                k.dv.wait(te, ln_done)
                if pg_free[slot] > 0:
                    k.ac.wait(te, pg_ac[slot])
                    k.dv.wait(te, pg_free[slot])
                ps = k.G[slot]
                for d in range(DC):
                    ins = te.matmul(ps[:, :],
                                    k.wbuf[:, WCOL + slot * 1024 + d * 128:WCOL + slot * 1024 + (d + 1) * 128],
                                    k.abf[:, LNQ + d * QT:LNQ + (d + 1) * QT],
                                    start=(d == 0), stop=(d == DC - 1))
                k.pe.inc(ins)
                k.wcol_free[slot] = k.pe.n
            k.sec("tensor", f_pe)
            pe_done = k.pe.n

            mo, eo = (ESC1, ESC2) if slot == 0 else (SQSC, SQSC + 512)

            def f_dv(v, dk=dk, slot=slot, pe_done=pe_done, mo=mo):
                k.pe.wait(v, pe_done)
                v.tensor_scalar(k.fscr[:, mo:mo + 512], k.G[slot][:, :],
                                bias[:, dk:dk + 1], 0.0, ALU.add, ALU.min)
            k.sec("vector", f_dv)

            def f_ac(s, dk=dk, slot=slot, mo=mo, eo=eo):
                k.dv.wait(s)
                s.activation(k.fscr[:, eo:eo + 512], k.fscr[:, mo:mo + 512], AF.Exp)
            k.sec("scalar", f_ac)

            def f_dv2(v, dk=dk, slot=slot, mo=mo, eo=eo):
                k.ac.wait(v)
                v.tensor_scalar(k.fscr[:, mo:mo + 512], k.G[slot][:, :],
                                bias[:, dk:dk + 1], 0.0, ALU.add, ALU.max)
                v.tensor_add(k.abf[:, dstoff + dk * QT:dstoff + (dk + 1) * QT],
                             k.fscr[:, mo:mo + 512], k.fscr[:, eo:eo + 512])
            k.sec("vector", f_dv2)
            pg_free[slot] = k.dv.n

    def _reload_cb(self, pairs):
        k = self

        def f_sy(sy):
            k.dv.wait(sy)
            for dst, srcr in pairs:
                k.s_const.inc(sy.dma_start(out=dst[:], in_=srcr.ap().broadcast_to([128, D])), 16)
        k.sec("sync", f_sy)
        n = k.s_const.n

        def f_dv(v):
            k.s_const.wait(v, n)
        k.sec("vector", f_dv)

    def emit_ffn(self, first):
        self._reload_cb([(self.cb0, self.w3s_r)])
        pe0 = self.pe.n
        self._wf_free = [pe0, pe0]
        self._w3_free = [pe0, pe0]
        for s in range(4):
            self._ffn_subtile(s, first and s == 0)

    def _ffn_subtile(self, s, write_cache):
        k = self
        pe_barrier = k.pe.n

        k.tnrm = F_TNRM

        def f_dv(v):
            k.pe.wait(v, pe_barrier)
        k.sec("vector", f_dv)
        items = [(k.xres[:, (s * 4 + tl) * D:(s * 4 + tl + 1) * D], None,
                  (F_LNX, QT, tl * 128)) for tl in range(4)]
        self.ln_group(items)
        lnx_done = k.dv.n

        ga_done_marks = []
        for o in range(FC):
            slot = o % 2
            self._ffn_wload(o, slot, [(k.w1t, k.c_w1, 0), (k.w2t, k.c_w2, 1024)],
                            DC, write_cache)
            pa, pb = k.G[0 + 2 * slot], k.G[1 + 2 * slot]

            def f_pe(te, o=o, slot=slot, pa=pa, pb=pb):
                k.s_wf[slot].wait(te)
                k.s_wfc[slot].wait(te)
                k.dv.wait(te, lnx_done)
                if k._ffn_g_free[slot] > 0:
                    k.dv.wait(te, k._ffn_g_free[slot])
                base = FW_SL[slot]
                for d in range(DC):
                    te.matmul(pa[:, :], k.wbuf[:, base + d * 128:base + (d + 1) * 128],
                              k.abf[:, F_LNX + d * QT:F_LNX + (d + 1) * QT],
                              start=(d == 0), stop=(d == DC - 1))
                for d in range(DC):
                    ins = te.matmul(pb[:, :], k.wbuf[:, base + 1024 + d * 128:base + 1024 + (d + 1) * 128],
                                    k.abf[:, F_LNX + d * QT:F_LNX + (d + 1) * QT],
                                    start=(d == 0), stop=(d == DC - 1))
                k.pe.inc(ins)
                k._wf_free[slot] = k.pe.n
            k.sec("tensor", f_pe)
            mm_done = k.pe.n

            scl = ESC1 if o % 2 == 0 else ESC2

            def f_ac(s_, o=o, pa=pa, mm_done=mm_done, scl=scl):
                k.pe.wait(s_, mm_done)
                if k._ffn_scl_free[o % 2] > 0:
                    k.dv.wait(s_, k._ffn_scl_free[o % 2])
                s_.activation(k.fscr[:, scl:scl + 512], pa[:, :], AF.Silu,
                                       scale=k.cw1s[:, o:o + 1])
            k.sec("scalar", f_ac)

            def f_dv(v, o=o, pb=pb, scl=scl):
                k.ac.wait(v)
                v.scalar_tensor_tensor(
                    k.abf[:, F_GA + o * QT:F_GA + (o + 1) * QT], pb[:, :],
                    k.cw2s[:, o:o + 1], k.fscr[:, scl:scl + 512], ALU.mult, ALU.mult)
            k.sec("vector", f_dv)
            k._ffn_scl_free[o % 2] = k.dv.n
            k._ffn_g_free[slot] = k.dv.n
        ga_done = k.dv.n

        gb_free = [0, 0]
        for o in range(FC):
            slot = o % 2
            self._ffn_wload(o, slot, [(k.h1t, k.c_h1, 0)], FC, write_cache)
            ps = k.G[slot]

            def f_pe(te, o=o, slot=slot, ps=ps):
                k.s_wf[slot].wait(te)
                k.s_wfc[slot].wait(te)
                k.dv.wait(te, ga_done)
                if gb_free[slot] > 0:
                    k.ac.wait(te, gb_free[slot])
                base = FW_SL[slot]
                for i in range(FC):
                    ins = te.matmul(ps[:, :], k.wbuf[:, base + i * 128:base + (i + 1) * 128],
                                    k.abf[:, F_GA + i * QT:F_GA + (i + 1) * QT],
                                    start=(i == 0), stop=(i == FC - 1))
                k.pe.inc(ins)
                k._wf_free[slot] = k.pe.n
            k.sec("tensor", f_pe)
            mm_done = k.pe.n

            def f_ac(s_, o=o, ps=ps, mm_done=mm_done):
                k.pe.wait(s_, mm_done)
                s_.activation(k.abf[:, F_GB + o * QT:F_GB + (o + 1) * QT],
                                       ps[:, :], AF.Gelu, scale=k.ch1s[:, o:o + 1])
            k.sec("scalar", f_ac)
            gb_free[slot] = k.ac.n
        gb_done = k.ac.n

        gc_free = [0, 0]
        for o in range(FC):
            slot = o % 2
            self._ffn_wload(o, slot, [(k.h2t, k.c_h2, 0)], FC, write_cache)
            ps = k.G[slot]

            def f_pe(te, o=o, slot=slot, ps=ps):
                k.s_wf[slot].wait(te)
                k.s_wfc[slot].wait(te)
                k.ac.wait(te, gb_done)
                if gc_free[slot] > 0:
                    k.ac.wait(te, gc_free[slot])
                base = FW_SL[slot]
                for i in range(FC):
                    ins = te.matmul(ps[:, :], k.wbuf[:, base + i * 128:base + (i + 1) * 128],
                                    k.abf[:, F_GB + i * QT:F_GB + (i + 1) * QT],
                                    start=(i == 0), stop=(i == FC - 1))
                k.pe.inc(ins)
                k._wf_free[slot] = k.pe.n
            k.sec("tensor", f_pe)
            mm_done = k.pe.n

            def f_ac(s_, o=o, ps=ps, mm_done=mm_done):
                k.pe.wait(s_, mm_done)
                s_.activation(k.abf[:, F_GA + o * QT:F_GA + (o + 1) * QT],
                                       ps[:, :], AF.Gelu, scale=k.ch2s[:, o:o + 1])
            k.sec("scalar", f_ac)
            gc_free[slot] = k.ac.n
        gc_done = k.ac.n

        w3_ev = 0
        for hp in range(2):
            for i in range(FC):
                slot = i % 2
                if write_cache and hp == 0:
                    def f_gp(g, i=i, slot=slot):
                        if k._w3_free[slot] > 0:
                            k.pe.wait(g, k._w3_free[slot])
                        k.s_w3c[slot].wait(g)
                        k.s_w3[slot].inc(g.dma_start(
                            out=k.wbuf[:, WCOL + slot * 1024:WCOL + (slot + 1) * 1024],
                            in_=k.w3t[i * 128:(i + 1) * 128, :]), 16)
                    k.sec("gpsimd", f_gp)

                    def f_sy2(sy, i=i, slot=slot):
                        k.s_w3[slot].wait(sy)
                        k.s_w3c[slot].inc(sy.dma_start(
                            out=k.c_w3[i],
                            in_=k.wbuf[:, WCOL + slot * 1024:WCOL + (slot + 1) * 1024]), 16)
                    k.sec("sync", f_sy2)
                else:
                    def f_sy2(sy, i=i, slot=slot):
                        if k._w3_free[slot] > 0:
                            k.pe.wait(sy, k._w3_free[slot])
                        k.s_w3c[slot].inc(sy.dma_start(
                            out=k.wbuf[:, WCOL + slot * 1024:WCOL + (slot + 1) * 1024],
                            in_=k.c_w3[i]), 16)
                    k.sec("sync", f_sy2)

                def f_pe(te, i=i, slot=slot, hp=hp, w3_ev=w3_ev):
                    k.s_w3[slot].wait(te)
                    k.s_w3c[slot].wait(te)
                    if i == 0:
                        k.ac.wait(te, gc_done)
                        if hp == 0:
                            k.dv.wait(te)
                        else:
                            k.dv.wait(te, w3_ev)
                    for tt_ in range(2):
                        for half in range(2):
                            ps = k.G[tt_ * 2 + half]
                            ins = te.matmul(
                                ps[:, :],
                                k.abf[:, F_GA + i * QT + (hp * 2 + tt_) * 128:F_GA + i * QT + (hp * 2 + tt_) * 128 + 128],
                                k.wbuf[:, WCOL + slot * 1024 + half * 512:WCOL + slot * 1024 + (half + 1) * 512],
                                start=(i == 0), stop=(i == FC - 1))
                    k.pe.inc(ins)
                k.sec("tensor", f_pe)
                k._w3_free[slot] = k.pe.n
            w3_pe = k.pe.n

            def f_dv(v, s=s, hp=hp, w3_pe=w3_pe):
                k.pe.wait(v, w3_pe)
                for tt_ in range(2):
                    t_glob = s * 4 + hp * 2 + tt_
                    for half in range(2):
                        ps = k.G[tt_ * 2 + half]
                        xsl = k.xres[:, t_glob * D + half * 512:t_glob * D + (half + 1) * 512]
                        v.tensor_tensor(k.fscr[:, ESC1:ESC1 + 512], ps[:, :],
                                        k.cb0[:, half * 512:(half + 1) * 512], ALU.mult)
                        ins = v.tensor_add(xsl, xsl, k.fscr[:, ESC1:ESC1 + 512])
            k.sec("vector", f_dv)
            w3_ev = k.dv.n

    _ffn_g_free = None
    _ffn_scl_free = None
    _w3_free = None

    def _ffn_wload(self, o, slot, specs, itiles, write_cache):
        k = self
        if write_cache:
            def f_gp(g, o=o, slot=slot):
                if k._wf_free[slot] > 0:
                    k.pe.wait(g, k._wf_free[slot])
                k.s_wfc[slot].wait(g)
                for wsrc, _, doff in specs:
                    for i8 in range(0, itiles, 8):
                        k.s_wf[slot].inc(g.dma_start(
                            out=k.wbuf[:, FW_SL[slot] + doff + i8 * 128:FW_SL[slot] + doff + (i8 + 8) * 128].rearrange(
                                "p (i c) -> p i c", c=128),
                            in_=wsrc[i8 * 128:(i8 + 8) * 128, o * 128:(o + 1) * 128].rearrange(
                                "(i p) c -> p i c", p=128)), 16)
            k.sec("gpsimd", f_gp)

            def f_sy(sy, o=o, slot=slot):
                k.s_wf[slot].wait(sy)
                for _, cache, doff in specs:
                    k.s_wfc[slot].inc(sy.dma_start(
                        out=cache[o],
                        in_=k.wbuf[:, FW_SL[slot] + doff:FW_SL[slot] + doff + itiles * 128]), 16)
            k.sec("sync", f_sy)
        else:
            def f_sy(sy, o=o, slot=slot):
                if k._wf_free[slot] > 0:
                    k.pe.wait(sy, k._wf_free[slot])
                for _, cache, doff in specs:
                    k.s_wfc[slot].inc(sy.dma_start(
                        out=k.wbuf[:, FW_SL[slot] + doff:FW_SL[slot] + doff + itiles * 128],
                        in_=cache[o]), 16)
            k.sec("sync", f_sy)

    _wf_free = None

    def emit_attn_b(self):
        k = self
        pe_barrier = k.pe.n
        self._bw_free = [pe_barrier, pe_barrier]
        self._bw2_free = [pe_barrier, pe_barrier]
        self._reload_cb([(k.cb0, k.n1w_r), (k.cb1, k.n2w_r)])

        k.tnrm = B_TN

        def f_sy(sy):
            k.dv.wait(sy)
            k.s_x15.inc(sy.dma_start(out=k.x2sp[15 * 128:16 * 128, :],
                                     in_=k.xres[:, 15 * D:16 * D]), 16)
            for t in range(15):
                k.s_x2w.inc(sy.dma_start(out=k.x2sp[t * 128:(t + 1) * 128, :],
                                         in_=k.xres[:, t * D:(t + 1) * D]), 16)
        k.sec("sync", f_sy)
        x2w_n = k.s_x2w.n

        self.ln_group_f32([(k.xres[:, 15 * D:16 * D], (k.s_x15, k.s_x15.n),
                            k.xres[:, 15 * D:16 * D])])

        self.ln_group([(k.xres[:, 15 * D:16 * D], None, (B_N1, 128, 0))],
                      rms=True, gamma=k.cb0[:])
        n1e_done = k.dv.n
        for g_ in range(GC):
            slot = g_ % 2
            self._bcol_load(k.wint, g_, slot, DC)

            def f_pe(te, g_=g_, slot=slot):
                k.s_wb[slot].wait(te)
                k.dv.wait(te)
                for d in range(DC):
                    ins = te.matmul(k.G[0][:, 0:128],
                                    k.wbuf[:, BW_SL[slot] + d * 128:BW_SL[slot] + (d + 1) * 128],
                                    k.abf[:, B_N1 + d * 128:B_N1 + (d + 1) * 128],
                                    start=(d == 0), stop=(d == DC - 1))
                k.pe.inc(ins)
                k._bw_free[slot] = k.pe.n
            k.sec("tensor", f_pe)
            u15_pe = k.pe.n

            def f_dv(v, g_=g_, u15_pe=u15_pe):
                k.pe.wait(v, u15_pe)
                v.tensor_copy(k.utail[:, g_ * 3:(g_ + 1) * 3], k.G[0][:, 125:128])
            k.sec("vector", f_dv)
        utail_done = k.dv.n

        def f_gp(g):
            k.dv.wait(g, utail_done)
            k.s_stage.inc(g.dma_start(out=k.cc_ut_in[:, :], in_=k.utail[:]), 16)
            k.s_stage.wait(g)
            k.s_cc.inc(g.collective_compute(
                "AllGather", ALU.bypass, replica_groups=[list(range(NCORES))],
                ins=[k.cc_ut_in[:, :]], outs=[k.cc_ut_out[:, :]]), 1)
            k.s_cc.wait(g)
            for j, r in enumerate((0, 2, 4, 6)):
                k.s_gth.inc(g.dma_start(out=k.utg[:, j * GC * 3:(j + 1) * GC * 3],
                                        in_=k.cc_ut_out[r * 128:(r + 1) * 128, :]), 16)
        k.sec("gpsimd", f_gp)
        gth_n = k.s_gth.n

        for t0 in range(0, 15, 4):
            tls = list(range(t0, min(t0 + 4, 15)))
            self.ln_group_f32([(k.xres[:, t * D:(t + 1) * D],
                                (k.s_x2w, x2w_n) if t0 == 0 else None,
                                k.xres[:, t * D:(t + 1) * D]) for t in tls])

        def f_dv(v):
            k.s_gth.wait(v, gth_n)
            v.tensor_scalar(k.urecv[:], k.utg[:, 0:GC * 3], k.csel[:, 0:1], None, ALU.mult)
            for j, r in ((1, 2), (2, 4), (3, 6)):
                v.scalar_tensor_tensor(k.urecv[:], k.utg[:, j * GC * 3:(j + 1) * GC * 3],
                                             k.csel[:, r:r + 1], k.urecv[:], ALU.mult, ALU.add)
        k.sec("vector", f_dv)
        urecv_done = k.dv.n

        def f_gp(g):
            k.pe.wait(g, pe_barrier)
            for g_ in range(GC):
                k.s_rows.inc(g.dma_start(
                    out=k.wbuf[:, BW_ROWS + g_ * D:BW_ROWS + (g_ + 1) * D],
                    in_=k.woutt[g_ * 128:(g_ + 1) * 128, :]), 16)
        k.sec("gpsimd", f_gp)
        wout_n = k.s_rows.n

        for q in range(4):
            self._attnb_quarter(q, urecv_done, wout_n)

        hcol_done = k.dv.n

        def f_gp(g):
            k.dv.wait(g, hcol_done)
            k.s_stage.inc(g.dma_start(out=k.cc_h_in[:, :], in_=k.hcol[:]), 16)
            k.s_stage.wait(g)
            k.s_cc.inc(g.collective_compute(
                "AllGather", ALU.bypass, replica_groups=[list(range(NCORES))],
                ins=[k.cc_h_in[:, :]], outs=[k.cc_h_out[:, :]]), 1)
            k.s_cc.wait(g)
            for j, r in enumerate((0, 2, 4, 6)):
                k.s_gth.inc(g.dma_start(out=k.hg[:, j * GC:(j + 1) * GC],
                                        in_=k.cc_h_out[r * 128:(r + 1) * 128, :]), 16)
        k.sec("gpsimd", f_gp)
        gth_n2 = k.s_gth.n

        def f_dv(v):
            k.s_gth.wait(v, gth_n2)
            v.tensor_scalar(k.hrecv[:], k.hg[:, 0:GC], k.csel[:, 0:1], None, ALU.mult)
            for j, r in ((1, 2), (2, 4), (3, 6)):
                v.scalar_tensor_tensor(k.hrecv[:], k.hg[:, j * GC:(j + 1) * GC],
                                             k.csel[:, r:r + 1], k.hrecv[:], ALU.mult, ALU.add)
        k.sec("vector", f_dv)
        hrecv_done = k.dv.n

        for q in range(4):
            def f_sy(sy, q=q):
                k.s_p2.wait(sy)
                k.pe.wait(sy)
                k.s_p2.inc(sy.dma_start(out=k.abf[:, B_UC:B_UC + GC * QT], in_=k.p2sp[q]), 16)
            k.sec("sync", f_sy)
            p2n = k.s_p2.n

            def f_dv(v, q=q, p2n=p2n):
                k.s_p2.wait(v, p2n)
                k.pe.wait(v)
                k.dv.wait(v, hrecv_done)
                for g_ in range(GC):
                    ins = v.tensor_scalar(k.abf[:, B_SIG + g_ * QT:B_SIG + (g_ + 1) * QT],
                                          k.abf[:, B_UC + g_ * QT:B_UC + (g_ + 1) * QT],
                                          k.hrecv[:, g_:g_ + 1], None, ALU.mult)
            k.sec("vector", f_dv)
            self._mix_mms(q, k.dv.n, wout_n)

        def f_gp(g):
            k.pe.wait(g)
            for g_ in range(GC):
                k.s_rows.inc(g.dma_start(
                    out=k.wbuf[:, BW_ROWS + g_ * D:BW_ROWS + (g_ + 1) * D],
                    in_=k.mdownt[g_ * 128:(g_ + 1) * 128, :]), 16)
        k.sec("gpsimd", f_gp)
        mdown_n = k.s_rows.n

        for q in range(4):
            self._mlp_quarter(q, mdown_n)

        self.emit_x2_add()


    def ln_tile_f32(self, src, dst, pre_wait=None):
        k = self

        def f_dv(v):
            if pre_wait is not None:
                pre_wait[0].wait(v, pre_wait[1])
            v.reduce_sum(k.stats[:, 0:1], src, axis=AX)
            v.scalar_tensor_tensor(k.fscr[:, SQSC:SQSC + D], src, 1.0, src,
                                            ALU.mult, ALU.mult, accum_out=k.stats[:, 1:2])
        k.sec("vector", f_dv)

        def f_dv2(v):
            v.tensor_scalar_mul(k.stats[:, 2:3], k.stats[:, 0:1], 1.0 / D)
            v.tensor_scalar_mul(k.stats[:, 3:4], k.stats[:, 1:2], 1.0 / D)
            v.tensor_tensor(k.stats[:, 5:6], k.stats[:, 2:3], k.stats[:, 2:3], ALU.mult)
            v.tensor_tensor(k.stats[:, 4:5], k.stats[:, 3:4],
                                     k.stats[:, 5:6], ALU.subtract)
        k.sec("vector", f_dv2)

        def f_ac(s):
            k.dv.wait(s)
            s.activation(k.stats[:, 6:7], k.stats[:, 4:5], AF.Ln, bias=EPS)
            s.activation(k.stats[:, 6:7], k.stats[:, 6:7], AF.Exp, scale=-0.5)
        k.sec("scalar", f_ac)

        def f_dv3(v):
            k.ac.wait(v)
            v.tensor_scalar(dst, src, k.stats[:, 2:3], k.stats[:, 6:7],
                                     ALU.subtract, ALU.mult)
        k.sec("vector", f_dv3)

    _bw_free = None

    def _bcol_load(self, wsrc, g_, slot, itiles):
        k = self

        def f_gp(g, g_=g_, slot=slot):
            if k._bw_free[slot] > 0:
                k.pe.wait(g, k._bw_free[slot])
            k.s_wb[slot].inc(g.dma_start(
                out=k.wbuf[:, BW_SL[slot]:BW_SL[slot] + itiles * 128].rearrange(
                    "p (i c) -> p i c", c=128),
                in_=wsrc[0:itiles * 128, g_ * 128:(g_ + 1) * 128].rearrange(
                    "(i p) c -> p i c", p=128)), 16)
        k.sec("gpsimd", f_gp)

    def _attnb_quarter(self, q, urecv_done, wout_n):
        k = self
        k.tnrm = B_TN
        self.ln_group([(k.xres[:, (q * 4 + tl) * D:(q * 4 + tl + 1) * D], None,
                        (B_N1, QT, tl * 128)) for tl in range(4)],
                      rms=True, gamma=k.cb0[:])
        n1_done = k.dv.n

        for g_ in range(GC):
            slot = g_ % 2
            self._bcol_load(k.wgatet, g_, slot, DC)
            ps = k.G[slot]

            def f_pe(te, g_=g_, slot=slot, ps=ps):
                k.s_wb[slot].wait(te)
                k.dv.wait(te, n1_done)
                if k._bg_free[slot] > 0:
                    k.ac.wait(te, k._bg_free[slot])
                for d in range(DC):
                    ins = te.matmul(ps[:, :],
                                    k.wbuf[:, BW_SL[slot] + d * 128:BW_SL[slot] + (d + 1) * 128],
                                    k.abf[:, B_N1 + d * QT:B_N1 + (d + 1) * QT],
                                    start=(d == 0), stop=(d == DC - 1))
                k.pe.inc(ins)
                k._bw_free[slot] = k.pe.n
            k.sec("tensor", f_pe)
            mm_n = k.pe.n

            def f_ac(s_, g_=g_, ps=ps, mm_n=mm_n):
                k.pe.wait(s_, mm_n)
                s_.activation(k.abf[:, B_GG + g_ * QT:B_GG + (g_ + 1) * QT],
                                       ps[:, :], AF.Gelu)
            k.sec("scalar", f_ac)
            k._bg_free[slot] = k.ac.n
        gate_done = k.ac.n

        for g_ in range(GC):
            slot = g_ % 2
            self._bcol_load(k.wint, g_, slot, DC)
            ps = k.G[2 + slot]

            def f_pe(te, g_=g_, slot=slot, ps=ps):
                k.s_wb[slot].wait(te)
                k.dv.wait(te, n1_done)
                if k._bu_free[slot] > 0:
                    k.dv.wait(te, k._bu_free[slot])
                for d in range(DC):
                    ins = te.matmul(ps[:, :],
                                    k.wbuf[:, BW_SL[slot] + d * 128:BW_SL[slot] + (d + 1) * 128],
                                    k.abf[:, B_N1 + d * QT:B_N1 + (d + 1) * QT],
                                    start=(d == 0), stop=(d == DC - 1))
                k.pe.inc(ins)
                k._bw_free[slot] = k.pe.n
            k.sec("tensor", f_pe)
            mm_n = k.pe.n

            def f_dv(v, g_=g_, q=q, ps=ps, mm_n=mm_n):
                k.pe.wait(v, mm_n)
                if g_ == 0 and q > 0:
                    k.s_p2.wait(v)
                if k._uext_free > 0:
                    k.gp.wait(v, k._uext_free)
                v.tensor_copy(k.fscr[:, UEXT + 3:UEXT + 3 + QT], ps[:, :])
            k.sec("vector", f_dv)
            k._bu_free[slot] = k.dv.n

            def f_gp(g, g_=g_, q=q):
                k.dv.wait(g)
                if q == 0:
                    g.tensor_copy(k.fscr[:, UEXT:UEXT + 3], k.urecv[:, g_ * 3:(g_ + 1) * 3])
                else:
                    g.tensor_copy(k.fscr[:, UEXT:UEXT + 3], k.uhalo[:, g_ * 3:(g_ + 1) * 3])
                g.tensor_copy(k.uhalo[:, g_ * 3:(g_ + 1) * 3], k.fscr[:, UEXT + QT:UEXT + QT + 3])
                g.tensor_scalar(k.fscr[:, S1:S1 + QT], k.fscr[:, UEXT + 3:UEXT + 3 + QT],
                                k.cconvw[:, g_ * 4 + 3:g_ * 4 + 4],
                                k.cconvb[:, g_:g_ + 1], ALU.mult, ALU.add)
                for kk in (2, 1):
                    g.tensor_scalar(k.fscr[:, ESC1:ESC1 + QT],
                                    k.fscr[:, UEXT + kk:UEXT + kk + QT],
                                    k.cconvw[:, g_ * 4 + kk:g_ * 4 + kk + 1], None, ALU.mult)
                    g.tensor_tensor(k.fscr[:, S1:S1 + QT], k.fscr[:, S1:S1 + QT],
                                    k.fscr[:, ESC1:ESC1 + QT], ALU.add)
                g.tensor_scalar(k.fscr[:, ESC1:ESC1 + QT], k.fscr[:, UEXT:UEXT + QT],
                                k.cconvw[:, g_ * 4:g_ * 4 + 1], None, ALU.mult)
                g.tensor_tensor(k.abf[:, B_UC + g_ * QT:B_UC + (g_ + 1) * QT],
                                k.fscr[:, S1:S1 + QT],
                                k.fscr[:, ESC1:ESC1 + QT], ALU.add)
            k.sec("gpsimd", f_gp)
            k._uext_free = k.gp.n
        uc_done_gp = k.gp.n
        uc_done = k.dv.n

        for g_ in range(GC):
            slot = g_ % 2
            self._bcol_load(k.rgint, g_, slot, GC)
            self._bcol_load2(k.rggatet, g_, slot, GC)
            pa, pb = k.G[0 + 2 * slot], k.G[1 + 2 * slot]

            def f_pe(te, g_=g_, slot=slot, pa=pa, pb=pb):
                k.s_wb[slot].wait(te)
                k.gp.wait(te, uc_done_gp)
                k.ac.wait(te, max(gate_done, k._brg_free[slot]))
                for h in range(GC):
                    te.matmul(pa[:, :], k.wbuf[:, BW_SL2[slot] + h * 128:BW_SL2[slot] + (h + 1) * 128],
                              k.abf[:, B_UC + h * QT:B_UC + (h + 1) * QT],
                              start=(h == 0), stop=(h == GC - 1))
                for h in range(GC):
                    ins = te.matmul(pb[:, :], k.wbuf[:, BW_SL2[slot] + 1536 + h * 128:BW_SL2[slot] + 1536 + (h + 1) * 128],
                                    k.abf[:, B_UC + h * QT:B_UC + (h + 1) * QT],
                                    start=(h == 0), stop=(h == GC - 1))
                k.pe.inc(ins)
                k._bw_free[slot] = k.pe.n
                k._bw2_free[slot] = k.pe.n
            k.sec("tensor", f_pe)
            mm_n = k.pe.n

            def f_ac(s_, g_=g_, pa=pa, pb=pb, mm_n=mm_n):
                k.pe.wait(s_, mm_n)
                s_.activation(k.abf[:, B_SIG + g_ * QT:B_SIG + (g_ + 1) * QT], pa[:, :],
                              AF.Sigmoid, bias=k.crginb[:, g_:g_ + 1])
                s_.activation(k.abf[:, B_SRG + g_ * QT:B_SRG + (g_ + 1) * QT], pb[:, :],
                                       AF.Sigmoid, bias=k.crggateb[:, g_:g_ + 1])
            k.sec("scalar", f_ac)
            k._brg_free[slot] = k.ac.n
        rg_done = k.ac.n
        rg_pe = k.pe.n

        ABANK = [k.G[0], k.G[1], k.G[2], k.G[3], k.PZ, k.PSA]
        SLOTS = [(S2, S3, S4, S5), (0, 512, 3072, 3584)]
        la_done = [0, 0]
        set_free = [0, 0]
        for hf in range(2):
            gs = list(range(hf * 6, hf * 6 + 6))
            for j, g_ in enumerate(gs):
                sl_la = SLOTS[j % 2][0]

                def f_gp(g, g_=g_, q=q, j=j, sl_la=sl_la, hf=hf):
                    if g_ == 0:
                        k.ac.wait(g, rg_done)
                        k.pe.wait(g, rg_pe)
                        if q > 0:
                            k.s_p2.wait(g)
                    if la_done[j % 2] > 0:
                        k.ac.wait(g, la_done[j % 2])
                    g.tensor_scalar(k.fscr[:, sl_la:sl_la + QT],
                                    k.abf[:, B_SRG + g_ * QT:B_SRG + (g_ + 1) * QT],
                                    k.clamp[:, g_:g_ + 1], None, ALU.mult)
                k.sec("gpsimd", f_gp)

                def f_ac(s_, j=j, sl_la=sl_la, hf=hf):
                    k.gp.wait(s_)
                    if j == 0:
                        k.pe.wait(s_)
                        k.dv.wait(s_)
                    s_.activation(ABANK[j][:, 0:QT], k.fscr[:, sl_la:sl_la + QT], AF.Exp)
                k.sec("scalar", f_ac)
                la_done[j % 2] = k.ac.n

            for j, g_ in enumerate(gs):
                s_la, s_z, s_b, s_h = SLOTS[j % 2]

                def f_ac(s_, j=j, s_z=s_z):
                    if set_free[j % 2] > 0:
                        k.gp.wait(s_, set_free[j % 2])
                    s_.activation(k.fscr[:, s_z:s_z + QT], ABANK[j][:, 0:QT], AF.Square)
                k.sec("scalar", f_ac)

                def f_gp(g, j=j, s_z=s_z):
                    k.ac.wait(g)
                    g.tensor_scalar(k.fscr[:, s_z:s_z + QT], k.fscr[:, s_z:s_z + QT],
                                    -1.0, 1.0 + 1e-6, ALU.mult, ALU.add)
                k.sec("gpsimd", f_gp)

                def f_ac2(s_, j=j, s_z=s_z, s_b=s_b):
                    k.gp.wait(s_)
                    s_.activation(k.fscr[:, s_b:s_b + QT], k.fscr[:, s_z:s_z + QT], AF.Sqrt)
                k.sec("scalar", f_ac2)

                def f_gp2(g, j=j, g_=g_, s_z=s_z, s_b=s_b):
                    k.ac.wait(g)
                    g.tensor_tensor(k.fscr[:, s_z:s_z + QT], k.fscr[:, s_b:s_b + QT],
                                    k.abf[:, B_SIG + g_ * QT:B_SIG + (g_ + 1) * QT], ALU.mult)
                    g.tensor_tensor(k.fscr[:, s_z:s_z + QT], k.fscr[:, s_z:s_z + QT],
                                    k.abf[:, B_UC + g_ * QT:B_UC + (g_ + 1) * QT], ALU.mult)
                k.sec("gpsimd", f_gp2)

                def f_dv(v, j=j, g_=g_, s_z=s_z, s_b=s_b, s_h=s_h):
                    k.gp.wait(v)
                    v.tensor_tensor_scan(k.fscr[:, s_h:s_h + QT], ABANK[j][:, 0:QT],
                                         k.fscr[:, s_z:s_z + QT], k.hcol[:, g_:g_ + 1],
                                         ALU.mult, ALU.add)
                    v.tensor_copy(k.hcol[:, g_:g_ + 1], k.fscr[:, s_h + QT - 1:s_h + QT])
                    v.tensor_tensor_scan(k.fscr[:, s_b:s_b + QT], ABANK[j][:, 0:QT],
                                         k.czero[:, 0:QT], k.pcol[:, g_:g_ + 1],
                                         ALU.mult, ALU.add)
                    v.tensor_copy(k.pcol[:, g_:g_ + 1], k.fscr[:, s_b + QT - 1:s_b + QT])
                k.sec("vector", f_dv)

                def f_gp3(g, j=j, g_=g_, s_b=s_b, s_h=s_h):
                    k.dv.wait(g)
                    g.tensor_tensor(k.abf[:, B_SIG + g_ * QT:B_SIG + (g_ + 1) * QT],
                                    k.abf[:, B_GG + g_ * QT:B_GG + (g_ + 1) * QT],
                                    k.fscr[:, s_h:s_h + QT], ALU.mult)
                    g.tensor_tensor(k.abf[:, B_UC + g_ * QT:B_UC + (g_ + 1) * QT],
                                    k.abf[:, B_GG + g_ * QT:B_GG + (g_ + 1) * QT],
                                    k.fscr[:, s_b:s_b + QT], ALU.mult)
                k.sec("gpsimd", f_gp3)
                set_free[j % 2] = k.gp.n
        scan_done = k.dv.n
        scan_done_gp = k.gp.n

        def f_sy(sy, q=q):
            k.gp.wait(sy, scan_done_gp)
            k.s_p2.inc(sy.dma_start(out=k.p2sp[q], in_=k.abf[:, B_UC:B_UC + GC * QT]), 16)
        k.sec("sync", f_sy)

        self._mix_gp = scan_done_gp
        self._mix_mms(q, scan_done, wout_n)

    _bg_free = None
    _bu_free = None
    _uext_free = 0
    _brg_free = None
    _bw2_free = None

    def _bcol_load2(self, wsrc, g_, slot, itiles):
        k = self

        def f_gp(g, g_=g_, slot=slot):
            if k._bw2_free[slot] > 0:
                k.pe.wait(g, k._bw2_free[slot])
            k.s_wb[slot].inc(g.dma_start(
                out=k.wbuf[:, BW_SL2[slot] + 1536:BW_SL2[slot] + 1536 + itiles * 128].rearrange(
                    "p (i c) -> p i c", c=128),
                in_=wsrc[0:itiles * 128, g_ * 128:(g_ + 1) * 128].rearrange(
                    "(i p) c -> p i c", p=128)), 16)
        k.sec("gpsimd", f_gp)

    def _mix_mms(self, q, src_done, wout_n):
        k = self
        ev = [0, 0]
        for tl in range(4):
            t_glob = q * 4 + tl

            def f_pe(te, tl=tl, src_done=src_done, ev0=None):
                k.dv.wait(te, src_done)
                if getattr(k, "_mix_gp", 0) > 0:
                    k.gp.wait(te, k._mix_gp)
                k.s_rows.wait(te, wout_n)
                if tl < 2:
                    k.dv.wait(te)
                elif ev[tl % 2] > 0:
                    k.dv.wait(te, ev[tl % 2])
                for half in range(2):
                    ps = k.G[tl % 2 * 2 + half]
                    for g_ in range(GC):
                        ins = te.matmul(ps[:, :],
                                        k.abf[:, B_SIG + g_ * QT + tl * 128:B_SIG + g_ * QT + tl * 128 + 128],
                                        k.wbuf[:, BW_ROWS + g_ * D + half * 512:BW_ROWS + g_ * D + (half + 1) * 512],
                                        start=(g_ == 0), stop=(g_ == GC - 1))
                k.pe.inc(ins)
            k.sec("tensor", f_pe)
            mm_n = k.pe.n

            def f_dv(v, t_glob=t_glob, tl=tl, mm_n=mm_n):
                k.pe.wait(v, mm_n)
                for half in range(2):
                    ps = k.G[tl % 2 * 2 + half]
                    xsl = k.xres[:, t_glob * D + half * 512:t_glob * D + (half + 1) * 512]
                    ins = v.tensor_add(xsl, xsl, ps[:, :])
            k.sec("vector", f_dv)
            ev[tl % 2] = k.dv.n

    def _mlp_quarter(self, q, mdown_n):
        k = self
        k.tnrm = B_TN
        self.ln_group([(k.xres[:, (q * 4 + tl) * D:(q * 4 + tl + 1) * D], None,
                        (B_N1, QT, tl * 128)) for tl in range(4)],
                      rms=True, gamma=k.cb1[:])
        n2_done = k.dv.n

        for g_ in range(GC):
            slot = g_ % 2
            self._bcol_load(k.mgatet, g_, slot, DC)
            self._bcol_load2(k.mupt, g_, slot, DC)
            pa, pb = k.G[0 + 2 * slot], k.G[1 + 2 * slot]

            def f_pe(te, g_=g_, slot=slot, pa=pa, pb=pb):
                k.s_wb[slot].wait(te)
                k.dv.wait(te, n2_done)
                if k._bmg_free[slot] > 0:
                    k.dv.wait(te, k._bmg_free[slot])
                for d in range(DC):
                    te.matmul(pa[:, :], k.wbuf[:, BW_SL[slot] + d * 128:BW_SL[slot] + (d + 1) * 128],
                              k.abf[:, B_N1 + d * QT:B_N1 + (d + 1) * QT],
                              start=(d == 0), stop=(d == DC - 1))
                for d in range(DC):
                    ins = te.matmul(pb[:, :], k.wbuf[:, BW_SL2[slot] + 1536 + d * 128:BW_SL2[slot] + 1536 + (d + 1) * 128],
                                    k.abf[:, B_N1 + d * QT:B_N1 + (d + 1) * QT],
                                    start=(d == 0), stop=(d == DC - 1))
                k.pe.inc(ins)
                k._bw_free[slot] = k.pe.n
                k._bw2_free[slot] = k.pe.n
            k.sec("tensor", f_pe)
            mm_n = k.pe.n

            def f_ac(s_, g_=g_, pa=pa, mm_n=mm_n):
                k.pe.wait(s_, mm_n)
                s_.activation(k.abf[:, B_GG + g_ * QT:B_GG + (g_ + 1) * QT],
                                       pa[:, :], AF.Gelu)
            k.sec("scalar", f_ac)

            def f_dv(v, g_=g_, pb=pb):
                k.ac.wait(v)
                v.tensor_tensor(k.abf[:, B_UC + g_ * QT:B_UC + (g_ + 1) * QT],
                                         pb[:, :], k.abf[:, B_GG + g_ * QT:B_GG + (g_ + 1) * QT],
                                         ALU.mult)
            k.sec("vector", f_dv)
            k._bmg_free[slot] = k.dv.n
        prd_done = k.dv.n

        dev = [0, 0]
        for tl in range(4):
            t_glob = q * 4 + tl

            def f_pe(te, tl=tl, prd_done=prd_done):
                k.dv.wait(te, prd_done)
                k.s_rows.wait(te, mdown_n)
                if tl < 2:
                    k.dv.wait(te)
                elif dev[tl % 2] > 0:
                    k.dv.wait(te, dev[tl % 2])
                for half in range(2):
                    ps = k.G[tl % 2 * 2 + half]
                    for g_ in range(GC):
                        ins = te.matmul(ps[:, :],
                                        k.abf[:, B_UC + g_ * QT + tl * 128:B_UC + g_ * QT + tl * 128 + 128],
                                        k.wbuf[:, BW_ROWS + g_ * D + half * 512:BW_ROWS + g_ * D + (half + 1) * 512],
                                        start=(g_ == 0), stop=(g_ == GC - 1))
                k.pe.inc(ins)
            k.sec("tensor", f_pe)
            mm_n = k.pe.n

            def f_dv(v, t_glob=t_glob, tl=tl, mm_n=mm_n):
                k.pe.wait(v, mm_n)
                for half in range(2):
                    ps = k.G[tl % 2 * 2 + half]
                    xsl = k.xres[:, t_glob * D + half * 512:t_glob * D + (half + 1) * 512]
                    ins = v.tensor_add(xsl, xsl, ps[:, :])
            k.sec("vector", f_dv)
            dev[tl % 2] = k.dv.n

    _bmg_free = None
    _x2add = None

    def emit_x2_add(self):
        k = self
        add_done = [0, 0]
        dv_base = k.dv.n
        for t in range(NT):
            def f_sy(sy, t=t):
                if t == 0:
                    k.dv.wait(sy, dv_base)
                if add_done[t % 2] > 0:
                    k.dv.wait(sy, add_done[t % 2])
                k.s_x2r[t % 2].inc(sy.dma_start(
                    out=k.fscr[:, XA + (t % 2) * 1024:XA + (t % 2) * 1024 + 1024],
                    in_=k.x2sp[t * 128:(t + 1) * 128, :]), 16)
            k.sec("sync", f_sy)
            x2n = k.s_x2r[t % 2].n

            def f_dv(v, t=t, x2n=x2n):
                k.s_x2r[t % 2].wait(v, x2n)
                xsl = k.xres[:, t * D:(t + 1) * D]
                v.tensor_add(xsl, xsl,
                                      k.fscr[:, XA + (t % 2) * 1024:XA + (t % 2) * 1024 + 1024])
            k.sec("vector", f_dv)
            add_done[t % 2] = k.dv.n

    def emit_store(self):
        k = self

        def f_sy(sy):
            k.dv.wait(sy)
            for t in range(NT):
                k.s_out.inc(sy.dma_start(out=k.out[t * 128:(t + 1) * 128, :],
                                         in_=k.xres[:, t * D:(t + 1) * D]), 16)
            k.s_out.wait(sy)
        k.sec("sync", f_sy)

    _x2gate = None


def _make_builder(phases):
    kb = KB(phases)
    kb._ffn_g_free = [0, 0]
    kb._ffn_scl_free = [0, 0]
    kb._w3_free = [0, 0]
    kb._wf_free = [0, 0]
    kb._bw_free = [0, 0]
    kb._bw2_free = [0, 0]
    kb._bg_free = [0, 0]
    kb._bu_free = [0, 0]
    kb._brg_free = [0, 0]
    kb._bmg_free = [0, 0]
    kb._x2gate = [0] * NT
    kb._x2add = [0] * NT
    return kb


def _shard_inputs(inputs):
    X = np.ascontiguousarray(np.asarray(inputs["L0_attn_a_x"], np.float32))

    def col_layout(v, chunks):
        return np.ascontiguousarray(np.asarray(v, np.float32).reshape(chunks, 128).T)

    common = dict(
        ident=np.eye(128, dtype=BF),
        triu=np.triu(np.ones((128, 128), dtype=np.float32)),
        wq=np.ascontiguousarray(np.asarray(inputs["L0_attn_a_Wq"], np.float32)),
        wk=np.ascontiguousarray(np.asarray(inputs["L0_attn_a_Wk"], np.float32)),
        wv=np.ascontiguousarray(np.asarray(inputs["L0_attn_a_Wv"], np.float32)),
        wo=np.ascontiguousarray(np.asarray(inputs["L0_attn_a_Wo"], np.float32)),
        bq=col_layout(inputs["L0_attn_a_bq"], DC),
        bk=col_layout(inputs["L0_attn_a_bk"], DC),
        bv_r=np.asarray(inputs["L0_attn_a_bv"], np.float32).reshape(1, D).copy(),
        bo_r=np.asarray(inputs["L0_attn_a_bo"], np.float32).reshape(1, D).copy(),
        w1t=np.ascontiguousarray(np.asarray(inputs["L0_ffn_w1_w"], np.float32).T),
        w2t=np.ascontiguousarray(np.asarray(inputs["L0_ffn_w2_w"], np.float32).T),
        h1t=np.ascontiguousarray(np.asarray(inputs["L0_ffn_h1_w"], np.float32).T),
        h2t=np.ascontiguousarray(np.asarray(inputs["L0_ffn_h2_w"], np.float32).T),
        w3t=np.ascontiguousarray(np.asarray(inputs["L0_ffn_w3_w"], np.float32).T),
        w1s=col_layout(inputs["L0_ffn_w1_s"], FC),
        w2s=col_layout(inputs["L0_ffn_w2_s"], FC),
        h1s=col_layout(inputs["L0_ffn_h1_s"], FC),
        h2s=col_layout(inputs["L0_ffn_h2_s"], FC),
        w3s_r=np.asarray(inputs["L0_ffn_w3_s"], np.float32).reshape(1, D).copy(),
        wint=np.ascontiguousarray(np.asarray(inputs["L0_attn_b_W_in"], np.float32).T),
        wgatet=np.ascontiguousarray(np.asarray(inputs["L0_attn_b_W_gate"], np.float32).T),
        rgint=np.ascontiguousarray(np.asarray(inputs["L0_attn_b_rg_in_w"], np.float32).T),
        rggatet=np.ascontiguousarray(np.asarray(inputs["L0_attn_b_rg_gate_w"], np.float32).T),
        woutt=np.ascontiguousarray(np.asarray(inputs["L0_attn_b_W_out"], np.float32).T),
        mgatet=np.ascontiguousarray(np.asarray(inputs["L0_attn_b_mlp_gate_w"], np.float32).T),
        mupt=np.ascontiguousarray(np.asarray(inputs["L0_attn_b_mlp_up_w"], np.float32).T),
        mdownt=np.ascontiguousarray(np.asarray(inputs["L0_attn_b_mlp_down_w"], np.float32).T),
        convw=np.ascontiguousarray(
            np.asarray(inputs["L0_attn_b_conv_w"], np.float32).reshape(GC, 128, 4)
            .transpose(1, 0, 2).reshape(128, GC * 4)),
        convb=col_layout(inputs["L0_attn_b_conv_b"], GC),
        rginb=col_layout(inputs["L0_attn_b_rg_in_b"], GC),
        rggateb=col_layout(inputs["L0_attn_b_rg_gate_b"], GC),
        lam=col_layout(inputs["L0_attn_b_forget_lambda"], GC),
        n1w_r=np.asarray(inputs["L0_attn_b_norm1_w"], np.float32).reshape(1, D).copy(),
        n2w_r=np.asarray(inputs["L0_attn_b_norm2_w"], np.float32).reshape(1, D).copy(),
    )
    in_maps = []
    for c in range(NCORES):
        b, half = c // 2, c % 2
        sel = np.zeros((128, 8), dtype=np.float32)
        if half == 1:
            sel[:, c - 1] = 1.0
        m = dict(common)
        m["xin"] = np.ascontiguousarray(X[b, half * T:(half + 1) * T])
        m["xprev"] = np.ascontiguousarray(X[b, 0:T])
        m["flag"] = np.full((128, 1), float(half), np.float32)
        m["sel"] = sel
        in_maps.append(m)
    return in_maps


_PHASES = ("ffn1", "attn_b", "ffn2")


def build_nc(phases=_PHASES):
    return _make_builder(phases).build()


def kernel(**inputs):
    inputs.get("L0_attn_a_mask")
    in_maps = _shard_inputs(inputs)
    nc = build_nc()
    res = run_bass_kernel_spmd(nc, in_maps, core_ids=list(range(NCORES)))
    outp = np.empty((4, 4096, 1024), dtype=np.float32)
    for c in range(NCORES):
        b, half = c // 2, c % 2
        outp[b, half * T:(half + 1) * T] = res.results[c]["out"]
    return outp


# version 07
# speedup vs baseline: 1.1539x; 1.1539x over previous
import numpy as np
import ml_dtypes
from contextlib import ExitStack

import concourse.bass as bass
import concourse.mybir as mybir
from concourse.bass_utils import run_bass_kernel_spmd

F32 = mybir.dt.float32
BF16 = mybir.dt.bfloat16
AF = mybir.ActivationFunctionType
ALU = mybir.AluOpType
AX = mybir.AxisListType.X
BF = ml_dtypes.bfloat16

T = 2048
D = 1024
DC = 8
HID = 1536
GC = 12
FH = 4096
FC = 32
NH = 16
QT = 512
NT = 16
EPS = 1e-5
QZEPS = 1e-6
NCORES = 8

LNQ, KFM, VTOK, TNRM, KTOK = 0, 4096, 8192, 12288, 13312
SNP = 14336
KCH, VCH, QCH, SSB, AMSK, VALS, VFM = 0, 2048, 4096, 6144, 8224, 10272, 11296
F_GB, F_TNRM, F_LNX, F_GA = 0, 11264, 12288, 16384
B_N1, B_GG, B_UC, B_SIG, B_SRG, B_TN = 0, 4096, 10240, 16384, 22528, 28672
WVO, WOO, WCOL = 0, 8192, 16384
FW_SL = [0, 8192]
BW_SL = [0, 3072]
BW_SL2 = [0, 3072]
BW_ROWS = 6144
UEXT, S1, S2, S3, S4, S5 = 0, 528, 1040, 1552, 2064, 2576
XA, XB = 0, 1024
ESC1, ESC2 = 2048, 2560
SQSC = 3072


class Sem:
    def __init__(self, ctx, nc, name):
        self.h = ctx.enter_context(nc.semaphore(name))
        self.n = 0

    def inc(self, instr, k=1):
        self.n += k
        instr.then_inc(self.h, k)
        return instr

    def wait(self, eng, n=None):
        eng.wait_ge(self.h, self.n if n is None else n)


_WRAP_OPS = {
    "memset", "memzero", "tensor_copy", "tensor_tensor", "tensor_add",
    "tensor_sub", "tensor_mul", "tensor_max", "tensor_scalar",
    "tensor_single_scalar", "tensor_scalar_add", "tensor_scalar_sub",
    "tensor_scalar_mul", "tensor_scalar_max", "tensor_scalar_min",
    "tensor_relu", "tensor_reduce", "reduce_max", "reduce_sum",
    "scalar_tensor_tensor", "tensor_tensor_scan", "reciprocal", "select",
    "transpose", "bn_stats", "bn_aggr", "activation", "square", "sqrt",
    "mul", "copy", "add", "sign",
}


class _EngProxy:
    def __init__(self, eng, sem):
        self._e = eng
        self._s = sem

    def __getattr__(self, name):
        f = getattr(self._e, name)
        if name in _WRAP_OPS:
            def g(*a, **kw):
                if self._s.n:
                    self._e.wait_ge(self._s.h, self._s.n)
                ins = f(*a, **kw)
                self._s.n += 1
                ins.then_inc(self._s.h, 1)
                return ins
            return g
        return f


class KB:
    def __init__(self, phases=("ffn1", "attn_b", "ffn2")):
        self.phases = phases
        self.nc = bass.Bass("TRN2", target_bir_lowering=False)
        self.xp_free = [0, 0]
        self.wcol_free = [0, 0]

    dbg_lvl = 99

    def load_lnexp(self):
        def f_ac(s):
            s.add_instruction(mybir.InstLoadActFuncSet(
                name=self.nc.get_next_instruction_name(),
                act_func_set_id=6, ins=[], outs=[]))
        self.sec("scalar", f_ac)

    def sec(self, eng, fn):
        if eng == "vector":
            getattr(self.blk, eng)(lambda e: fn(_EngProxy(e, self.dv)))
        elif eng == "scalar":
            getattr(self.blk, eng)(lambda e: fn(_EngProxy(e, self.ac)))
        elif eng == "gpsimd":
            getattr(self.blk, eng)(lambda e: fn(_EngProxy(e, self.gp)))
        else:
            getattr(self.blk, eng)(fn)

    def build(self):
        nc = self.nc
        dt = nc.dram_tensor
        self.xin = dt("xin", [T, D], F32, kind="ExternalInput")
        self.xprev = dt("xprev", [T, D], F32, kind="ExternalInput")
        self.flag = dt("flag", [128, 1], F32, kind="ExternalInput")
        self.sel = dt("sel", [128, 8], F32, kind="ExternalInput")
        self.ident = dt("ident", [128, 128], BF16, kind="ExternalInput")
        self.triu = dt("triu", [128, 128], F32, kind="ExternalInput")
        self.wq = dt("wq", [D, D], F32, kind="ExternalInput")
        self.wk = dt("wk", [D, D], F32, kind="ExternalInput")
        self.wv = dt("wv", [D, D], F32, kind="ExternalInput")
        self.wo = dt("wo", [D, D], F32, kind="ExternalInput")
        self.bq = dt("bq", [128, DC], F32, kind="ExternalInput")
        self.bk = dt("bk", [128, DC], F32, kind="ExternalInput")
        self.bv_r = dt("bv_r", [1, D], F32, kind="ExternalInput")
        self.bo_r = dt("bo_r", [1, D], F32, kind="ExternalInput")
        self.w1t = dt("w1t", [D, FH], F32, kind="ExternalInput")
        self.w2t = dt("w2t", [D, FH], F32, kind="ExternalInput")
        self.h1t = dt("h1t", [FH, FH], F32, kind="ExternalInput")
        self.h2t = dt("h2t", [FH, FH], F32, kind="ExternalInput")
        self.w3t = dt("w3t", [FH, D], F32, kind="ExternalInput")
        self.w1s = dt("w1s", [128, FC], F32, kind="ExternalInput")
        self.w2s = dt("w2s", [128, FC], F32, kind="ExternalInput")
        self.h1s = dt("h1s", [128, FC], F32, kind="ExternalInput")
        self.h2s = dt("h2s", [128, FC], F32, kind="ExternalInput")
        self.w3s_r = dt("w3s_r", [1, D], F32, kind="ExternalInput")
        self.wint = dt("wint", [D, HID], F32, kind="ExternalInput")
        self.wgatet = dt("wgatet", [D, HID], F32, kind="ExternalInput")
        self.rgint = dt("rgint", [HID, HID], F32, kind="ExternalInput")
        self.rggatet = dt("rggatet", [HID, HID], F32, kind="ExternalInput")
        self.woutt = dt("woutt", [HID, D], F32, kind="ExternalInput")
        self.mgatet = dt("mgatet", [D, HID], F32, kind="ExternalInput")
        self.mupt = dt("mupt", [D, HID], F32, kind="ExternalInput")
        self.mdownt = dt("mdownt", [HID, D], F32, kind="ExternalInput")
        self.convw = dt("convw", [128, GC * 4], F32, kind="ExternalInput")
        self.convb = dt("convb", [128, GC], F32, kind="ExternalInput")
        self.rginb = dt("rginb", [128, GC], F32, kind="ExternalInput")
        self.rggateb = dt("rggateb", [128, GC], F32, kind="ExternalInput")
        self.lam = dt("lam", [128, GC], F32, kind="ExternalInput")
        self.n1w_r = dt("n1w_r", [1, D], F32, kind="ExternalInput")
        self.n2w_r = dt("n2w_r", [1, D], F32, kind="ExternalInput")
        self.out = dt("out", [T, D], F32, kind="ExternalOutput")
        self.x2sp = dt("x2sp", [T, D], F32)
        self.ksp = dt("ksp", [16, 128, 1024], BF16)
        self.vsp = dt("vsp", [16, 128, 1024], BF16)
        self.qsp = dt("qsp", [16, 128, 1024], BF16)
        self.p2sp = dt("p2sp", [4, 128, GC * QT], BF16)
        self.c_w1 = dt("c_w1", [FC, 128, DC * 128], BF16)
        self.c_w2 = dt("c_w2", [FC, 128, DC * 128], BF16)
        self.c_h1 = dt("c_h1", [FC, 128, FC * 128], BF16)
        self.c_h2 = dt("c_h2", [FC, 128, FC * 128], BF16)
        self.c_w3 = dt("c_w3", [FC, 128, D], BF16)
        self.cc_ut_in = dt("cc_ut_in", [128, GC * 3], F32)
        self.cc_ut_out = dt("cc_ut_out", [128 * NCORES, GC * 3], F32, addr_space="Shared")
        self.cc_h_in = dt("cc_h_in", [128, GC], F32)
        self.cc_h_out = dt("cc_h_out", [128 * NCORES, GC], F32, addr_space="Shared")

        with ExitStack() as ctx:
            en = ctx.enter_context
            st = nc.sbuf_tensor
            self.xres = en(st("xres", [128, NT * D], F32))
            self.abf = en(st("abf", [128, 32768], BF16))
            self.wbuf = en(st("wbuf", [128, 18432], BF16))
            self.fscr = en(st("fscr", [128, 4096], F32))
            self.cid = en(st("cid", [128, 128], BF16))
            self.ctri = en(st("ctri", [128, 128], F32))
            self.cones = en(st("cones", [128, 1], BF16))
            self.fones = en(st("fones", [128, 1], BF16))
            self.cflag = en(st("cflag", [128, 1], F32))
            self.csel = en(st("csel", [128, 8], F32))
            self.cbq = en(st("cbq", [128, DC], F32))
            self.cbk = en(st("cbk", [128, DC], F32))
            self.cb0 = en(st("cb0", [128, D], F32))
            self.cb1 = en(st("cb1", [128, D], F32))
            self.cw1s = en(st("cw1s", [128, FC], F32))
            self.cw2s = en(st("cw2s", [128, FC], F32))
            self.ch1s = en(st("ch1s", [128, FC], F32))
            self.ch2s = en(st("ch2s", [128, FC], F32))
            self.cconvw = en(st("cconvw", [128, GC * 4], F32))
            self.cconvb = en(st("cconvb", [128, GC], F32))
            self.crginb = en(st("crginb", [128, GC], F32))
            self.crggateb = en(st("crggateb", [128, GC], F32))
            self.clam = en(st("clam", [128, GC], F32))
            self.clamp = en(st("clamp", [128, GC], F32))
            self.czero = en(st("czero", [128, QT], F32))
            self.stats = en(st("stats", [128, 16], F32))
            self.zsb = en(st("zsb", [128, NH], BF16))
            self.ssf = en(st("ssf", [64, 1024], F32))
            self.zsf = en(st("zsf", [64, NH], F32))
            self.uhalo = en(st("uhalo", [128, GC * 3], F32))
            self.utail = en(st("utail", [128, GC * 3], F32))
            self.utg = en(st("utg", [128, 4 * GC * 3], F32))
            self.urecv = en(st("urecv", [128, GC * 3], F32))
            self.hcol = en(st("hcol", [128, GC], F32))
            self.pcol = en(st("pcol", [128, GC], F32))
            self.hg = en(st("hg", [128, 4 * GC], F32))
            self.hrecv = en(st("hrecv", [128, GC], F32))
            self.qzr = en(st("qzr", [128, NH], F32))
            self.cvals = en(st("cvals", [128, 2], F32))
            nc.const_aps.register(F32, EPS, self.cvals[:, 0:1])
            nc.const_aps.register(F32, 1.0, self.cvals[:, 1:2])
            pt = nc.psum_tensor
            self.G = [en(pt(f"G{i}", [128, 512], F32)) for i in range(4)]
            self.PT = en(pt("PTr", [128, 1024], BF16))
            self.PZ = en(pt("PZ", [128, 512], F32))
            self.PSA = en(pt("PSA", [128, 512], F32))
            self.PSB = en(pt("PSB", [128, 512], F32))

            self.pe = Sem(ctx, nc, "pe")
            self.gp = Sem(ctx, nc, "gp")
            self.dv = Sem(ctx, nc, "dv")
            self.ac = Sem(ctx, nc, "ac")
            self.s_x0 = Sem(ctx, nc, "s_x0")
            self.s_const = Sem(ctx, nc, "s_const")
            self.s_wvo = Sem(ctx, nc, "s_wvo")
            self.s_xp = [Sem(ctx, nc, f"s_xp{i}") for i in range(2)]
            self.s_wc = [Sem(ctx, nc, f"s_wc{i}") for i in range(2)]
            self.s_wf = [Sem(ctx, nc, f"s_wf{i}") for i in range(2)]
            self.s_wfc = [Sem(ctx, nc, f"s_wfc{i}") for i in range(2)]
            self.s_w3c = [Sem(ctx, nc, f"s_w3c{i}") for i in range(2)]
            self.s_w3 = [Sem(ctx, nc, f"s_w3{i}") for i in range(2)]
            self.s_wb = [Sem(ctx, nc, f"s_wb{i}") for i in range(2)]
            self.s_rows = Sem(ctx, nc, "s_rows")
            self.s_x2w = Sem(ctx, nc, "s_x2w")
            self.s_x2r = [Sem(ctx, nc, f"s_x2r{i}") for i in range(2)]
            self.s_p2 = Sem(ctx, nc, "s_p2")
            self.s_cc = Sem(ctx, nc, "s_cc")
            self.s_stage = Sem(ctx, nc, "s_stage")
            self.s_gth = Sem(ctx, nc, "s_gth")
            self.s_out = Sem(ctx, nc, "s_out")
            self.s_ksp = Sem(ctx, nc, "s_ksp")
            self.s_x15 = Sem(ctx, nc, "s_x15")
            self.s_kv = [Sem(ctx, nc, f"s_kv{i}") for i in range(2)]

            self.blk = en(nc.Block())
            self.emit_setup()
            self.emit_attn_a()
            if "ffn1" in self.phases:
                self.emit_ffn(first=True)
            if "attn_b" in self.phases:
                self.emit_attn_b()
            if "ffn2" in self.phases:
                self.emit_ffn(first=False)
            self.emit_store()
        return nc

    def emit_setup(self):
        k = self

        def f_sync(sy):
            for t in range(NT):
                k.s_x0.inc(sy.dma_start(out=k.xres[:, t * D:(t + 1) * D],
                                        in_=k.xin[t * 128:(t + 1) * 128, :]), 16)
            for dst, src in [(k.cid, k.ident), (k.ctri, k.triu), (k.cflag, k.flag),
                             (k.csel, k.sel), (k.cbq, k.bq), (k.cbk, k.bk),
                             (k.cw1s, k.w1s), (k.cw2s, k.w2s), (k.ch1s, k.h1s),
                             (k.ch2s, k.h2s), (k.cconvw, k.convw), (k.cconvb, k.convb),
                             (k.crginb, k.rginb), (k.crggateb, k.rggateb),
                             (k.clam, k.lam)]:
                k.s_const.inc(sy.dma_start(out=dst[:], in_=src[:, :]), 16)
            for dst, src in [(k.cb0, k.bv_r), (k.cb1, k.bo_r)]:
                k.s_const.inc(sy.dma_start(out=dst[:], in_=src.ap().broadcast_to([128, D])), 16)
        k.sec("sync", f_sync)

        def f_gp(g):
            for d in range(DC):
                k.s_wvo.inc(g.dma_start(out=k.wbuf[:, WVO + d * D:WVO + (d + 1) * D],
                                        in_=k.wv[d * 128:(d + 1) * 128, :]), 16)
            for v in range(DC):
                k.s_wvo.inc(g.dma_start(out=k.wbuf[:, WOO + v * D:WOO + (v + 1) * D],
                                        in_=k.wo[v * 128:(v + 1) * 128, :]), 16)
        k.sec("gpsimd", f_gp)

        def f_dv(v):
            k.s_const.wait(v)
            v.memset(k.cvals[:, 0:1], EPS)
            v.memset(k.cvals[:, 1:2], 1.0)
            v.memset(k.czero[:], 0.0)
            v.memset(k.cones[:], 1.0)
            v.memset(k.ssf[:], 0.0)
            v.memset(k.zsf[:], 0.0)
            v.memset(k.uhalo[:], 0.0)
            v.memset(k.hcol[:], 0.0)
            v.memset(k.pcol[:], 1.0)
            v.tensor_scalar(k.fones[:], k.cones[:], k.cflag[:, 0:1], None, ALU.mult)
        k.sec("vector", f_dv)

        def f_ac(s):
            k.dv.wait(s)
            s.activation(k.fscr[:, ESC1:ESC1 + GC], k.clam[:], AF.Abs)
            s.activation(k.fscr[:, ESC2:ESC2 + GC], k.fscr[:, ESC1:ESC1 + GC], AF.Exp, scale=-1.0)
        k.sec("scalar", f_ac)

        def f_dv2(v):
            k.ac.wait(v)
            v.tensor_scalar_max(k.fscr[:, ESC1:ESC1 + GC], k.clam[:], 0.0)
        k.sec("vector", f_dv2)

        def f_ac2(s):
            k.dv.wait(s)
            s.activation(k.fscr[:, ESC2:ESC2 + GC], k.fscr[:, ESC2:ESC2 + GC], AF.Ln, bias=1.0)
        k.sec("scalar", f_ac2)

        def f_dv3(v):
            k.ac.wait(v)
            v.tensor_add(k.clamp[:], k.fscr[:, ESC2:ESC2 + GC], k.fscr[:, ESC1:ESC1 + GC])
            v.tensor_scalar_mul(k.clamp[:], k.clamp[:], -8.0)
        k.sec("vector", f_dv3)

    def ln_group(self, items, rms=False, gamma=None):
        k = self
        n = len(items)

        def f_dv(v):
            for i, (src_, pw, _) in enumerate(items):
                if pw is not None:
                    pw[0].wait(v, pw[1])
                if not rms:
                    v.reduce_sum(k.stats[:, i:i + 1], src_, axis=AX)
                v.scalar_tensor_tensor(k.abf[:, self.tnrm:self.tnrm + D], src_, 1.0, src_,
                                       ALU.mult, ALU.mult,
                                       accum_out=k.stats[:, 4 + i:5 + i])
            if rms:
                v.tensor_scalar_mul(k.stats[:, 12:12 + n], k.stats[:, 4:4 + n], 1.0 / D)
            else:
                v.tensor_scalar_mul(k.stats[:, 8:8 + n], k.stats[:, 0:n], 1.0 / D)
                v.tensor_scalar_mul(k.stats[:, 12:12 + n], k.stats[:, 4:4 + n], 1.0 / D)
                v.tensor_tensor(k.stats[:, 0:n], k.stats[:, 8:8 + n],
                                k.stats[:, 8:8 + n], ALU.mult)
                v.tensor_tensor(k.stats[:, 12:12 + n], k.stats[:, 12:12 + n],
                                k.stats[:, 0:n], ALU.subtract)
        k.sec("vector", f_dv)

        def f_ac(s):
            k.dv.wait(s)
            s.activation(k.stats[:, 12:12 + n], k.stats[:, 12:12 + n], AF.Sqrt, bias=EPS)
        k.sec("scalar", f_ac)

        def f_dv2(v):
            k.ac.wait(v)
            v.reciprocal(k.stats[:, 12:12 + n], k.stats[:, 12:12 + n])
        k.sec("vector", f_dv2)

        for i, (src_, _, dst) in enumerate(items):
            def f_dv3(v, i=i, src_=src_):
                if rms:
                    v.tensor_scalar(k.fscr[:, SQSC:SQSC + D], src_,
                                    k.stats[:, 12 + i:13 + i], None, ALU.mult)
                    v.tensor_tensor(k.abf[:, k.tnrm:k.tnrm + D],
                                    k.fscr[:, SQSC:SQSC + D], gamma, ALU.mult)
                else:
                    v.tensor_scalar(k.abf[:, k.tnrm:k.tnrm + D], src_,
                                    k.stats[:, 8 + i:9 + i], k.stats[:, 12 + i:13 + i],
                                    ALU.subtract, ALU.mult)
            k.sec("vector", f_dv3)
            self.tr_tile(self.tnrm, *dst)

    def ln_group_f32(self, items):
        k = self
        n = len(items)

        def f_dv(v):
            for i, (src_, pw, _) in enumerate(items):
                if pw is not None:
                    pw[0].wait(v, pw[1])
                v.reduce_sum(k.stats[:, i:i + 1], src_, axis=AX)
                v.scalar_tensor_tensor(k.abf[:, self.tnrm:self.tnrm + D], src_, 1.0, src_,
                                       ALU.mult, ALU.mult,
                                       accum_out=k.stats[:, 4 + i:5 + i])
            v.tensor_scalar_mul(k.stats[:, 8:8 + n], k.stats[:, 0:n], 1.0 / D)
            v.tensor_scalar_mul(k.stats[:, 12:12 + n], k.stats[:, 4:4 + n], 1.0 / D)
            v.tensor_tensor(k.stats[:, 0:n], k.stats[:, 8:8 + n],
                            k.stats[:, 8:8 + n], ALU.mult)
            v.tensor_tensor(k.stats[:, 12:12 + n], k.stats[:, 12:12 + n],
                            k.stats[:, 0:n], ALU.subtract)
        k.sec("vector", f_dv)

        def f_ac(s):
            k.dv.wait(s)
            s.activation(k.stats[:, 12:12 + n], k.stats[:, 12:12 + n], AF.Sqrt, bias=EPS)
        k.sec("scalar", f_ac)

        def f_dv2(v):
            k.ac.wait(v)
            v.reciprocal(k.stats[:, 12:12 + n], k.stats[:, 12:12 + n])
            for i, (src_, _, dst_) in enumerate(items):
                v.tensor_scalar(dst_, src_, k.stats[:, 8 + i:9 + i],
                                k.stats[:, 12 + i:13 + i], ALU.subtract, ALU.mult)
        k.sec("vector", f_dv2)

    def ln_tile(self, src, out_view, rms=False, gamma=None, pre_wait=None):
        k = self

        def f_dv(v):
            if pre_wait is not None:
                pre_wait[0].wait(v, pre_wait[1])
            if not rms:
                v.reduce_sum(k.stats[:, 0:1], src, axis=AX)
            v.scalar_tensor_tensor(k.fscr[:, SQSC:SQSC + D], src, 1.0, src,
                                            ALU.mult, ALU.mult,
                                            accum_out=k.stats[:, 1:2])
        k.sec("vector", f_dv)

        def f_dv2(v):
            if rms:
                v.tensor_scalar_mul(k.stats[:, 4:5], k.stats[:, 1:2], 1.0 / D)
            else:
                v.tensor_scalar_mul(k.stats[:, 2:3], k.stats[:, 0:1], 1.0 / D)
                v.tensor_scalar_mul(k.stats[:, 3:4], k.stats[:, 1:2], 1.0 / D)
                v.tensor_tensor(k.stats[:, 5:6], k.stats[:, 2:3], k.stats[:, 2:3], ALU.mult)
                v.tensor_tensor(k.stats[:, 4:5], k.stats[:, 3:4],
                                         k.stats[:, 5:6], ALU.subtract)
        k.sec("vector", f_dv2)

        def f_ac(s):
            k.dv.wait(s)
            s.activation(k.stats[:, 6:7], k.stats[:, 4:5], AF.Sqrt, bias=EPS)
        k.sec("scalar", f_ac)

        def f_dv3(v):
            k.ac.wait(v)
            v.reciprocal(k.stats[:, 6:7], k.stats[:, 6:7])
            if rms:
                v.tensor_scalar(k.fscr[:, SQSC:SQSC + D], src, k.stats[:, 6:7], None, ALU.mult)
                v.tensor_tensor(out_view, k.fscr[:, SQSC:SQSC + D], gamma, ALU.mult)
            else:
                v.tensor_scalar(out_view, src, k.stats[:, 2:3], k.stats[:, 6:7],
                                         ALU.subtract, ALU.mult)
        k.sec("vector", f_dv3)

    def tr_tile(self, src_off, fm_off, cs, t_off, nch=DC):
        k = self

        def f_pe(te):
            k.dv.wait(te)
            for d in range(nch):
                ins = te.transpose(k.PT[:, d * 128:(d + 1) * 128],
                                   k.abf[:, src_off + d * 128:src_off + (d + 1) * 128],
                                   k.cid[:])
            k.pe.inc(ins)
        k.sec("tensor", f_pe)

        def f_dv(v):
            k.pe.wait(v)
            dst = k.abf[:, fm_off:fm_off + nch * cs].rearrange(
                "p (c q) -> p c q", q=cs)[:, :, t_off:t_off + 128]
            v.tensor_copy(dst, k.PT[:, 0:nch * 128].rearrange(
                "p (c q) -> p c q", q=128))
        k.sec("vector", f_dv)

    def emit_attn_a(self):
        for q in range(8):
            self._a2_pass1(q)
        self.kv_free = [0, 0]
        for q in range(4):
            self._a2_pass2(q)

    def _a2_pass1(self, q):
        k = self
        own = q >= 4
        qq = q % 4

        k.tnrm = TNRM
        if own:
            items = [(k.xres[:, (qq * 4 + tl) * D:(qq * 4 + tl + 1) * D],
                      (k.s_x0, 16 * NT), (LNQ, QT, tl * 128)) for tl in range(4)]
            self.ln_group(items)
        else:
            for tp in range(2):
                items = []
                for tl in (tp * 2, tp * 2 + 1):
                    t_glob = qq * 4 + tl
                    slot = tl % 2

                    def f_sy(sy, t_glob=t_glob, slot=slot):
                        if k.xp_free[slot] > 0:
                            k.dv.wait(sy, k.xp_free[slot])
                        k.s_xp[slot].inc(sy.dma_start(
                            out=k.fscr[:, slot * 1024:(slot + 1) * 1024],
                            in_=k.xprev[t_glob * 128:(t_glob + 1) * 128, :]), 16)
                    k.sec("sync", f_sy)
                    items.append((k.fscr[:, slot * 1024:(slot + 1) * 1024],
                                  (k.s_xp[slot], k.s_xp[slot].n), (LNQ, QT, tl * 128)))
                self.ln_group(items)
                k.xp_free[0] = k.dv.n
                k.xp_free[1] = k.dv.n

        def f_dv(v):
            k.s_ksp.wait(v)
        k.sec("vector", f_dv)
        self._proj_elu(k.wk, k.cbk, KFM)

        for cl in range(4):
            self._a2_state_chunk(q, own, qq, cl)

        if own:
            def f_dv(v):
                k.s_ksp.wait(v)
            k.sec("vector", f_dv)
            self._proj_elu(k.wq, k.cbq, KFM)
            for cl in range(4):
                idx = qq * 4 + cl

                def f_sy(sy, cl=cl, idx=idx):
                    k.dv.wait(sy)
                    k.s_ksp.inc(sy.dma_start(
                        out=k.qsp[idx],
                        in_=k.abf[:, KFM:KFM + 4096].rearrange(
                            "p (d c) -> p d c", c=QT)[:, :, cl * 128:(cl + 1) * 128]), 16)
                k.sec("sync", f_sy)

    def _a2_state_chunk(self, q, own, qq, cl):
        k = self
        c_off = cl * 128
        g_c = q * 4 + cl

        def f_pe(te, cl=cl):
            k.s_wvo.wait(te)
            k.dv.wait(te)
            for half in range(2):
                ps = k.G[2 + half]
                for d in range(DC):
                    ins = te.matmul(ps[:, :],
                                    k.abf[:, LNQ + d * QT + c_off:LNQ + d * QT + c_off + 128],
                                    k.wbuf[:, WVO + d * D + half * 512:WVO + d * D + (half + 1) * 512],
                                    start=(d == 0), stop=(d == DC - 1))
            k.pe.inc(ins)
        k.sec("tensor", f_pe)
        vproj_done = k.pe.n

        def f_dv(v, cl=cl, own=own, vproj_done=vproj_done):
            k.pe.wait(v, vproj_done)
            for half in range(2):
                ps = k.G[2 + half]
                dst = k.abf[:, VTOK + cl * 1024 + half * 512:VTOK + cl * 1024 + (half + 1) * 512]
                if own:
                    v.tensor_tensor(dst, ps[:, :], k.cb0[:, half * 512:(half + 1) * 512], ALU.add)
                else:
                    v.tensor_tensor(k.fscr[:, ESC1:ESC1 + 512], ps[:, :],
                                    k.cb0[:, half * 512:(half + 1) * 512], ALU.add)
                    v.tensor_scalar(dst, k.fscr[:, ESC1:ESC1 + 512],
                                    k.cflag[:, 0:1], None, ALU.mult)
        k.sec("vector", f_dv)
        v_done = k.dv.n

        def f_pe(te, cl=cl):
            k.dv.wait(te)
            for d in range(DC):
                ins = te.transpose(k.PT[:, d * 128:(d + 1) * 128],
                                   k.abf[:, KFM + d * QT + c_off:KFM + d * QT + c_off + 128],
                                   k.cid[:])
            k.pe.inc(ins)
        k.sec("tensor", f_pe)
        kt_pe = k.pe.n

        def f_dv(v, kt_pe=kt_pe):
            k.pe.wait(v, kt_pe)
            v.tensor_copy(k.abf[:, KTOK:KTOK + 1024], k.PT.ap())
        k.sec("vector", f_dv)
        state_gate = k.dv.n

        def f_pe(te, cl=cl, own=own, state_gate=state_gate):
            k.dv.wait(te, state_gate)
            ones = k.cones if own else k.fones
            for h in range(NH):
                ps = k.PSA if h < 8 else k.PSB
                col = (h % 8) * 64
                te.matmul(ps[0:64, col:col + 64],
                          k.abf[:, KTOK + h * 64:KTOK + (h + 1) * 64],
                          k.abf[:, VTOK + cl * 1024 + h * 64:VTOK + cl * 1024 + (h + 1) * 64],
                          start=(h in (0, 8)), stop=(h in (7, 15)))
                ins = te.matmul(k.PZ[0:64, h:h + 1],
                                k.abf[:, KTOK + h * 64:KTOK + (h + 1) * 64],
                                ones[:], start=(h == 0), stop=(h == 15))
            k.pe.inc(ins)
        k.sec("tensor", f_pe)
        st_pe = k.pe.n

        def f_dv(v, st_pe=st_pe, g_c=g_c):
            k.pe.wait(v, st_pe)
            v.tensor_add(k.ssf[:, 0:512], k.ssf[:, 0:512], k.PSA[0:64, :])
            v.tensor_add(k.ssf[:, 512:1024], k.ssf[:, 512:1024], k.PSB[0:64, :])
            v.tensor_add(k.zsf[:, :], k.zsf[:, :], k.PZ[0:64, 0:NH])
            j = g_c - 15
            if 0 <= j <= 15:
                sv = k.abf[0:64, SNP + j * 1040:SNP + (j + 1) * 1040].rearrange(
                    "p (h c) -> p h c", c=65)
                v.tensor_copy(sv[:, :, 0:64], k.ssf.ap().rearrange("p (h c) -> p h c", c=64))
                v.tensor_copy(sv[:, :, 64:65], k.zsf.ap().rearrange("p (h c) -> p h c", c=1))
        k.sec("vector", f_dv)

        if own:
            idx = qq * 4 + cl

            def f_sy(sy, cl=cl, idx=idx):
                k.dv.wait(sy)
                k.s_ksp.inc(sy.dma_start(
                    out=k.ksp[idx],
                    in_=k.abf[:, KFM:KFM + 4096].rearrange(
                        "p (d c) -> p d c", c=QT)[:, :, cl * 128:(cl + 1) * 128]), 16)
                k.s_ksp.inc(sy.dma_start(
                    out=k.vsp[idx],
                    in_=k.abf[:, VTOK + cl * 1024:VTOK + (cl + 1) * 1024]), 16)
            k.sec("sync", f_sy)

    def _a2_pass2(self, q):
        k = self
        ksp_all = k.s_ksp.n
        kv_free = self.kv_free
        for cl in range(4):
            j = q * 4 + cl
            slot = cl % 2
            t_glob = q * 4 + cl

            def f_sy(sy, j=j, slot=slot):
                if q == 0 and cl < 2:
                    k.s_ksp.wait(sy, ksp_all)
                if kv_free[slot] > 0:
                    k.pe.wait(sy, kv_free[slot])
                k.s_kv[slot].inc(sy.dma_start(
                    out=k.abf[:, KCH + slot * 1024:KCH + (slot + 1) * 1024], in_=k.ksp[j]), 16)
                k.s_kv[slot].inc(sy.dma_start(
                    out=k.abf[:, VCH + slot * 1024:VCH + (slot + 1) * 1024], in_=k.vsp[j]), 16)
                k.s_kv[slot].inc(sy.dma_start(
                    out=k.abf[:, QCH + slot * 1024:QCH + (slot + 1) * 1024], in_=k.qsp[j]), 16)
            k.sec("sync", f_sy)
            kvn = k.s_kv[slot].n

            def f_dv(v, j=j):
                if q == 0 and cl == 0:
                    k.s_ksp.wait(v, ksp_all)
                v.tensor_copy(k.abf[0:64, SSB:SSB + 1040],
                              k.abf[0:64, SNP + j * 1040:SNP + (j + 1) * 1040])
                v.tensor_copy(k.abf[64:128, SSB:SSB + 1040],
                              k.abf[0:64, SNP + j * 1040:SNP + (j + 1) * 1040])
            k.sec("vector", f_dv)
            ssb_done = k.dv.n

            ko = KCH + slot * 1024
            vo = VCH + slot * 1024
            qo = QCH + slot * 1024

            for hb in range(2):
                hs = list(range(hb * 8, hb * 8 + 8))

                def f_pe(te, hs=hs, kvn=kvn, slot=slot):
                    k.s_kv[slot].wait(te, kvn)
                    k.dv.wait(te)
                    for h in hs:
                        pc, row = h // 2, (h % 2) * 64
                        ps = k.G[h % 2]
                        col = ((h % 8) // 2) * 128
                        ins = te.matmul(
                            ps[:, col:col + 128],
                            k.abf[row:row + 64, ko + pc * 128:ko + (pc + 1) * 128],
                            k.abf[row:row + 64, qo + pc * 128:qo + (pc + 1) * 128],
                            start=(h in (hs[0], hs[1])), stop=(h in (hs[6], hs[7])))
                    k.pe.inc(ins)
                k.sec("tensor", f_pe)
                a_pe = k.pe.n

                def f_dv(v, hs=hs, a_pe=a_pe):
                    k.pe.wait(v, a_pe)
                    for h in hs:
                        ps = k.G[h % 2]
                        col = ((h % 8) // 2) * 128
                        v.tensor_tensor(k.abf[:, AMSK + h * 128:AMSK + (h + 1) * 128],
                                        ps[:, col:col + 128], k.ctri[:], ALU.mult)
                k.sec("vector", f_dv)
                amask_done = k.dv.n

                def f_pe(te, hs=hs, amask_done=amask_done, ssb_done=ssb_done, slot=slot):
                    k.dv.wait(te, amask_done)
                    for h in hs:
                        ps = k.G[2 + h % 2]
                        col = ((h % 8) // 2) * 65
                        te.matmul(ps[:, col:col + 64],
                                  k.abf[:, AMSK + h * 128:AMSK + (h + 1) * 128],
                                  k.abf[:, vo + h * 64:vo + (h + 1) * 64],
                                  start=(h in (hs[0], hs[1])), stop=False)
                        te.matmul(ps[:, col + 64:col + 65],
                                  k.abf[:, AMSK + h * 128:AMSK + (h + 1) * 128],
                                  k.cones[:], start=False, stop=False)
                    for h in hs:
                        if h % 2 == 0:
                            pc, row = h // 2, 0
                            col = ((h % 8) // 2) * 65
                            te.matmul(k.G[2][:, col:col + 65],
                                      k.abf[row:row + 64, qo + pc * 128:qo + (pc + 1) * 128],
                                      k.abf[row:row + 64, SSB + h * 65:SSB + (h + 1) * 65],
                                      start=False, stop=(h == hs[6]))
                    for h in hs:
                        if h % 2 == 1:
                            pc, row = h // 2, 64
                            col = ((h % 8) // 2) * 65
                            ins = te.matmul(k.G[3][:, col:col + 65],
                                            k.abf[row:row + 64, qo + pc * 128:qo + (pc + 1) * 128],
                                            k.abf[row:row + 64, SSB + h * 65:SSB + (h + 1) * 65],
                                            start=False, stop=(h == hs[7]))
                    k.pe.inc(ins)
                k.sec("tensor", f_pe)
                o_pe = k.pe.n

                def f_dv(v, hs=hs, o_pe=o_pe):
                    k.pe.wait(v, o_pe)
                    for par in range(2):
                        ps = k.G[2 + par]
                        v.tensor_copy(k.qzr[:, par * 4:par * 4 + 4],
                                      ps[:, 0:260].rearrange("p (h c) -> p h c", c=65)[:, :, 64:65])
                    v.tensor_scalar_add(k.qzr[:, 0:8], k.qzr[:, 0:8], QZEPS)
                    v.reciprocal(k.qzr[:, 0:8], k.qzr[:, 0:8])
                    for h in hs:
                        ps = k.G[2 + h % 2]
                        col = ((h % 8) // 2) * 65
                        qi = (h % 2) * 4 + (h % 8) // 2
                        v.tensor_scalar(k.abf[:, VALS + h * 64:VALS + (h + 1) * 64],
                                        ps[:, col:col + 64], k.qzr[:, qi:qi + 1],
                                        None, ALU.mult)
                k.sec("vector", f_dv)

            self.tr_tile(VALS, VFM, 128, 0)
            vfm_done = k.dv.n

            def f_pe(te, vfm_done=vfm_done):
                k.dv.wait(te, vfm_done)
                for half in range(2):
                    ps = k.G[3] if half == 0 else k.G[0]
                    for vc in range(DC):
                        ins = te.matmul(ps[:, :],
                                        k.abf[:, VFM + vc * 128:VFM + (vc + 1) * 128],
                                        k.wbuf[:, WOO + vc * D + half * 512:WOO + vc * D + (half + 1) * 512],
                                        start=(vc == 0), stop=(vc == DC - 1))
                k.pe.inc(ins)
            k.sec("tensor", f_pe)
            wo_pe = k.pe.n
            kv_free[slot] = k.pe.n

            def f_dv(v, t_glob=t_glob, wo_pe=wo_pe):
                k.pe.wait(v, wo_pe)
                for half in range(2):
                    ps = k.G[3] if half == 0 else k.G[0]
                    xsl = k.xres[:, t_glob * D + half * 512:t_glob * D + (half + 1) * 512]
                    v.tensor_tensor(k.fscr[:, ESC1:ESC1 + 512], ps[:, :],
                                    k.cb1[:, half * 512:(half + 1) * 512], ALU.add)
                    v.tensor_add(xsl, xsl, k.fscr[:, ESC1:ESC1 + 512])
            k.sec("vector", f_dv)

    def _proj_elu(self, wsrc, bias, dstoff):
        k = self
        ln_done = k.dv.n
        pg_free = [0, 0]
        pg_ac = [0, 0]
        for dk in range(DC):
            slot = dk % 2

            def f_gp(g, dk=dk, slot=slot):
                if k.wcol_free[slot] > 0:
                    k.pe.wait(g, k.wcol_free[slot])
                k.s_wc[slot].inc(g.dma_start(
                    out=k.wbuf[:, WCOL + slot * 1024:WCOL + (slot + 1) * 1024].rearrange(
                        "p (i o) -> p i o", o=128),
                    in_=wsrc[:, dk * 128:(dk + 1) * 128].rearrange(
                        "(i p) o -> p i o", p=128)), 16)
            k.sec("gpsimd", f_gp)

            def f_pe(te, dk=dk, slot=slot):
                k.s_wc[slot].wait(te)
                k.dv.wait(te, ln_done)
                if pg_free[slot] > 0:
                    k.ac.wait(te, pg_ac[slot])
                    k.dv.wait(te, pg_free[slot])
                ps = k.G[slot]
                for d in range(DC):
                    ins = te.matmul(ps[:, :],
                                    k.wbuf[:, WCOL + slot * 1024 + d * 128:WCOL + slot * 1024 + (d + 1) * 128],
                                    k.abf[:, LNQ + d * QT:LNQ + (d + 1) * QT],
                                    start=(d == 0), stop=(d == DC - 1))
                k.pe.inc(ins)
                k.wcol_free[slot] = k.pe.n
            k.sec("tensor", f_pe)
            pe_done = k.pe.n

            mo, eo = (ESC1, ESC2) if slot == 0 else (SQSC, SQSC + 512)

            def f_dv(v, dk=dk, slot=slot, pe_done=pe_done, mo=mo):
                k.pe.wait(v, pe_done)
                v.tensor_scalar(k.fscr[:, mo:mo + 512], k.G[slot][:, :],
                                bias[:, dk:dk + 1], 0.0, ALU.add, ALU.min)
            k.sec("vector", f_dv)

            def f_ac(s, dk=dk, slot=slot, mo=mo, eo=eo):
                k.dv.wait(s)
                s.activation(k.fscr[:, eo:eo + 512], k.fscr[:, mo:mo + 512], AF.Exp)
            k.sec("scalar", f_ac)

            def f_dv2(v, dk=dk, slot=slot, mo=mo, eo=eo):
                k.ac.wait(v)
                v.tensor_scalar(k.fscr[:, mo:mo + 512], k.G[slot][:, :],
                                bias[:, dk:dk + 1], 0.0, ALU.add, ALU.max)
                v.tensor_add(k.abf[:, dstoff + dk * QT:dstoff + (dk + 1) * QT],
                             k.fscr[:, mo:mo + 512], k.fscr[:, eo:eo + 512])
            k.sec("vector", f_dv2)
            pg_free[slot] = k.dv.n

    def _reload_cb(self, pairs):
        k = self

        def f_sy(sy):
            k.dv.wait(sy)
            for dst, srcr in pairs:
                k.s_const.inc(sy.dma_start(out=dst[:], in_=srcr.ap().broadcast_to([128, D])), 16)
        k.sec("sync", f_sy)
        n = k.s_const.n

        def f_dv(v):
            k.s_const.wait(v, n)
        k.sec("vector", f_dv)

    def emit_ffn(self, first):
        self._reload_cb([(self.cb0, self.w3s_r)])
        pe0 = self.pe.n
        self._wf_free = [pe0, pe0]
        self._w3_free = [pe0, pe0]
        for s in range(4):
            self._ffn_subtile(s, first and s == 0)

    def _ffn_subtile(self, s, write_cache):
        k = self
        pe_barrier = k.pe.n

        k.tnrm = F_TNRM

        def f_dv(v):
            k.pe.wait(v, pe_barrier)
        k.sec("vector", f_dv)
        items = [(k.xres[:, (s * 4 + tl) * D:(s * 4 + tl + 1) * D], None,
                  (F_LNX, QT, tl * 128)) for tl in range(4)]
        self.ln_group(items)
        lnx_done = k.dv.n

        ga_done_marks = []
        for o in range(FC):
            slot = o % 2
            self._ffn_wload(o, slot, [(k.w1t, k.c_w1, 0), (k.w2t, k.c_w2, 1024)],
                            DC, write_cache)
            pa, pb = k.G[0 + 2 * slot], k.G[1 + 2 * slot]

            def f_pe(te, o=o, slot=slot, pa=pa, pb=pb):
                k.s_wf[slot].wait(te)
                k.s_wfc[slot].wait(te)
                k.dv.wait(te, lnx_done)
                if k._ffn_g_free[slot] > 0:
                    k.dv.wait(te, k._ffn_g_free[slot])
                base = FW_SL[slot]
                for d in range(DC):
                    te.matmul(pa[:, :], k.wbuf[:, base + d * 128:base + (d + 1) * 128],
                              k.abf[:, F_LNX + d * QT:F_LNX + (d + 1) * QT],
                              start=(d == 0), stop=(d == DC - 1))
                for d in range(DC):
                    ins = te.matmul(pb[:, :], k.wbuf[:, base + 1024 + d * 128:base + 1024 + (d + 1) * 128],
                                    k.abf[:, F_LNX + d * QT:F_LNX + (d + 1) * QT],
                                    start=(d == 0), stop=(d == DC - 1))
                k.pe.inc(ins)
                k._wf_free[slot] = k.pe.n
            k.sec("tensor", f_pe)
            mm_done = k.pe.n

            scl = ESC1 if o % 2 == 0 else ESC2

            def f_ac(s_, o=o, pa=pa, mm_done=mm_done, scl=scl):
                k.pe.wait(s_, mm_done)
                if k._ffn_scl_free[o % 2] > 0:
                    k.dv.wait(s_, k._ffn_scl_free[o % 2])
                s_.activation(k.fscr[:, scl:scl + 512], pa[:, :], AF.Silu,
                                       scale=k.cw1s[:, o:o + 1])
            k.sec("scalar", f_ac)

            def f_dv(v, o=o, pb=pb, scl=scl):
                k.ac.wait(v)
                v.scalar_tensor_tensor(
                    k.abf[:, F_GA + o * QT:F_GA + (o + 1) * QT], pb[:, :],
                    k.cw2s[:, o:o + 1], k.fscr[:, scl:scl + 512], ALU.mult, ALU.mult)
            k.sec("vector", f_dv)
            k._ffn_scl_free[o % 2] = k.dv.n
            k._ffn_g_free[slot] = k.dv.n
        ga_done = k.dv.n

        gb_free = [0, 0]
        for o in range(FC):
            slot = o % 2
            self._ffn_wload(o, slot, [(k.h1t, k.c_h1, 0)], FC, write_cache)
            ps = k.G[slot]

            def f_pe(te, o=o, slot=slot, ps=ps):
                k.s_wf[slot].wait(te)
                k.s_wfc[slot].wait(te)
                k.dv.wait(te, ga_done)
                if gb_free[slot] > 0:
                    k.ac.wait(te, gb_free[slot])
                base = FW_SL[slot]
                for i in range(FC):
                    ins = te.matmul(ps[:, :], k.wbuf[:, base + i * 128:base + (i + 1) * 128],
                                    k.abf[:, F_GA + i * QT:F_GA + (i + 1) * QT],
                                    start=(i == 0), stop=(i == FC - 1))
                k.pe.inc(ins)
                k._wf_free[slot] = k.pe.n
            k.sec("tensor", f_pe)
            mm_done = k.pe.n

            def f_ac(s_, o=o, ps=ps, mm_done=mm_done):
                k.pe.wait(s_, mm_done)
                s_.activation(k.abf[:, F_GB + o * QT:F_GB + (o + 1) * QT],
                                       ps[:, :], AF.Gelu, scale=k.ch1s[:, o:o + 1])
            k.sec("scalar", f_ac)
            gb_free[slot] = k.ac.n
        gb_done = k.ac.n

        gc_free = [0, 0]
        for o in range(FC):
            slot = o % 2
            self._ffn_wload(o, slot, [(k.h2t, k.c_h2, 0)], FC, write_cache)
            ps = k.G[slot]

            def f_pe(te, o=o, slot=slot, ps=ps):
                k.s_wf[slot].wait(te)
                k.s_wfc[slot].wait(te)
                k.ac.wait(te, gb_done)
                if gc_free[slot] > 0:
                    k.ac.wait(te, gc_free[slot])
                base = FW_SL[slot]
                for i in range(FC):
                    ins = te.matmul(ps[:, :], k.wbuf[:, base + i * 128:base + (i + 1) * 128],
                                    k.abf[:, F_GB + i * QT:F_GB + (i + 1) * QT],
                                    start=(i == 0), stop=(i == FC - 1))
                k.pe.inc(ins)
                k._wf_free[slot] = k.pe.n
            k.sec("tensor", f_pe)
            mm_done = k.pe.n

            def f_ac(s_, o=o, ps=ps, mm_done=mm_done):
                k.pe.wait(s_, mm_done)
                s_.activation(k.abf[:, F_GA + o * QT:F_GA + (o + 1) * QT],
                                       ps[:, :], AF.Gelu, scale=k.ch2s[:, o:o + 1])
            k.sec("scalar", f_ac)
            gc_free[slot] = k.ac.n
        gc_done = k.ac.n

        w3_ev = 0
        for hp in range(2):
            for i in range(FC):
                slot = i % 2
                if write_cache and hp == 0:
                    def f_gp(g, i=i, slot=slot):
                        if k._w3_free[slot] > 0:
                            k.pe.wait(g, k._w3_free[slot])
                        k.s_w3c[slot].wait(g)
                        k.s_w3[slot].inc(g.dma_start(
                            out=k.wbuf[:, WCOL + slot * 1024:WCOL + (slot + 1) * 1024],
                            in_=k.w3t[i * 128:(i + 1) * 128, :]), 16)
                    k.sec("gpsimd", f_gp)

                    def f_sy2(sy, i=i, slot=slot):
                        k.s_w3[slot].wait(sy)
                        k.s_w3c[slot].inc(sy.dma_start(
                            out=k.c_w3[i],
                            in_=k.wbuf[:, WCOL + slot * 1024:WCOL + (slot + 1) * 1024]), 16)
                    k.sec("sync", f_sy2)
                else:
                    def f_sy2(sy, i=i, slot=slot):
                        if k._w3_free[slot] > 0:
                            k.pe.wait(sy, k._w3_free[slot])
                        k.s_w3c[slot].inc(sy.dma_start(
                            out=k.wbuf[:, WCOL + slot * 1024:WCOL + (slot + 1) * 1024],
                            in_=k.c_w3[i]), 16)
                    k.sec("sync", f_sy2)

                def f_pe(te, i=i, slot=slot, hp=hp, w3_ev=w3_ev):
                    k.s_w3[slot].wait(te)
                    k.s_w3c[slot].wait(te)
                    if i == 0:
                        k.ac.wait(te, gc_done)
                        if hp == 0:
                            k.dv.wait(te)
                        else:
                            k.dv.wait(te, w3_ev)
                    for tt_ in range(2):
                        for half in range(2):
                            ps = k.G[tt_ * 2 + half]
                            ins = te.matmul(
                                ps[:, :],
                                k.abf[:, F_GA + i * QT + (hp * 2 + tt_) * 128:F_GA + i * QT + (hp * 2 + tt_) * 128 + 128],
                                k.wbuf[:, WCOL + slot * 1024 + half * 512:WCOL + slot * 1024 + (half + 1) * 512],
                                start=(i == 0), stop=(i == FC - 1))
                    k.pe.inc(ins)
                k.sec("tensor", f_pe)
                k._w3_free[slot] = k.pe.n
            w3_pe = k.pe.n

            def f_dv(v, s=s, hp=hp, w3_pe=w3_pe):
                k.pe.wait(v, w3_pe)
                for tt_ in range(2):
                    t_glob = s * 4 + hp * 2 + tt_
                    for half in range(2):
                        ps = k.G[tt_ * 2 + half]
                        xsl = k.xres[:, t_glob * D + half * 512:t_glob * D + (half + 1) * 512]
                        v.tensor_tensor(k.fscr[:, ESC1:ESC1 + 512], ps[:, :],
                                        k.cb0[:, half * 512:(half + 1) * 512], ALU.mult)
                        ins = v.tensor_add(xsl, xsl, k.fscr[:, ESC1:ESC1 + 512])
            k.sec("vector", f_dv)
            w3_ev = k.dv.n

    _ffn_g_free = None
    _ffn_scl_free = None
    _w3_free = None

    def _ffn_wload(self, o, slot, specs, itiles, write_cache):
        k = self
        if write_cache:
            def f_gp(g, o=o, slot=slot):
                if k._wf_free[slot] > 0:
                    k.pe.wait(g, k._wf_free[slot])
                k.s_wfc[slot].wait(g)
                for wsrc, _, doff in specs:
                    for i8 in range(0, itiles, 8):
                        k.s_wf[slot].inc(g.dma_start(
                            out=k.wbuf[:, FW_SL[slot] + doff + i8 * 128:FW_SL[slot] + doff + (i8 + 8) * 128].rearrange(
                                "p (i c) -> p i c", c=128),
                            in_=wsrc[i8 * 128:(i8 + 8) * 128, o * 128:(o + 1) * 128].rearrange(
                                "(i p) c -> p i c", p=128)), 16)
            k.sec("gpsimd", f_gp)

            def f_sy(sy, o=o, slot=slot):
                k.s_wf[slot].wait(sy)
                for _, cache, doff in specs:
                    k.s_wfc[slot].inc(sy.dma_start(
                        out=cache[o],
                        in_=k.wbuf[:, FW_SL[slot] + doff:FW_SL[slot] + doff + itiles * 128]), 16)
            k.sec("sync", f_sy)
        else:
            def f_sy(sy, o=o, slot=slot):
                if k._wf_free[slot] > 0:
                    k.pe.wait(sy, k._wf_free[slot])
                for _, cache, doff in specs:
                    k.s_wfc[slot].inc(sy.dma_start(
                        out=k.wbuf[:, FW_SL[slot] + doff:FW_SL[slot] + doff + itiles * 128],
                        in_=cache[o]), 16)
            k.sec("sync", f_sy)

    _wf_free = None

    def emit_attn_b(self):
        k = self
        pe_barrier = k.pe.n
        self._bw_free = [pe_barrier, pe_barrier]
        self._bw2_free = [pe_barrier, pe_barrier]
        self._reload_cb([(k.cb0, k.n1w_r), (k.cb1, k.n2w_r)])

        k.tnrm = B_TN

        def f_sy(sy):
            k.dv.wait(sy)
            k.s_x15.inc(sy.dma_start(out=k.x2sp[15 * 128:16 * 128, :],
                                     in_=k.xres[:, 15 * D:16 * D]), 16)
            for t in range(15):
                k.s_x2w.inc(sy.dma_start(out=k.x2sp[t * 128:(t + 1) * 128, :],
                                         in_=k.xres[:, t * D:(t + 1) * D]), 16)
        k.sec("sync", f_sy)
        x2w_n = k.s_x2w.n

        self.ln_group_f32([(k.xres[:, 15 * D:16 * D], (k.s_x15, k.s_x15.n),
                            k.xres[:, 15 * D:16 * D])])

        self.ln_group([(k.xres[:, 15 * D:16 * D], None, (B_N1, 128, 0))],
                      rms=True, gamma=k.cb0[:])
        n1e_done = k.dv.n
        for g_ in range(GC):
            slot = g_ % 2
            self._bcol_load(k.wint, g_, slot, DC)

            def f_pe(te, g_=g_, slot=slot):
                k.s_wb[slot].wait(te)
                k.dv.wait(te)
                for d in range(DC):
                    ins = te.matmul(k.G[0][:, 0:128],
                                    k.wbuf[:, BW_SL[slot] + d * 128:BW_SL[slot] + (d + 1) * 128],
                                    k.abf[:, B_N1 + d * 128:B_N1 + (d + 1) * 128],
                                    start=(d == 0), stop=(d == DC - 1))
                k.pe.inc(ins)
                k._bw_free[slot] = k.pe.n
            k.sec("tensor", f_pe)
            u15_pe = k.pe.n

            def f_dv(v, g_=g_, u15_pe=u15_pe):
                k.pe.wait(v, u15_pe)
                v.tensor_copy(k.utail[:, g_ * 3:(g_ + 1) * 3], k.G[0][:, 125:128])
            k.sec("vector", f_dv)
        utail_done = k.dv.n

        def f_gp(g):
            k.dv.wait(g, utail_done)
            k.s_stage.inc(g.dma_start(out=k.cc_ut_in[:, :], in_=k.utail[:]), 16)
            k.s_stage.wait(g)
            k.s_cc.inc(g.collective_compute(
                "AllGather", ALU.bypass, replica_groups=[list(range(NCORES))],
                ins=[k.cc_ut_in[:, :]], outs=[k.cc_ut_out[:, :]]), 1)
            k.s_cc.wait(g)
            for j, r in enumerate((0, 2, 4, 6)):
                k.s_gth.inc(g.dma_start(out=k.utg[:, j * GC * 3:(j + 1) * GC * 3],
                                        in_=k.cc_ut_out[r * 128:(r + 1) * 128, :]), 16)
        k.sec("gpsimd", f_gp)
        gth_n = k.s_gth.n

        for t0 in range(0, 15, 4):
            tls = list(range(t0, min(t0 + 4, 15)))
            self.ln_group_f32([(k.xres[:, t * D:(t + 1) * D],
                                (k.s_x2w, x2w_n) if t0 == 0 else None,
                                k.xres[:, t * D:(t + 1) * D]) for t in tls])

        def f_dv(v):
            k.s_gth.wait(v, gth_n)
            v.tensor_scalar(k.urecv[:], k.utg[:, 0:GC * 3], k.csel[:, 0:1], None, ALU.mult)
            for j, r in ((1, 2), (2, 4), (3, 6)):
                v.scalar_tensor_tensor(k.urecv[:], k.utg[:, j * GC * 3:(j + 1) * GC * 3],
                                             k.csel[:, r:r + 1], k.urecv[:], ALU.mult, ALU.add)
        k.sec("vector", f_dv)
        urecv_done = k.dv.n

        def f_gp(g):
            k.pe.wait(g, pe_barrier)
            for g_ in range(GC):
                k.s_rows.inc(g.dma_start(
                    out=k.wbuf[:, BW_ROWS + g_ * D:BW_ROWS + (g_ + 1) * D],
                    in_=k.woutt[g_ * 128:(g_ + 1) * 128, :]), 16)
        k.sec("gpsimd", f_gp)
        wout_n = k.s_rows.n

        for q in range(4):
            self._attnb_quarter(q, urecv_done, wout_n)

        hcol_done = k.dv.n

        def f_gp(g):
            k.dv.wait(g, hcol_done)
            k.s_stage.inc(g.dma_start(out=k.cc_h_in[:, :], in_=k.hcol[:]), 16)
            k.s_stage.wait(g)
            k.s_cc.inc(g.collective_compute(
                "AllGather", ALU.bypass, replica_groups=[list(range(NCORES))],
                ins=[k.cc_h_in[:, :]], outs=[k.cc_h_out[:, :]]), 1)
            k.s_cc.wait(g)
            for j, r in enumerate((0, 2, 4, 6)):
                k.s_gth.inc(g.dma_start(out=k.hg[:, j * GC:(j + 1) * GC],
                                        in_=k.cc_h_out[r * 128:(r + 1) * 128, :]), 16)
        k.sec("gpsimd", f_gp)
        gth_n2 = k.s_gth.n

        def f_dv(v):
            k.s_gth.wait(v, gth_n2)
            v.tensor_scalar(k.hrecv[:], k.hg[:, 0:GC], k.csel[:, 0:1], None, ALU.mult)
            for j, r in ((1, 2), (2, 4), (3, 6)):
                v.scalar_tensor_tensor(k.hrecv[:], k.hg[:, j * GC:(j + 1) * GC],
                                             k.csel[:, r:r + 1], k.hrecv[:], ALU.mult, ALU.add)
        k.sec("vector", f_dv)
        hrecv_done = k.dv.n

        for q in range(4):
            def f_sy(sy, q=q):
                k.s_p2.wait(sy)
                k.pe.wait(sy)
                k.s_p2.inc(sy.dma_start(out=k.abf[:, B_UC:B_UC + GC * QT], in_=k.p2sp[q]), 16)
            k.sec("sync", f_sy)
            p2n = k.s_p2.n

            def f_dv(v, q=q, p2n=p2n):
                k.s_p2.wait(v, p2n)
                k.pe.wait(v)
                k.dv.wait(v, hrecv_done)
                for g_ in range(GC):
                    ins = v.tensor_scalar(k.abf[:, B_SIG + g_ * QT:B_SIG + (g_ + 1) * QT],
                                          k.abf[:, B_UC + g_ * QT:B_UC + (g_ + 1) * QT],
                                          k.hrecv[:, g_:g_ + 1], None, ALU.mult)
            k.sec("vector", f_dv)
            self._mix_mms(q, k.dv.n, wout_n)

        def f_gp(g):
            k.pe.wait(g)
            for g_ in range(GC):
                k.s_rows.inc(g.dma_start(
                    out=k.wbuf[:, BW_ROWS + g_ * D:BW_ROWS + (g_ + 1) * D],
                    in_=k.mdownt[g_ * 128:(g_ + 1) * 128, :]), 16)
        k.sec("gpsimd", f_gp)
        mdown_n = k.s_rows.n

        for q in range(4):
            self._mlp_quarter(q, mdown_n)

        self.emit_x2_add()


    def ln_tile_f32(self, src, dst, pre_wait=None):
        k = self

        def f_dv(v):
            if pre_wait is not None:
                pre_wait[0].wait(v, pre_wait[1])
            v.reduce_sum(k.stats[:, 0:1], src, axis=AX)
            v.scalar_tensor_tensor(k.fscr[:, SQSC:SQSC + D], src, 1.0, src,
                                            ALU.mult, ALU.mult, accum_out=k.stats[:, 1:2])
        k.sec("vector", f_dv)

        def f_dv2(v):
            v.tensor_scalar_mul(k.stats[:, 2:3], k.stats[:, 0:1], 1.0 / D)
            v.tensor_scalar_mul(k.stats[:, 3:4], k.stats[:, 1:2], 1.0 / D)
            v.tensor_tensor(k.stats[:, 5:6], k.stats[:, 2:3], k.stats[:, 2:3], ALU.mult)
            v.tensor_tensor(k.stats[:, 4:5], k.stats[:, 3:4],
                                     k.stats[:, 5:6], ALU.subtract)
        k.sec("vector", f_dv2)

        def f_ac(s):
            k.dv.wait(s)
            s.activation(k.stats[:, 6:7], k.stats[:, 4:5], AF.Ln, bias=EPS)
            s.activation(k.stats[:, 6:7], k.stats[:, 6:7], AF.Exp, scale=-0.5)
        k.sec("scalar", f_ac)

        def f_dv3(v):
            k.ac.wait(v)
            v.tensor_scalar(dst, src, k.stats[:, 2:3], k.stats[:, 6:7],
                                     ALU.subtract, ALU.mult)
        k.sec("vector", f_dv3)

    _bw_free = None

    def _bcol_load(self, wsrc, g_, slot, itiles):
        k = self

        def f_gp(g, g_=g_, slot=slot):
            if k._bw_free[slot] > 0:
                k.pe.wait(g, k._bw_free[slot])
            k.s_wb[slot].inc(g.dma_start(
                out=k.wbuf[:, BW_SL[slot]:BW_SL[slot] + itiles * 128].rearrange(
                    "p (i c) -> p i c", c=128),
                in_=wsrc[0:itiles * 128, g_ * 128:(g_ + 1) * 128].rearrange(
                    "(i p) c -> p i c", p=128)), 16)
        k.sec("gpsimd", f_gp)

    def _attnb_quarter(self, q, urecv_done, wout_n):
        k = self
        k.tnrm = B_TN
        self.ln_group([(k.xres[:, (q * 4 + tl) * D:(q * 4 + tl + 1) * D], None,
                        (B_N1, QT, tl * 128)) for tl in range(4)],
                      rms=True, gamma=k.cb0[:])
        n1_done = k.dv.n

        for g_ in range(GC):
            slot = g_ % 2
            self._bcol_load(k.wgatet, g_, slot, DC)
            ps = k.G[slot]

            def f_pe(te, g_=g_, slot=slot, ps=ps):
                k.s_wb[slot].wait(te)
                k.dv.wait(te, n1_done)
                if k._bg_free[slot] > 0:
                    k.ac.wait(te, k._bg_free[slot])
                for d in range(DC):
                    ins = te.matmul(ps[:, :],
                                    k.wbuf[:, BW_SL[slot] + d * 128:BW_SL[slot] + (d + 1) * 128],
                                    k.abf[:, B_N1 + d * QT:B_N1 + (d + 1) * QT],
                                    start=(d == 0), stop=(d == DC - 1))
                k.pe.inc(ins)
                k._bw_free[slot] = k.pe.n
            k.sec("tensor", f_pe)
            mm_n = k.pe.n

            def f_ac(s_, g_=g_, ps=ps, mm_n=mm_n):
                k.pe.wait(s_, mm_n)
                s_.activation(k.abf[:, B_GG + g_ * QT:B_GG + (g_ + 1) * QT],
                                       ps[:, :], AF.Gelu)
            k.sec("scalar", f_ac)
            k._bg_free[slot] = k.ac.n
        gate_done = k.ac.n

        for g_ in range(GC):
            slot = g_ % 2
            self._bcol_load(k.wint, g_, slot, DC)
            ps = k.G[2 + slot]

            def f_pe(te, g_=g_, slot=slot, ps=ps):
                k.s_wb[slot].wait(te)
                k.dv.wait(te, n1_done)
                if k._bu_free[slot] > 0:
                    k.dv.wait(te, k._bu_free[slot])
                for d in range(DC):
                    ins = te.matmul(ps[:, :],
                                    k.wbuf[:, BW_SL[slot] + d * 128:BW_SL[slot] + (d + 1) * 128],
                                    k.abf[:, B_N1 + d * QT:B_N1 + (d + 1) * QT],
                                    start=(d == 0), stop=(d == DC - 1))
                k.pe.inc(ins)
                k._bw_free[slot] = k.pe.n
            k.sec("tensor", f_pe)
            mm_n = k.pe.n

            def f_dv(v, g_=g_, q=q, ps=ps, mm_n=mm_n):
                k.pe.wait(v, mm_n)
                if g_ == 0 and q > 0:
                    k.s_p2.wait(v)
                v.tensor_copy(k.fscr[:, UEXT + 3:UEXT + 3 + QT], ps[:, :])
            k.sec("vector", f_dv)
            k._bu_free[slot] = k.dv.n

            def f_gp(g, g_=g_, q=q):
                if q == 0:
                    g.tensor_copy(k.fscr[:, UEXT:UEXT + 3], k.urecv[:, g_ * 3:(g_ + 1) * 3])
                else:
                    g.tensor_copy(k.fscr[:, UEXT:UEXT + 3], k.uhalo[:, g_ * 3:(g_ + 1) * 3])
                g.tensor_copy(k.uhalo[:, g_ * 3:(g_ + 1) * 3], k.fscr[:, UEXT + QT:UEXT + QT + 3])
                g.tensor_scalar(k.fscr[:, S1:S1 + QT], k.fscr[:, UEXT + 3:UEXT + 3 + QT],
                                k.cconvw[:, g_ * 4 + 3:g_ * 4 + 4],
                                k.cconvb[:, g_:g_ + 1], ALU.mult, ALU.add)
                for kk in (2, 1):
                    g.tensor_scalar(k.fscr[:, ESC1:ESC1 + QT],
                                    k.fscr[:, UEXT + kk:UEXT + kk + QT],
                                    k.cconvw[:, g_ * 4 + kk:g_ * 4 + kk + 1], None, ALU.mult)
                    g.tensor_tensor(k.fscr[:, S1:S1 + QT], k.fscr[:, S1:S1 + QT],
                                    k.fscr[:, ESC1:ESC1 + QT], ALU.add)
                g.tensor_scalar(k.fscr[:, ESC1:ESC1 + QT], k.fscr[:, UEXT:UEXT + QT],
                                k.cconvw[:, g_ * 4:g_ * 4 + 1], None, ALU.mult)
                g.tensor_tensor(k.abf[:, B_UC + g_ * QT:B_UC + (g_ + 1) * QT],
                                k.fscr[:, S1:S1 + QT],
                                k.fscr[:, ESC1:ESC1 + QT], ALU.add)
            k.sec("vector", f_gp)
        uc_done = k.dv.n

        for g_ in range(GC):
            slot = g_ % 2
            self._bcol_load(k.rgint, g_, slot, GC)
            self._bcol_load2(k.rggatet, g_, slot, GC)
            pa, pb = k.G[0 + 2 * slot], k.G[1 + 2 * slot]

            def f_pe(te, g_=g_, slot=slot, pa=pa, pb=pb):
                k.s_wb[slot].wait(te)
                k.dv.wait(te, uc_done)
                k.ac.wait(te, max(gate_done, k._brg_free[slot]))
                for h in range(GC):
                    te.matmul(pa[:, :], k.wbuf[:, BW_SL2[slot] + h * 128:BW_SL2[slot] + (h + 1) * 128],
                              k.abf[:, B_UC + h * QT:B_UC + (h + 1) * QT],
                              start=(h == 0), stop=(h == GC - 1))
                for h in range(GC):
                    ins = te.matmul(pb[:, :], k.wbuf[:, BW_SL2[slot] + 1536 + h * 128:BW_SL2[slot] + 1536 + (h + 1) * 128],
                                    k.abf[:, B_UC + h * QT:B_UC + (h + 1) * QT],
                                    start=(h == 0), stop=(h == GC - 1))
                k.pe.inc(ins)
                k._bw_free[slot] = k.pe.n
                k._bw2_free[slot] = k.pe.n
            k.sec("tensor", f_pe)
            mm_n = k.pe.n

            def f_ac(s_, g_=g_, pa=pa, pb=pb, mm_n=mm_n):
                k.pe.wait(s_, mm_n)
                s_.activation(k.abf[:, B_SIG + g_ * QT:B_SIG + (g_ + 1) * QT], pa[:, :],
                              AF.Sigmoid, bias=k.crginb[:, g_:g_ + 1])
                s_.activation(k.abf[:, B_SRG + g_ * QT:B_SRG + (g_ + 1) * QT], pb[:, :],
                                       AF.Sigmoid, bias=k.crggateb[:, g_:g_ + 1])
            k.sec("scalar", f_ac)
            k._brg_free[slot] = k.ac.n
        rg_done = k.ac.n
        rg_pe = k.pe.n

        ABANK = [k.G[0], k.G[1], k.G[2], k.G[3], k.PZ, k.PSA]
        SLOTS = [(S2, S3, S4, S5), (0, 512, 3072, 3584)]
        la_done = [0, 0]
        set_free = [0, 0]
        for hf in range(2):
            gs = list(range(hf * 6, hf * 6 + 6))
            for j, g_ in enumerate(gs):
                sl_la = SLOTS[j % 2][0]

                def f_gp(g, g_=g_, q=q, j=j, sl_la=sl_la, hf=hf):
                    if g_ == 0:
                        k.ac.wait(g, rg_done)
                        k.pe.wait(g, rg_pe)
                        if q > 0:
                            k.s_p2.wait(g)
                    if la_done[j % 2] > 0:
                        k.ac.wait(g, la_done[j % 2])
                    g.tensor_scalar(k.fscr[:, sl_la:sl_la + QT],
                                    k.abf[:, B_SRG + g_ * QT:B_SRG + (g_ + 1) * QT],
                                    k.clamp[:, g_:g_ + 1], None, ALU.mult)
                k.sec("vector", f_gp)

                def f_ac(s_, j=j, sl_la=sl_la, hf=hf):
                    k.dv.wait(s_)
                    if j == 0:
                        k.pe.wait(s_)
                        k.dv.wait(s_)
                    s_.activation(ABANK[j][:, 0:QT], k.fscr[:, sl_la:sl_la + QT], AF.Exp)
                k.sec("scalar", f_ac)
                la_done[j % 2] = k.ac.n

            for j, g_ in enumerate(gs):
                s_la, s_z, s_b, s_h = SLOTS[j % 2]

                def f_ac(s_, j=j, s_z=s_z):
                    if set_free[j % 2] > 0:
                        k.dv.wait(s_, set_free[j % 2])
                    s_.activation(k.fscr[:, s_z:s_z + QT], ABANK[j][:, 0:QT], AF.Square)
                k.sec("scalar", f_ac)

                def f_gp(g, j=j, s_z=s_z):
                    k.ac.wait(g)
                    g.tensor_scalar(k.fscr[:, s_z:s_z + QT], k.fscr[:, s_z:s_z + QT],
                                    -1.0, 1.0 + 1e-6, ALU.mult, ALU.add)
                k.sec("vector", f_gp)

                def f_ac2(s_, j=j, s_z=s_z, s_b=s_b):
                    k.dv.wait(s_)
                    s_.activation(k.fscr[:, s_b:s_b + QT], k.fscr[:, s_z:s_z + QT], AF.Sqrt)
                k.sec("scalar", f_ac2)

                def f_gp2(g, j=j, g_=g_, s_z=s_z, s_b=s_b):
                    k.ac.wait(g)
                    g.tensor_tensor(k.fscr[:, s_z:s_z + QT], k.fscr[:, s_b:s_b + QT],
                                    k.abf[:, B_SIG + g_ * QT:B_SIG + (g_ + 1) * QT], ALU.mult)
                    g.tensor_tensor(k.fscr[:, s_z:s_z + QT], k.fscr[:, s_z:s_z + QT],
                                    k.abf[:, B_UC + g_ * QT:B_UC + (g_ + 1) * QT], ALU.mult)
                k.sec("vector", f_gp2)

                def f_dv(v, j=j, g_=g_, s_z=s_z, s_b=s_b, s_h=s_h):
                    v.tensor_tensor_scan(k.fscr[:, s_h:s_h + QT], ABANK[j][:, 0:QT],
                                         k.fscr[:, s_z:s_z + QT], k.hcol[:, g_:g_ + 1],
                                         ALU.mult, ALU.add)
                    v.tensor_copy(k.hcol[:, g_:g_ + 1], k.fscr[:, s_h + QT - 1:s_h + QT])
                    v.tensor_tensor_scan(k.fscr[:, s_b:s_b + QT], ABANK[j][:, 0:QT],
                                         k.czero[:, 0:QT], k.pcol[:, g_:g_ + 1],
                                         ALU.mult, ALU.add)
                    v.tensor_copy(k.pcol[:, g_:g_ + 1], k.fscr[:, s_b + QT - 1:s_b + QT])
                k.sec("vector", f_dv)

                def f_gp3(g, j=j, g_=g_, s_b=s_b, s_h=s_h):
                    g.tensor_tensor(k.abf[:, B_SIG + g_ * QT:B_SIG + (g_ + 1) * QT],
                                    k.abf[:, B_GG + g_ * QT:B_GG + (g_ + 1) * QT],
                                    k.fscr[:, s_h:s_h + QT], ALU.mult)
                    g.tensor_tensor(k.abf[:, B_UC + g_ * QT:B_UC + (g_ + 1) * QT],
                                    k.abf[:, B_GG + g_ * QT:B_GG + (g_ + 1) * QT],
                                    k.fscr[:, s_b:s_b + QT], ALU.mult)
                k.sec("vector", f_gp3)
                set_free[j % 2] = k.dv.n
        scan_done = k.dv.n

        def f_sy(sy, q=q):
            k.dv.wait(sy, scan_done)
            k.s_p2.inc(sy.dma_start(out=k.p2sp[q], in_=k.abf[:, B_UC:B_UC + GC * QT]), 16)
        k.sec("sync", f_sy)

        self._mix_mms(q, scan_done, wout_n)

    _bg_free = None
    _bu_free = None
    _uext_free = 0
    _brg_free = None
    _bw2_free = None

    def _bcol_load2(self, wsrc, g_, slot, itiles):
        k = self

        def f_gp(g, g_=g_, slot=slot):
            if k._bw2_free[slot] > 0:
                k.pe.wait(g, k._bw2_free[slot])
            k.s_wb[slot].inc(g.dma_start(
                out=k.wbuf[:, BW_SL2[slot] + 1536:BW_SL2[slot] + 1536 + itiles * 128].rearrange(
                    "p (i c) -> p i c", c=128),
                in_=wsrc[0:itiles * 128, g_ * 128:(g_ + 1) * 128].rearrange(
                    "(i p) c -> p i c", p=128)), 16)
        k.sec("gpsimd", f_gp)

    def _mix_mms(self, q, src_done, wout_n):
        k = self
        ev = [0, 0]
        for tl in range(4):
            t_glob = q * 4 + tl

            def f_pe(te, tl=tl, src_done=src_done, ev0=None):
                k.dv.wait(te, src_done)
                k.s_rows.wait(te, wout_n)
                if tl < 2:
                    k.dv.wait(te)
                elif ev[tl % 2] > 0:
                    k.dv.wait(te, ev[tl % 2])
                for half in range(2):
                    ps = k.G[tl % 2 * 2 + half]
                    for g_ in range(GC):
                        ins = te.matmul(ps[:, :],
                                        k.abf[:, B_SIG + g_ * QT + tl * 128:B_SIG + g_ * QT + tl * 128 + 128],
                                        k.wbuf[:, BW_ROWS + g_ * D + half * 512:BW_ROWS + g_ * D + (half + 1) * 512],
                                        start=(g_ == 0), stop=(g_ == GC - 1))
                k.pe.inc(ins)
            k.sec("tensor", f_pe)
            mm_n = k.pe.n

            def f_dv(v, t_glob=t_glob, tl=tl, mm_n=mm_n):
                k.pe.wait(v, mm_n)
                for half in range(2):
                    ps = k.G[tl % 2 * 2 + half]
                    xsl = k.xres[:, t_glob * D + half * 512:t_glob * D + (half + 1) * 512]
                    ins = v.tensor_add(xsl, xsl, ps[:, :])
            k.sec("vector", f_dv)
            ev[tl % 2] = k.dv.n

    def _mlp_quarter(self, q, mdown_n):
        k = self
        k.tnrm = B_TN
        self.ln_group([(k.xres[:, (q * 4 + tl) * D:(q * 4 + tl + 1) * D], None,
                        (B_N1, QT, tl * 128)) for tl in range(4)],
                      rms=True, gamma=k.cb1[:])
        n2_done = k.dv.n

        for g_ in range(GC):
            slot = g_ % 2
            self._bcol_load(k.mgatet, g_, slot, DC)
            self._bcol_load2(k.mupt, g_, slot, DC)
            pa, pb = k.G[0 + 2 * slot], k.G[1 + 2 * slot]

            def f_pe(te, g_=g_, slot=slot, pa=pa, pb=pb):
                k.s_wb[slot].wait(te)
                k.dv.wait(te, n2_done)
                if k._bmg_free[slot] > 0:
                    k.dv.wait(te, k._bmg_free[slot])
                for d in range(DC):
                    te.matmul(pa[:, :], k.wbuf[:, BW_SL[slot] + d * 128:BW_SL[slot] + (d + 1) * 128],
                              k.abf[:, B_N1 + d * QT:B_N1 + (d + 1) * QT],
                              start=(d == 0), stop=(d == DC - 1))
                for d in range(DC):
                    ins = te.matmul(pb[:, :], k.wbuf[:, BW_SL2[slot] + 1536 + d * 128:BW_SL2[slot] + 1536 + (d + 1) * 128],
                                    k.abf[:, B_N1 + d * QT:B_N1 + (d + 1) * QT],
                                    start=(d == 0), stop=(d == DC - 1))
                k.pe.inc(ins)
                k._bw_free[slot] = k.pe.n
                k._bw2_free[slot] = k.pe.n
            k.sec("tensor", f_pe)
            mm_n = k.pe.n

            def f_ac(s_, g_=g_, pa=pa, mm_n=mm_n):
                k.pe.wait(s_, mm_n)
                s_.activation(k.abf[:, B_GG + g_ * QT:B_GG + (g_ + 1) * QT],
                                       pa[:, :], AF.Gelu)
            k.sec("scalar", f_ac)

            def f_dv(v, g_=g_, pb=pb):
                k.ac.wait(v)
                v.tensor_tensor(k.abf[:, B_UC + g_ * QT:B_UC + (g_ + 1) * QT],
                                         pb[:, :], k.abf[:, B_GG + g_ * QT:B_GG + (g_ + 1) * QT],
                                         ALU.mult)
            k.sec("vector", f_dv)
            k._bmg_free[slot] = k.dv.n
        prd_done = k.dv.n

        dev = [0, 0]
        for tl in range(4):
            t_glob = q * 4 + tl

            def f_pe(te, tl=tl, prd_done=prd_done):
                k.dv.wait(te, prd_done)
                k.s_rows.wait(te, mdown_n)
                if tl < 2:
                    k.dv.wait(te)
                elif dev[tl % 2] > 0:
                    k.dv.wait(te, dev[tl % 2])
                for half in range(2):
                    ps = k.G[tl % 2 * 2 + half]
                    for g_ in range(GC):
                        ins = te.matmul(ps[:, :],
                                        k.abf[:, B_UC + g_ * QT + tl * 128:B_UC + g_ * QT + tl * 128 + 128],
                                        k.wbuf[:, BW_ROWS + g_ * D + half * 512:BW_ROWS + g_ * D + (half + 1) * 512],
                                        start=(g_ == 0), stop=(g_ == GC - 1))
                k.pe.inc(ins)
            k.sec("tensor", f_pe)
            mm_n = k.pe.n

            def f_dv(v, t_glob=t_glob, tl=tl, mm_n=mm_n):
                k.pe.wait(v, mm_n)
                for half in range(2):
                    ps = k.G[tl % 2 * 2 + half]
                    xsl = k.xres[:, t_glob * D + half * 512:t_glob * D + (half + 1) * 512]
                    ins = v.tensor_add(xsl, xsl, ps[:, :])
            k.sec("vector", f_dv)
            dev[tl % 2] = k.dv.n

    _bmg_free = None
    _x2add = None

    def emit_x2_add(self):
        k = self
        add_done = [0, 0]
        dv_base = k.dv.n
        for t in range(NT):
            def f_sy(sy, t=t):
                if t == 0:
                    k.dv.wait(sy, dv_base)
                if add_done[t % 2] > 0:
                    k.dv.wait(sy, add_done[t % 2])
                k.s_x2r[t % 2].inc(sy.dma_start(
                    out=k.fscr[:, XA + (t % 2) * 1024:XA + (t % 2) * 1024 + 1024],
                    in_=k.x2sp[t * 128:(t + 1) * 128, :]), 16)
            k.sec("sync", f_sy)
            x2n = k.s_x2r[t % 2].n

            def f_dv(v, t=t, x2n=x2n):
                k.s_x2r[t % 2].wait(v, x2n)
                xsl = k.xres[:, t * D:(t + 1) * D]
                v.tensor_add(xsl, xsl,
                                      k.fscr[:, XA + (t % 2) * 1024:XA + (t % 2) * 1024 + 1024])
            k.sec("vector", f_dv)
            add_done[t % 2] = k.dv.n

    def emit_store(self):
        k = self

        def f_sy(sy):
            k.dv.wait(sy)
            for t in range(NT):
                k.s_out.inc(sy.dma_start(out=k.out[t * 128:(t + 1) * 128, :],
                                         in_=k.xres[:, t * D:(t + 1) * D]), 16)
            k.s_out.wait(sy)
        k.sec("sync", f_sy)

    _x2gate = None


def _make_builder(phases):
    kb = KB(phases)
    kb._ffn_g_free = [0, 0]
    kb._ffn_scl_free = [0, 0]
    kb._w3_free = [0, 0]
    kb._wf_free = [0, 0]
    kb._bw_free = [0, 0]
    kb._bw2_free = [0, 0]
    kb._bg_free = [0, 0]
    kb._bu_free = [0, 0]
    kb._brg_free = [0, 0]
    kb._bmg_free = [0, 0]
    kb._x2gate = [0] * NT
    kb._x2add = [0] * NT
    return kb


def _shard_inputs(inputs):
    X = np.ascontiguousarray(np.asarray(inputs["L0_attn_a_x"], np.float32))

    def col_layout(v, chunks):
        return np.ascontiguousarray(np.asarray(v, np.float32).reshape(chunks, 128).T)

    common = dict(
        ident=np.eye(128, dtype=BF),
        triu=np.triu(np.ones((128, 128), dtype=np.float32)),
        wq=np.ascontiguousarray(np.asarray(inputs["L0_attn_a_Wq"], np.float32)),
        wk=np.ascontiguousarray(np.asarray(inputs["L0_attn_a_Wk"], np.float32)),
        wv=np.ascontiguousarray(np.asarray(inputs["L0_attn_a_Wv"], np.float32)),
        wo=np.ascontiguousarray(np.asarray(inputs["L0_attn_a_Wo"], np.float32)),
        bq=col_layout(inputs["L0_attn_a_bq"], DC),
        bk=col_layout(inputs["L0_attn_a_bk"], DC),
        bv_r=np.asarray(inputs["L0_attn_a_bv"], np.float32).reshape(1, D).copy(),
        bo_r=np.asarray(inputs["L0_attn_a_bo"], np.float32).reshape(1, D).copy(),
        w1t=np.ascontiguousarray(np.asarray(inputs["L0_ffn_w1_w"], np.float32).T),
        w2t=np.ascontiguousarray(np.asarray(inputs["L0_ffn_w2_w"], np.float32).T),
        h1t=np.ascontiguousarray(np.asarray(inputs["L0_ffn_h1_w"], np.float32).T),
        h2t=np.ascontiguousarray(np.asarray(inputs["L0_ffn_h2_w"], np.float32).T),
        w3t=np.ascontiguousarray(np.asarray(inputs["L0_ffn_w3_w"], np.float32).T),
        w1s=col_layout(inputs["L0_ffn_w1_s"], FC),
        w2s=col_layout(inputs["L0_ffn_w2_s"], FC),
        h1s=col_layout(inputs["L0_ffn_h1_s"], FC),
        h2s=col_layout(inputs["L0_ffn_h2_s"], FC),
        w3s_r=np.asarray(inputs["L0_ffn_w3_s"], np.float32).reshape(1, D).copy(),
        wint=np.ascontiguousarray(np.asarray(inputs["L0_attn_b_W_in"], np.float32).T),
        wgatet=np.ascontiguousarray(np.asarray(inputs["L0_attn_b_W_gate"], np.float32).T),
        rgint=np.ascontiguousarray(np.asarray(inputs["L0_attn_b_rg_in_w"], np.float32).T),
        rggatet=np.ascontiguousarray(np.asarray(inputs["L0_attn_b_rg_gate_w"], np.float32).T),
        woutt=np.ascontiguousarray(np.asarray(inputs["L0_attn_b_W_out"], np.float32).T),
        mgatet=np.ascontiguousarray(np.asarray(inputs["L0_attn_b_mlp_gate_w"], np.float32).T),
        mupt=np.ascontiguousarray(np.asarray(inputs["L0_attn_b_mlp_up_w"], np.float32).T),
        mdownt=np.ascontiguousarray(np.asarray(inputs["L0_attn_b_mlp_down_w"], np.float32).T),
        convw=np.ascontiguousarray(
            np.asarray(inputs["L0_attn_b_conv_w"], np.float32).reshape(GC, 128, 4)
            .transpose(1, 0, 2).reshape(128, GC * 4)),
        convb=col_layout(inputs["L0_attn_b_conv_b"], GC),
        rginb=col_layout(inputs["L0_attn_b_rg_in_b"], GC),
        rggateb=col_layout(inputs["L0_attn_b_rg_gate_b"], GC),
        lam=col_layout(inputs["L0_attn_b_forget_lambda"], GC),
        n1w_r=np.asarray(inputs["L0_attn_b_norm1_w"], np.float32).reshape(1, D).copy(),
        n2w_r=np.asarray(inputs["L0_attn_b_norm2_w"], np.float32).reshape(1, D).copy(),
    )
    in_maps = []
    for c in range(NCORES):
        b, half = c // 2, c % 2
        sel = np.zeros((128, 8), dtype=np.float32)
        if half == 1:
            sel[:, c - 1] = 1.0
        m = dict(common)
        m["xin"] = np.ascontiguousarray(X[b, half * T:(half + 1) * T])
        m["xprev"] = np.ascontiguousarray(X[b, 0:T])
        m["flag"] = np.full((128, 1), float(half), np.float32)
        m["sel"] = sel
        in_maps.append(m)
    return in_maps


_PHASES = ("ffn1", "attn_b", "ffn2")


def build_nc(phases=_PHASES):
    return _make_builder(phases).build()


def kernel(**inputs):
    inputs.get("L0_attn_a_mask")
    in_maps = _shard_inputs(inputs)
    nc = build_nc()
    res = run_bass_kernel_spmd(nc, in_maps, core_ids=list(range(NCORES)))
    outp = np.empty((4, 4096, 1024), dtype=np.float32)
    for c in range(NCORES):
        b, half = c // 2, c % 2
        outp[b, half * T:(half + 1) * T] = res.results[c]["out"]
    return outp


# version 08 (final)
# speedup vs baseline: 1.1681x; 1.0123x over previous
import numpy as np
import ml_dtypes
from contextlib import ExitStack

import concourse.bass as bass
import concourse.mybir as mybir
from concourse.bass_utils import run_bass_kernel_spmd

F32 = mybir.dt.float32
BF16 = mybir.dt.bfloat16
AF = mybir.ActivationFunctionType
ALU = mybir.AluOpType
AX = mybir.AxisListType.X
BF = ml_dtypes.bfloat16

T = 2048
D = 1024
DC = 8
HID = 1536
GC = 12
FH = 4096
FC = 32
NH = 16
QT = 512
NT = 16
EPS = 1e-5
QZEPS = 1e-6
NCORES = 8

LNQ, KFM, VTOK, TNRM, KTOK = 0, 4096, 8192, 12288, 13312
SNP = 14336
KCH, VCH, QCH, SSB, AMSK, VALS, VFM = 0, 2048, 4096, 6144, 8224, 10272, 11296
F_GB, F_TNRM, F_LNX, F_GA = 0, 11264, 12288, 16384
B_N1, B_GG, B_UC, B_SIG, B_SRG, B_TN = 0, 4096, 10240, 16384, 22528, 28672
WVO, WOO, WCOL = 0, 8192, 16384
FW_SL = [0, 8192]
BW_SL = [0, 3072]
BW_SL2 = [0, 3072]
BW_ROWS = 6144
UEXT, S1, S2, S3, S4, S5 = 0, 528, 1040, 1552, 2064, 2576
XA, XB = 0, 1024
ESC1, ESC2 = 2048, 2560
SQSC = 3072


class Sem:
    def __init__(self, ctx, nc, name):
        self.h = ctx.enter_context(nc.semaphore(name))
        self.n = 0

    def inc(self, instr, k=1):
        self.n += k
        instr.then_inc(self.h, k)
        return instr

    def wait(self, eng, n=None):
        eng.wait_ge(self.h, self.n if n is None else n)


_WRAP_OPS = {
    "memset", "memzero", "tensor_copy", "tensor_tensor", "tensor_add",
    "tensor_sub", "tensor_mul", "tensor_max", "tensor_scalar",
    "tensor_single_scalar", "tensor_scalar_add", "tensor_scalar_sub",
    "tensor_scalar_mul", "tensor_scalar_max", "tensor_scalar_min",
    "tensor_relu", "tensor_reduce", "reduce_max", "reduce_sum",
    "scalar_tensor_tensor", "tensor_tensor_scan", "reciprocal", "select",
    "transpose", "bn_stats", "bn_aggr", "activation", "square", "sqrt",
    "mul", "copy", "add", "sign",
}


class _EngProxy:
    def __init__(self, eng, sem):
        self._e = eng
        self._s = sem

    def __getattr__(self, name):
        f = getattr(self._e, name)
        if name in _WRAP_OPS:
            def g(*a, **kw):
                if self._s.n:
                    self._e.wait_ge(self._s.h, self._s.n)
                ins = f(*a, **kw)
                self._s.n += 1
                ins.then_inc(self._s.h, 1)
                return ins
            return g
        return f


class KB:
    def __init__(self, phases=("ffn1", "attn_b", "ffn2")):
        self.phases = phases
        self.nc = bass.Bass("TRN2", target_bir_lowering=False)
        self.xp_free = [0, 0]
        self.wcol_free = [0, 0]

    dbg_lvl = 99

    def load_lnexp(self):
        def f_ac(s):
            s.add_instruction(mybir.InstLoadActFuncSet(
                name=self.nc.get_next_instruction_name(),
                act_func_set_id=6, ins=[], outs=[]))
        self.sec("scalar", f_ac)

    def sec(self, eng, fn):
        if eng == "vector":
            getattr(self.blk, eng)(lambda e: fn(_EngProxy(e, self.dv)))
        elif eng == "scalar":
            getattr(self.blk, eng)(lambda e: fn(_EngProxy(e, self.ac)))
        elif eng == "gpsimd":
            getattr(self.blk, eng)(lambda e: fn(_EngProxy(e, self.gp)))
        else:
            getattr(self.blk, eng)(fn)

    def build(self):
        nc = self.nc
        dt = nc.dram_tensor
        self.xin = dt("xin", [T, D], F32, kind="ExternalInput")
        self.xprev = dt("xprev", [T, D], F32, kind="ExternalInput")
        self.flag = dt("flag", [128, 1], F32, kind="ExternalInput")
        self.sel = dt("sel", [128, 8], F32, kind="ExternalInput")
        self.ident = dt("ident", [128, 128], BF16, kind="ExternalInput")
        self.triu = dt("triu", [128, 128], F32, kind="ExternalInput")
        self.wq = dt("wq", [D, D], F32, kind="ExternalInput")
        self.wk = dt("wk", [D, D], F32, kind="ExternalInput")
        self.wv = dt("wv", [D, D], F32, kind="ExternalInput")
        self.wo = dt("wo", [D, D], F32, kind="ExternalInput")
        self.bq = dt("bq", [128, DC], F32, kind="ExternalInput")
        self.bk = dt("bk", [128, DC], F32, kind="ExternalInput")
        self.bv_r = dt("bv_r", [1, D], F32, kind="ExternalInput")
        self.bo_r = dt("bo_r", [1, D], F32, kind="ExternalInput")
        self.w1t = dt("w1t", [D, FH], F32, kind="ExternalInput")
        self.w2t = dt("w2t", [D, FH], F32, kind="ExternalInput")
        self.h1t = dt("h1t", [FH, FH], F32, kind="ExternalInput")
        self.h2t = dt("h2t", [FH, FH], F32, kind="ExternalInput")
        self.w3t = dt("w3t", [FH, D], F32, kind="ExternalInput")
        self.w1s = dt("w1s", [128, FC], F32, kind="ExternalInput")
        self.w2s = dt("w2s", [128, FC], F32, kind="ExternalInput")
        self.h1s = dt("h1s", [128, FC], F32, kind="ExternalInput")
        self.h2s = dt("h2s", [128, FC], F32, kind="ExternalInput")
        self.w3s_r = dt("w3s_r", [1, D], F32, kind="ExternalInput")
        self.wint = dt("wint", [D, HID], F32, kind="ExternalInput")
        self.wgatet = dt("wgatet", [D, HID], F32, kind="ExternalInput")
        self.rgint = dt("rgint", [HID, HID], F32, kind="ExternalInput")
        self.rggatet = dt("rggatet", [HID, HID], F32, kind="ExternalInput")
        self.woutt = dt("woutt", [HID, D], F32, kind="ExternalInput")
        self.mgatet = dt("mgatet", [D, HID], F32, kind="ExternalInput")
        self.mupt = dt("mupt", [D, HID], F32, kind="ExternalInput")
        self.mdownt = dt("mdownt", [HID, D], F32, kind="ExternalInput")
        self.convw = dt("convw", [128, GC * 4], F32, kind="ExternalInput")
        self.convb = dt("convb", [128, GC], F32, kind="ExternalInput")
        self.rginb = dt("rginb", [128, GC], F32, kind="ExternalInput")
        self.rggateb = dt("rggateb", [128, GC], F32, kind="ExternalInput")
        self.lam = dt("lam", [128, GC], F32, kind="ExternalInput")
        self.n1w_r = dt("n1w_r", [1, D], F32, kind="ExternalInput")
        self.n2w_r = dt("n2w_r", [1, D], F32, kind="ExternalInput")
        self.out = dt("out", [T, D], F32, kind="ExternalOutput")
        self.x2sp = dt("x2sp", [T, D], F32)
        self.ksp = dt("ksp", [16, 128, 1024], BF16)
        self.vsp = dt("vsp", [16, 128, 1024], BF16)
        self.qsp = dt("qsp", [16, 128, 1024], BF16)
        self.p2sp = dt("p2sp", [4, 128, GC * QT], BF16)
        self.c_w1 = dt("c_w1", [FC, 128, DC * 128], BF16)
        self.c_w2 = dt("c_w2", [FC, 128, DC * 128], BF16)
        self.c_h1 = dt("c_h1", [FC, 128, FC * 128], BF16)
        self.c_h2 = dt("c_h2", [FC, 128, FC * 128], BF16)
        self.c_w3 = dt("c_w3", [FC, 128, D], BF16)
        self.cc_ut_in = dt("cc_ut_in", [128, GC * 3], F32)
        self.cc_ut_out = dt("cc_ut_out", [128 * NCORES, GC * 3], F32, addr_space="Shared")
        self.cc_h_in = dt("cc_h_in", [128, GC], F32)
        self.cc_h_out = dt("cc_h_out", [128 * NCORES, GC], F32, addr_space="Shared")

        with ExitStack() as ctx:
            en = ctx.enter_context
            st = nc.sbuf_tensor
            self.xres = en(st("xres", [128, NT * D], F32))
            self.abf = en(st("abf", [128, 32768], BF16))
            self.wbuf = en(st("wbuf", [128, 18432], BF16))
            self.fscr = en(st("fscr", [128, 4096], F32))
            self.cid = en(st("cid", [128, 128], BF16))
            self.ctri = en(st("ctri", [128, 128], F32))
            self.cones = en(st("cones", [128, 1], BF16))
            self.fones = en(st("fones", [128, 1], BF16))
            self.cflag = en(st("cflag", [128, 1], F32))
            self.csel = en(st("csel", [128, 8], F32))
            self.cbq = en(st("cbq", [128, DC], F32))
            self.cbk = en(st("cbk", [128, DC], F32))
            self.cb0 = en(st("cb0", [128, D], F32))
            self.cb1 = en(st("cb1", [128, D], F32))
            self.cw1s = en(st("cw1s", [128, FC], F32))
            self.cw2s = en(st("cw2s", [128, FC], F32))
            self.ch1s = en(st("ch1s", [128, FC], F32))
            self.ch2s = en(st("ch2s", [128, FC], F32))
            self.cconvw = en(st("cconvw", [128, GC * 4], F32))
            self.cconvb = en(st("cconvb", [128, GC], F32))
            self.crginb = en(st("crginb", [128, GC], F32))
            self.crggateb = en(st("crggateb", [128, GC], F32))
            self.clam = en(st("clam", [128, GC], F32))
            self.clamp = en(st("clamp", [128, GC], F32))
            self.czero = en(st("czero", [128, QT], F32))
            self.stats = en(st("stats", [128, 16], F32))
            self.zsb = en(st("zsb", [128, NH], BF16))
            self.ssf = en(st("ssf", [64, 1024], F32))
            self.zsf = en(st("zsf", [64, NH], F32))
            self.uhalo = en(st("uhalo", [128, GC * 3], F32))
            self.utail = en(st("utail", [128, GC * 3], F32))
            self.utg = en(st("utg", [128, 4 * GC * 3], F32))
            self.urecv = en(st("urecv", [128, GC * 3], F32))
            self.hcol = en(st("hcol", [128, GC], F32))
            self.pcol = en(st("pcol", [128, GC], F32))
            self.hg = en(st("hg", [128, 4 * GC], F32))
            self.hrecv = en(st("hrecv", [128, GC], F32))
            self.qzr = en(st("qzr", [128, NH], F32))
            self.cvals = en(st("cvals", [128, 2], F32))
            nc.const_aps.register(F32, EPS, self.cvals[:, 0:1])
            nc.const_aps.register(F32, 1.0, self.cvals[:, 1:2])
            pt = nc.psum_tensor
            self.G = [en(pt(f"G{i}", [128, 512], F32)) for i in range(4)]
            self.PT = en(pt("PTr", [128, 1024], BF16))
            self.PZ = en(pt("PZ", [128, 512], F32))
            self.PSA = en(pt("PSA", [128, 512], F32))
            self.PSB = en(pt("PSB", [128, 512], F32))

            self.pe = Sem(ctx, nc, "pe")
            self.gp = Sem(ctx, nc, "gp")
            self.dv = Sem(ctx, nc, "dv")
            self.ac = Sem(ctx, nc, "ac")
            self.s_x0 = Sem(ctx, nc, "s_x0")
            self.s_const = Sem(ctx, nc, "s_const")
            self.s_wvo = Sem(ctx, nc, "s_wvo")
            self.s_xp = [Sem(ctx, nc, f"s_xp{i}") for i in range(2)]
            self.s_wc = [Sem(ctx, nc, f"s_wc{i}") for i in range(2)]
            self.s_wf = [Sem(ctx, nc, f"s_wf{i}") for i in range(2)]
            self.s_wfc = [Sem(ctx, nc, f"s_wfc{i}") for i in range(2)]
            self.s_w3c = [Sem(ctx, nc, f"s_w3c{i}") for i in range(2)]
            self.s_w3 = [Sem(ctx, nc, f"s_w3{i}") for i in range(2)]
            self.s_wb = [Sem(ctx, nc, f"s_wb{i}") for i in range(2)]
            self.s_rows = Sem(ctx, nc, "s_rows")
            self.s_x2w = Sem(ctx, nc, "s_x2w")
            self.s_x2r = [Sem(ctx, nc, f"s_x2r{i}") for i in range(2)]
            self.s_p2 = Sem(ctx, nc, "s_p2")
            self.s_cc = Sem(ctx, nc, "s_cc")
            self.s_stage = Sem(ctx, nc, "s_stage")
            self.s_gth = Sem(ctx, nc, "s_gth")
            self.s_out = Sem(ctx, nc, "s_out")
            self.s_ksp = Sem(ctx, nc, "s_ksp")
            self.s_x15 = Sem(ctx, nc, "s_x15")
            self.s_kv = [Sem(ctx, nc, f"s_kv{i}") for i in range(2)]

            self.blk = en(nc.Block())
            self.emit_setup()
            self.emit_attn_a()
            if "ffn1" in self.phases:
                self.emit_ffn(first=True)
            if "attn_b" in self.phases:
                self.emit_attn_b()
            if "ffn2" in self.phases:
                self.emit_ffn(first=False)
            self.emit_store()
        return nc

    def emit_setup(self):
        k = self

        def f_sync(sy):
            for t in range(NT):
                k.s_x0.inc(sy.dma_start(out=k.xres[:, t * D:(t + 1) * D],
                                        in_=k.xin[t * 128:(t + 1) * 128, :]), 16)
            for dst, src in [(k.cid, k.ident), (k.ctri, k.triu), (k.cflag, k.flag),
                             (k.csel, k.sel), (k.cbq, k.bq), (k.cbk, k.bk),
                             (k.cw1s, k.w1s), (k.cw2s, k.w2s), (k.ch1s, k.h1s),
                             (k.ch2s, k.h2s), (k.cconvw, k.convw), (k.cconvb, k.convb),
                             (k.crginb, k.rginb), (k.crggateb, k.rggateb),
                             (k.clam, k.lam)]:
                k.s_const.inc(sy.dma_start(out=dst[:], in_=src[:, :]), 16)
            for dst, src in [(k.cb0, k.bv_r), (k.cb1, k.bo_r)]:
                k.s_const.inc(sy.dma_start(out=dst[:], in_=src.ap().broadcast_to([128, D])), 16)
        k.sec("sync", f_sync)

        def f_gp(g):
            for d in range(DC):
                k.s_wvo.inc(g.dma_start(out=k.wbuf[:, WVO + d * D:WVO + (d + 1) * D],
                                        in_=k.wv[d * 128:(d + 1) * 128, :]), 16)
            for v in range(DC):
                k.s_wvo.inc(g.dma_start(out=k.wbuf[:, WOO + v * D:WOO + (v + 1) * D],
                                        in_=k.wo[v * 128:(v + 1) * 128, :]), 16)
        k.sec("gpsimd", f_gp)

        def f_dv(v):
            k.s_const.wait(v)
            v.memset(k.cvals[:, 0:1], EPS)
            v.memset(k.cvals[:, 1:2], 1.0)
            v.memset(k.czero[:], 0.0)
            v.memset(k.cones[:], 1.0)
            v.memset(k.ssf[:], 0.0)
            v.memset(k.zsf[:], 0.0)
            v.memset(k.uhalo[:], 0.0)
            v.memset(k.hcol[:], 0.0)
            v.memset(k.pcol[:], 1.0)
            v.tensor_scalar(k.fones[:], k.cones[:], k.cflag[:, 0:1], None, ALU.mult)
        k.sec("vector", f_dv)

        def f_ac(s):
            k.dv.wait(s)
            s.activation(k.fscr[:, ESC1:ESC1 + GC], k.clam[:], AF.Abs)
            s.activation(k.fscr[:, ESC2:ESC2 + GC], k.fscr[:, ESC1:ESC1 + GC], AF.Exp, scale=-1.0)
        k.sec("scalar", f_ac)

        def f_dv2(v):
            k.ac.wait(v)
            v.tensor_scalar_max(k.fscr[:, ESC1:ESC1 + GC], k.clam[:], 0.0)
        k.sec("vector", f_dv2)

        def f_ac2(s):
            k.dv.wait(s)
            s.activation(k.fscr[:, ESC2:ESC2 + GC], k.fscr[:, ESC2:ESC2 + GC], AF.Ln, bias=1.0)
        k.sec("scalar", f_ac2)

        def f_dv3(v):
            k.ac.wait(v)
            v.tensor_add(k.clamp[:], k.fscr[:, ESC2:ESC2 + GC], k.fscr[:, ESC1:ESC1 + GC])
            v.tensor_scalar_mul(k.clamp[:], k.clamp[:], -8.0)
        k.sec("vector", f_dv3)

    def ln_group(self, items, rms=False, gamma=None):
        k = self
        n = len(items)

        def f_dv(v):
            for i, (src_, pw, _) in enumerate(items):
                if pw is not None:
                    pw[0].wait(v, pw[1])
                if not rms:
                    v.reduce_sum(k.stats[:, i:i + 1], src_, axis=AX)
                v.scalar_tensor_tensor(k.abf[:, self.tnrm:self.tnrm + D], src_, 1.0, src_,
                                       ALU.mult, ALU.mult,
                                       accum_out=k.stats[:, 4 + i:5 + i])
            if rms:
                v.tensor_scalar_mul(k.stats[:, 12:12 + n], k.stats[:, 4:4 + n], 1.0 / D)
            else:
                v.tensor_scalar_mul(k.stats[:, 8:8 + n], k.stats[:, 0:n], 1.0 / D)
                v.tensor_scalar_mul(k.stats[:, 12:12 + n], k.stats[:, 4:4 + n], 1.0 / D)
                v.tensor_tensor(k.stats[:, 0:n], k.stats[:, 8:8 + n],
                                k.stats[:, 8:8 + n], ALU.mult)
                v.tensor_tensor(k.stats[:, 12:12 + n], k.stats[:, 12:12 + n],
                                k.stats[:, 0:n], ALU.subtract)
        k.sec("vector", f_dv)

        def f_ac(s):
            k.dv.wait(s)
            s.activation(k.stats[:, 12:12 + n], k.stats[:, 12:12 + n], AF.Sqrt, bias=EPS)
        k.sec("scalar", f_ac)

        def f_dv2(v):
            k.ac.wait(v)
            v.reciprocal(k.stats[:, 12:12 + n], k.stats[:, 12:12 + n])
        k.sec("vector", f_dv2)

        for i, (src_, _, dst) in enumerate(items):
            def f_dv3(v, i=i, src_=src_):
                if rms:
                    v.tensor_scalar(k.fscr[:, SQSC:SQSC + D], src_,
                                    k.stats[:, 12 + i:13 + i], None, ALU.mult)
                    v.tensor_tensor(k.abf[:, k.tnrm:k.tnrm + D],
                                    k.fscr[:, SQSC:SQSC + D], gamma, ALU.mult)
                else:
                    v.tensor_scalar(k.abf[:, k.tnrm:k.tnrm + D], src_,
                                    k.stats[:, 8 + i:9 + i], k.stats[:, 12 + i:13 + i],
                                    ALU.subtract, ALU.mult)
            k.sec("vector", f_dv3)
            self.tr_tile(self.tnrm, *dst)

    def ln_group_f32(self, items):
        k = self
        n = len(items)

        def f_dv(v):
            for i, (src_, pw, _) in enumerate(items):
                if pw is not None:
                    pw[0].wait(v, pw[1])
                v.reduce_sum(k.stats[:, i:i + 1], src_, axis=AX)
                v.scalar_tensor_tensor(k.abf[:, self.tnrm:self.tnrm + D], src_, 1.0, src_,
                                       ALU.mult, ALU.mult,
                                       accum_out=k.stats[:, 4 + i:5 + i])
            v.tensor_scalar_mul(k.stats[:, 8:8 + n], k.stats[:, 0:n], 1.0 / D)
            v.tensor_scalar_mul(k.stats[:, 12:12 + n], k.stats[:, 4:4 + n], 1.0 / D)
            v.tensor_tensor(k.stats[:, 0:n], k.stats[:, 8:8 + n],
                            k.stats[:, 8:8 + n], ALU.mult)
            v.tensor_tensor(k.stats[:, 12:12 + n], k.stats[:, 12:12 + n],
                            k.stats[:, 0:n], ALU.subtract)
        k.sec("vector", f_dv)

        def f_ac(s):
            k.dv.wait(s)
            s.activation(k.stats[:, 12:12 + n], k.stats[:, 12:12 + n], AF.Sqrt, bias=EPS)
        k.sec("scalar", f_ac)

        def f_dv2(v):
            k.ac.wait(v)
            v.reciprocal(k.stats[:, 12:12 + n], k.stats[:, 12:12 + n])
            for i, (src_, _, dst_) in enumerate(items):
                v.tensor_scalar(dst_, src_, k.stats[:, 8 + i:9 + i],
                                k.stats[:, 12 + i:13 + i], ALU.subtract, ALU.mult)
        k.sec("vector", f_dv2)

    def ln_tile(self, src, out_view, rms=False, gamma=None, pre_wait=None):
        k = self

        def f_dv(v):
            if pre_wait is not None:
                pre_wait[0].wait(v, pre_wait[1])
            if not rms:
                v.reduce_sum(k.stats[:, 0:1], src, axis=AX)
            v.scalar_tensor_tensor(k.fscr[:, SQSC:SQSC + D], src, 1.0, src,
                                            ALU.mult, ALU.mult,
                                            accum_out=k.stats[:, 1:2])
        k.sec("vector", f_dv)

        def f_dv2(v):
            if rms:
                v.tensor_scalar_mul(k.stats[:, 4:5], k.stats[:, 1:2], 1.0 / D)
            else:
                v.tensor_scalar_mul(k.stats[:, 2:3], k.stats[:, 0:1], 1.0 / D)
                v.tensor_scalar_mul(k.stats[:, 3:4], k.stats[:, 1:2], 1.0 / D)
                v.tensor_tensor(k.stats[:, 5:6], k.stats[:, 2:3], k.stats[:, 2:3], ALU.mult)
                v.tensor_tensor(k.stats[:, 4:5], k.stats[:, 3:4],
                                         k.stats[:, 5:6], ALU.subtract)
        k.sec("vector", f_dv2)

        def f_ac(s):
            k.dv.wait(s)
            s.activation(k.stats[:, 6:7], k.stats[:, 4:5], AF.Sqrt, bias=EPS)
        k.sec("scalar", f_ac)

        def f_dv3(v):
            k.ac.wait(v)
            v.reciprocal(k.stats[:, 6:7], k.stats[:, 6:7])
            if rms:
                v.tensor_scalar(k.fscr[:, SQSC:SQSC + D], src, k.stats[:, 6:7], None, ALU.mult)
                v.tensor_tensor(out_view, k.fscr[:, SQSC:SQSC + D], gamma, ALU.mult)
            else:
                v.tensor_scalar(out_view, src, k.stats[:, 2:3], k.stats[:, 6:7],
                                         ALU.subtract, ALU.mult)
        k.sec("vector", f_dv3)

    def tr_tile(self, src_off, fm_off, cs, t_off, nch=DC):
        k = self

        def f_pe(te):
            k.dv.wait(te)
            for d in range(nch):
                ins = te.transpose(k.PT[:, d * 128:(d + 1) * 128],
                                   k.abf[:, src_off + d * 128:src_off + (d + 1) * 128],
                                   k.cid[:])
            k.pe.inc(ins)
        k.sec("tensor", f_pe)

        def f_dv(v):
            k.pe.wait(v)
            dst = k.abf[:, fm_off:fm_off + nch * cs].rearrange(
                "p (c q) -> p c q", q=cs)[:, :, t_off:t_off + 128]
            v.tensor_copy(dst, k.PT[:, 0:nch * 128].rearrange(
                "p (c q) -> p c q", q=128))
        k.sec("vector", f_dv)

    def emit_attn_a(self):
        for q in range(8):
            self._a2_pass1(q)
        self.kv_free = [0, 0]
        for q in range(4):
            self._a2_pass2(q)

    def _a2_pass1(self, q):
        k = self
        own = q >= 4
        qq = q % 4

        k.tnrm = TNRM
        if own:
            items = [(k.xres[:, (qq * 4 + tl) * D:(qq * 4 + tl + 1) * D],
                      (k.s_x0, 16 * NT), (LNQ, QT, tl * 128)) for tl in range(4)]
            self.ln_group(items)
        else:
            for tp in range(2):
                items = []
                for tl in (tp * 2, tp * 2 + 1):
                    t_glob = qq * 4 + tl
                    slot = tl % 2

                    def f_sy(sy, t_glob=t_glob, slot=slot):
                        if k.xp_free[slot] > 0:
                            k.dv.wait(sy, k.xp_free[slot])
                        k.s_xp[slot].inc(sy.dma_start(
                            out=k.fscr[:, slot * 1024:(slot + 1) * 1024],
                            in_=k.xprev[t_glob * 128:(t_glob + 1) * 128, :]), 16)
                    k.sec("sync", f_sy)
                    items.append((k.fscr[:, slot * 1024:(slot + 1) * 1024],
                                  (k.s_xp[slot], k.s_xp[slot].n), (LNQ, QT, tl * 128)))
                self.ln_group(items)
                k.xp_free[0] = k.dv.n
                k.xp_free[1] = k.dv.n

        def f_dv(v):
            k.s_ksp.wait(v)
        k.sec("vector", f_dv)
        self._proj_elu(k.wk, k.cbk, KFM)

        k._a2_lnfree = k.dv.n
        for cl in range(4):
            self._a2_state_chunk(q, own, qq, cl)

        if own:
            def f_dv(v):
                k.s_ksp.wait(v)
            k.sec("vector", f_dv)
            self._proj_elu(k.wq, k.cbq, KFM)
            for cl in range(4):
                idx = qq * 4 + cl

                def f_sy(sy, cl=cl, idx=idx):
                    k.dv.wait(sy)
                    k.s_ksp.inc(sy.dma_start(
                        out=k.qsp[idx],
                        in_=k.abf[:, KFM:KFM + 4096].rearrange(
                            "p (d c) -> p d c", c=QT)[:, :, cl * 128:(cl + 1) * 128]), 16)
                k.sec("sync", f_sy)

    def _a2_state_chunk(self, q, own, qq, cl):
        k = self
        c_off = cl * 128
        g_c = q * 4 + cl

        def f_pe(te, cl=cl):
            k.s_wvo.wait(te)
            k.dv.wait(te, max(k._a2_vfree, k._a2_lnfree))
            for half in range(2):
                ps = k.G[2 + half]
                for d in range(DC):
                    ins = te.matmul(ps[:, :],
                                    k.abf[:, LNQ + d * QT + c_off:LNQ + d * QT + c_off + 128],
                                    k.wbuf[:, WVO + d * D + half * 512:WVO + d * D + (half + 1) * 512],
                                    start=(d == 0), stop=(d == DC - 1))
            k.pe.inc(ins)
        k.sec("tensor", f_pe)
        vproj_done = k.pe.n

        def f_dv(v, cl=cl, own=own, vproj_done=vproj_done):
            k.pe.wait(v, vproj_done)
            for half in range(2):
                ps = k.G[2 + half]
                dst = k.abf[:, VTOK + cl * 1024 + half * 512:VTOK + cl * 1024 + (half + 1) * 512]
                if own:
                    v.tensor_tensor(dst, ps[:, :], k.cb0[:, half * 512:(half + 1) * 512], ALU.add)
                else:
                    v.tensor_tensor(k.fscr[:, ESC1:ESC1 + 512], ps[:, :],
                                    k.cb0[:, half * 512:(half + 1) * 512], ALU.add)
                    v.tensor_scalar(dst, k.fscr[:, ESC1:ESC1 + 512],
                                    k.cflag[:, 0:1], None, ALU.mult)
        k.sec("vector", f_dv)
        v_done = k.dv.n
        k._a2_vfree = v_done

        def f_pe(te, cl=cl, v_done=v_done):
            k.dv.wait(te, v_done)
            for d in range(DC):
                ins = te.transpose(k.PT[:, d * 128:(d + 1) * 128],
                                   k.abf[:, KFM + d * QT + c_off:KFM + d * QT + c_off + 128],
                                   k.cid[:])
            k.pe.inc(ins)
        k.sec("tensor", f_pe)
        kt_pe = k.pe.n

        def f_dv(v, kt_pe=kt_pe):
            k.pe.wait(v, kt_pe)
            v.tensor_copy(k.abf[:, KTOK:KTOK + 1024], k.PT.ap())
        k.sec("vector", f_dv)
        state_gate = k.dv.n

        def f_pe(te, cl=cl, own=own, state_gate=state_gate):
            k.dv.wait(te, state_gate)
            ones = k.cones if own else k.fones
            for h in range(NH):
                ps = k.PSA if h < 8 else k.PSB
                col = (h % 8) * 64
                te.matmul(ps[0:64, col:col + 64],
                          k.abf[:, KTOK + h * 64:KTOK + (h + 1) * 64],
                          k.abf[:, VTOK + cl * 1024 + h * 64:VTOK + cl * 1024 + (h + 1) * 64],
                          start=(h in (0, 8)), stop=(h in (7, 15)))
                ins = te.matmul(k.PZ[0:64, h:h + 1],
                                k.abf[:, KTOK + h * 64:KTOK + (h + 1) * 64],
                                ones[:], start=(h == 0), stop=(h == 15))
            k.pe.inc(ins)
        k.sec("tensor", f_pe)
        st_pe = k.pe.n

        def f_dv(v, st_pe=st_pe, g_c=g_c):
            k.pe.wait(v, st_pe)
            v.tensor_add(k.ssf[:, 0:512], k.ssf[:, 0:512], k.PSA[0:64, :])
            v.tensor_add(k.ssf[:, 512:1024], k.ssf[:, 512:1024], k.PSB[0:64, :])
            v.tensor_add(k.zsf[:, :], k.zsf[:, :], k.PZ[0:64, 0:NH])
            j = g_c - 15
            if 0 <= j <= 15:
                sv = k.abf[0:64, SNP + j * 1040:SNP + (j + 1) * 1040].rearrange(
                    "p (h c) -> p h c", c=65)
                v.tensor_copy(sv[:, :, 0:64], k.ssf.ap().rearrange("p (h c) -> p h c", c=64))
                v.tensor_copy(sv[:, :, 64:65], k.zsf.ap().rearrange("p (h c) -> p h c", c=1))
        k.sec("vector", f_dv)

        if own:
            idx = qq * 4 + cl

            def f_sy(sy, cl=cl, idx=idx):
                k.dv.wait(sy)
                k.s_ksp.inc(sy.dma_start(
                    out=k.ksp[idx],
                    in_=k.abf[:, KFM:KFM + 4096].rearrange(
                        "p (d c) -> p d c", c=QT)[:, :, cl * 128:(cl + 1) * 128]), 16)
                k.s_ksp.inc(sy.dma_start(
                    out=k.vsp[idx],
                    in_=k.abf[:, VTOK + cl * 1024:VTOK + (cl + 1) * 1024]), 16)
            k.sec("sync", f_sy)

    def _a2_pass2(self, q):
        k = self
        ksp_all = k.s_ksp.n
        kv_free = self.kv_free
        for cl in range(4):
            j = q * 4 + cl
            slot = cl % 2
            t_glob = q * 4 + cl

            def f_sy(sy, j=j, slot=slot):
                if q == 0 and cl < 2:
                    k.s_ksp.wait(sy, ksp_all)
                if kv_free[slot] > 0:
                    k.pe.wait(sy, kv_free[slot])
                k.s_kv[slot].inc(sy.dma_start(
                    out=k.abf[:, KCH + slot * 1024:KCH + (slot + 1) * 1024], in_=k.ksp[j]), 16)
                k.s_kv[slot].inc(sy.dma_start(
                    out=k.abf[:, VCH + slot * 1024:VCH + (slot + 1) * 1024], in_=k.vsp[j]), 16)
                k.s_kv[slot].inc(sy.dma_start(
                    out=k.abf[:, QCH + slot * 1024:QCH + (slot + 1) * 1024], in_=k.qsp[j]), 16)
            k.sec("sync", f_sy)
            kvn = k.s_kv[slot].n

            def f_dv(v, j=j):
                if q == 0 and cl == 0:
                    k.s_ksp.wait(v, ksp_all)
                v.tensor_copy(k.abf[0:64, SSB:SSB + 1040],
                              k.abf[0:64, SNP + j * 1040:SNP + (j + 1) * 1040])
                v.tensor_copy(k.abf[64:128, SSB:SSB + 1040],
                              k.abf[0:64, SNP + j * 1040:SNP + (j + 1) * 1040])
            k.sec("vector", f_dv)
            ssb_done = k.dv.n

            ko = KCH + slot * 1024
            vo = VCH + slot * 1024
            qo = QCH + slot * 1024

            for hb in range(2):
                hs = list(range(hb * 8, hb * 8 + 8))

                def f_pe(te, hs=hs, kvn=kvn, slot=slot):
                    k.s_kv[slot].wait(te, kvn)
                    k.dv.wait(te)
                    for h in hs:
                        pc, row = h // 2, (h % 2) * 64
                        ps = k.G[h % 2]
                        col = ((h % 8) // 2) * 128
                        ins = te.matmul(
                            ps[:, col:col + 128],
                            k.abf[row:row + 64, ko + pc * 128:ko + (pc + 1) * 128],
                            k.abf[row:row + 64, qo + pc * 128:qo + (pc + 1) * 128],
                            start=(h in (hs[0], hs[1])), stop=(h in (hs[6], hs[7])))
                    k.pe.inc(ins)
                k.sec("tensor", f_pe)
                a_pe = k.pe.n

                def f_dv(v, hs=hs, a_pe=a_pe):
                    k.pe.wait(v, a_pe)
                    for h in hs:
                        ps = k.G[h % 2]
                        col = ((h % 8) // 2) * 128
                        v.tensor_tensor(k.abf[:, AMSK + h * 128:AMSK + (h + 1) * 128],
                                        ps[:, col:col + 128], k.ctri[:], ALU.mult)
                k.sec("vector", f_dv)
                amask_done = k.dv.n

                def f_pe(te, hs=hs, amask_done=amask_done, ssb_done=ssb_done, slot=slot):
                    k.dv.wait(te, amask_done)
                    for h in hs:
                        ps = k.G[2 + h % 2]
                        col = ((h % 8) // 2) * 65
                        te.matmul(ps[:, col:col + 64],
                                  k.abf[:, AMSK + h * 128:AMSK + (h + 1) * 128],
                                  k.abf[:, vo + h * 64:vo + (h + 1) * 64],
                                  start=(h in (hs[0], hs[1])), stop=False)
                        te.matmul(ps[:, col + 64:col + 65],
                                  k.abf[:, AMSK + h * 128:AMSK + (h + 1) * 128],
                                  k.cones[:], start=False, stop=False)
                    for h in hs:
                        if h % 2 == 0:
                            pc, row = h // 2, 0
                            col = ((h % 8) // 2) * 65
                            te.matmul(k.G[2][:, col:col + 65],
                                      k.abf[row:row + 64, qo + pc * 128:qo + (pc + 1) * 128],
                                      k.abf[row:row + 64, SSB + h * 65:SSB + (h + 1) * 65],
                                      start=False, stop=(h == hs[6]))
                    for h in hs:
                        if h % 2 == 1:
                            pc, row = h // 2, 64
                            col = ((h % 8) // 2) * 65
                            ins = te.matmul(k.G[3][:, col:col + 65],
                                            k.abf[row:row + 64, qo + pc * 128:qo + (pc + 1) * 128],
                                            k.abf[row:row + 64, SSB + h * 65:SSB + (h + 1) * 65],
                                            start=False, stop=(h == hs[7]))
                    k.pe.inc(ins)
                k.sec("tensor", f_pe)
                o_pe = k.pe.n

                def f_dv(v, hs=hs, o_pe=o_pe):
                    k.pe.wait(v, o_pe)
                    for par in range(2):
                        ps = k.G[2 + par]
                        v.tensor_copy(k.qzr[:, par * 4:par * 4 + 4],
                                      ps[:, 0:260].rearrange("p (h c) -> p h c", c=65)[:, :, 64:65])
                    v.tensor_scalar_add(k.qzr[:, 0:8], k.qzr[:, 0:8], QZEPS)
                    v.reciprocal(k.qzr[:, 0:8], k.qzr[:, 0:8])
                    for h in hs:
                        ps = k.G[2 + h % 2]
                        col = ((h % 8) // 2) * 65
                        qi = (h % 2) * 4 + (h % 8) // 2
                        v.tensor_scalar(k.abf[:, VALS + h * 64:VALS + (h + 1) * 64],
                                        ps[:, col:col + 64], k.qzr[:, qi:qi + 1],
                                        None, ALU.mult)
                k.sec("vector", f_dv)

            self.tr_tile(VALS, VFM, 128, 0)
            vfm_done = k.dv.n

            def f_pe(te, vfm_done=vfm_done):
                k.dv.wait(te, vfm_done)
                for half in range(2):
                    ps = k.G[3] if half == 0 else k.G[0]
                    for vc in range(DC):
                        ins = te.matmul(ps[:, :],
                                        k.abf[:, VFM + vc * 128:VFM + (vc + 1) * 128],
                                        k.wbuf[:, WOO + vc * D + half * 512:WOO + vc * D + (half + 1) * 512],
                                        start=(vc == 0), stop=(vc == DC - 1))
                k.pe.inc(ins)
            k.sec("tensor", f_pe)
            wo_pe = k.pe.n
            kv_free[slot] = k.pe.n

            def f_dv(v, t_glob=t_glob, wo_pe=wo_pe):
                k.pe.wait(v, wo_pe)
                for half in range(2):
                    ps = k.G[3] if half == 0 else k.G[0]
                    xsl = k.xres[:, t_glob * D + half * 512:t_glob * D + (half + 1) * 512]
                    v.tensor_tensor(k.fscr[:, ESC1:ESC1 + 512], ps[:, :],
                                    k.cb1[:, half * 512:(half + 1) * 512], ALU.add)
                    v.tensor_add(xsl, xsl, k.fscr[:, ESC1:ESC1 + 512])
            k.sec("vector", f_dv)

    def _proj_elu(self, wsrc, bias, dstoff):
        k = self
        ln_done = k.dv.n
        pg_free = [0, 0]
        pg_ac = [0, 0]
        for dk in range(DC):
            slot = dk % 2

            def f_gp(g, dk=dk, slot=slot):
                if k.wcol_free[slot] > 0:
                    k.pe.wait(g, k.wcol_free[slot])
                k.s_wc[slot].inc(g.dma_start(
                    out=k.wbuf[:, WCOL + slot * 1024:WCOL + (slot + 1) * 1024].rearrange(
                        "p (i o) -> p i o", o=128),
                    in_=wsrc[:, dk * 128:(dk + 1) * 128].rearrange(
                        "(i p) o -> p i o", p=128)), 16)
            k.sec("gpsimd", f_gp)

            def f_pe(te, dk=dk, slot=slot):
                k.s_wc[slot].wait(te)
                k.dv.wait(te, ln_done)
                if pg_free[slot] > 0:
                    k.ac.wait(te, pg_ac[slot])
                    k.dv.wait(te, pg_free[slot])
                ps = k.G[slot]
                for d in range(DC):
                    ins = te.matmul(ps[:, :],
                                    k.wbuf[:, WCOL + slot * 1024 + d * 128:WCOL + slot * 1024 + (d + 1) * 128],
                                    k.abf[:, LNQ + d * QT:LNQ + (d + 1) * QT],
                                    start=(d == 0), stop=(d == DC - 1))
                k.pe.inc(ins)
                k.wcol_free[slot] = k.pe.n
            k.sec("tensor", f_pe)
            pe_done = k.pe.n

            mo, eo = (ESC1, ESC2) if slot == 0 else (SQSC, SQSC + 512)

            def f_dv(v, dk=dk, slot=slot, pe_done=pe_done, mo=mo):
                k.pe.wait(v, pe_done)
                v.tensor_scalar(k.fscr[:, mo:mo + 512], k.G[slot][:, :],
                                bias[:, dk:dk + 1], 0.0, ALU.add, ALU.min)
            k.sec("vector", f_dv)

            def f_ac(s, dk=dk, slot=slot, mo=mo, eo=eo):
                k.dv.wait(s)
                s.activation(k.fscr[:, eo:eo + 512], k.fscr[:, mo:mo + 512], AF.Exp)
            k.sec("scalar", f_ac)

            def f_dv2(v, dk=dk, slot=slot, mo=mo, eo=eo):
                k.ac.wait(v)
                v.tensor_scalar(k.fscr[:, mo:mo + 512], k.G[slot][:, :],
                                bias[:, dk:dk + 1], 0.0, ALU.add, ALU.max)
                v.tensor_add(k.abf[:, dstoff + dk * QT:dstoff + (dk + 1) * QT],
                             k.fscr[:, mo:mo + 512], k.fscr[:, eo:eo + 512])
            k.sec("vector", f_dv2)
            pg_free[slot] = k.dv.n

    def _reload_cb(self, pairs):
        k = self

        def f_sy(sy):
            k.dv.wait(sy)
            for dst, srcr in pairs:
                k.s_const.inc(sy.dma_start(out=dst[:], in_=srcr.ap().broadcast_to([128, D])), 16)
        k.sec("sync", f_sy)
        n = k.s_const.n

        def f_dv(v):
            k.s_const.wait(v, n)
        k.sec("vector", f_dv)

    def emit_ffn(self, first):
        self._reload_cb([(self.cb0, self.w3s_r)])
        pe0 = self.pe.n
        self._wf_free = [pe0, pe0]
        self._w3_free = [pe0, pe0]
        for s in range(4):
            self._ffn_subtile(s, first and s == 0)

    def _ffn_subtile(self, s, write_cache):
        k = self
        pe_barrier = k.pe.n

        k.tnrm = F_TNRM

        def f_dv(v):
            k.pe.wait(v, pe_barrier)
        k.sec("vector", f_dv)
        items = [(k.xres[:, (s * 4 + tl) * D:(s * 4 + tl + 1) * D], None,
                  (F_LNX, QT, tl * 128)) for tl in range(4)]
        self.ln_group(items)
        lnx_done = k.dv.n

        ga_done_marks = []
        for o in range(FC):
            slot = o % 2
            self._ffn_wload(o, slot, [(k.w1t, k.c_w1, 0), (k.w2t, k.c_w2, 1024)],
                            DC, write_cache)
            pa, pb = k.G[0 + 2 * slot], k.G[1 + 2 * slot]

            def f_pe(te, o=o, slot=slot, pa=pa, pb=pb):
                k.s_wf[slot].wait(te)
                k.s_wfc[slot].wait(te)
                k.dv.wait(te, lnx_done)
                if k._ffn_g_free[slot] > 0:
                    k.dv.wait(te, k._ffn_g_free[slot])
                base = FW_SL[slot]
                for d in range(DC):
                    te.matmul(pa[:, :], k.wbuf[:, base + d * 128:base + (d + 1) * 128],
                              k.abf[:, F_LNX + d * QT:F_LNX + (d + 1) * QT],
                              start=(d == 0), stop=(d == DC - 1))
                for d in range(DC):
                    ins = te.matmul(pb[:, :], k.wbuf[:, base + 1024 + d * 128:base + 1024 + (d + 1) * 128],
                                    k.abf[:, F_LNX + d * QT:F_LNX + (d + 1) * QT],
                                    start=(d == 0), stop=(d == DC - 1))
                k.pe.inc(ins)
                k._wf_free[slot] = k.pe.n
            k.sec("tensor", f_pe)
            mm_done = k.pe.n

            scl = ESC1 if o % 2 == 0 else ESC2

            def f_ac(s_, o=o, pa=pa, mm_done=mm_done, scl=scl):
                k.pe.wait(s_, mm_done)
                if k._ffn_scl_free[o % 2] > 0:
                    k.dv.wait(s_, k._ffn_scl_free[o % 2])
                s_.activation(k.fscr[:, scl:scl + 512], pa[:, :], AF.Silu,
                                       scale=k.cw1s[:, o:o + 1])
            k.sec("scalar", f_ac)

            def f_dv(v, o=o, pb=pb, scl=scl):
                k.ac.wait(v)
                v.scalar_tensor_tensor(
                    k.abf[:, F_GA + o * QT:F_GA + (o + 1) * QT], pb[:, :],
                    k.cw2s[:, o:o + 1], k.fscr[:, scl:scl + 512], ALU.mult, ALU.mult)
            k.sec("vector", f_dv)
            k._ffn_scl_free[o % 2] = k.dv.n
            k._ffn_g_free[slot] = k.dv.n
        ga_done = k.dv.n

        gb_free = [0, 0]
        for o in range(FC):
            slot = o % 2
            self._ffn_wload(o, slot, [(k.h1t, k.c_h1, 0)], FC, write_cache)
            ps = k.G[slot]

            def f_pe(te, o=o, slot=slot, ps=ps):
                k.s_wf[slot].wait(te)
                k.s_wfc[slot].wait(te)
                k.dv.wait(te, ga_done)
                if gb_free[slot] > 0:
                    k.ac.wait(te, gb_free[slot])
                base = FW_SL[slot]
                for i in range(FC):
                    ins = te.matmul(ps[:, :], k.wbuf[:, base + i * 128:base + (i + 1) * 128],
                                    k.abf[:, F_GA + i * QT:F_GA + (i + 1) * QT],
                                    start=(i == 0), stop=(i == FC - 1))
                k.pe.inc(ins)
                k._wf_free[slot] = k.pe.n
            k.sec("tensor", f_pe)
            mm_done = k.pe.n

            def f_ac(s_, o=o, ps=ps, mm_done=mm_done):
                k.pe.wait(s_, mm_done)
                s_.activation(k.abf[:, F_GB + o * QT:F_GB + (o + 1) * QT],
                                       ps[:, :], AF.Gelu, scale=k.ch1s[:, o:o + 1])
            k.sec("scalar", f_ac)
            gb_free[slot] = k.ac.n
        gb_done = k.ac.n

        gc_free = [0, 0]
        for o in range(FC):
            slot = o % 2
            self._ffn_wload(o, slot, [(k.h2t, k.c_h2, 0)], FC, write_cache)
            ps = k.G[slot]

            def f_pe(te, o=o, slot=slot, ps=ps):
                k.s_wf[slot].wait(te)
                k.s_wfc[slot].wait(te)
                k.ac.wait(te, gb_done)
                if gc_free[slot] > 0:
                    k.ac.wait(te, gc_free[slot])
                base = FW_SL[slot]
                for i in range(FC):
                    ins = te.matmul(ps[:, :], k.wbuf[:, base + i * 128:base + (i + 1) * 128],
                                    k.abf[:, F_GB + i * QT:F_GB + (i + 1) * QT],
                                    start=(i == 0), stop=(i == FC - 1))
                k.pe.inc(ins)
                k._wf_free[slot] = k.pe.n
            k.sec("tensor", f_pe)
            mm_done = k.pe.n

            def f_ac(s_, o=o, ps=ps, mm_done=mm_done):
                k.pe.wait(s_, mm_done)
                s_.activation(k.abf[:, F_GA + o * QT:F_GA + (o + 1) * QT],
                                       ps[:, :], AF.Gelu, scale=k.ch2s[:, o:o + 1])
            k.sec("scalar", f_ac)
            gc_free[slot] = k.ac.n
        gc_done = k.ac.n

        w3_ev = 0
        for hp in range(2):
            for i in range(FC):
                slot = i % 2
                if write_cache and hp == 0:
                    def f_gp(g, i=i, slot=slot):
                        if k._w3_free[slot] > 0:
                            k.pe.wait(g, k._w3_free[slot])
                        k.s_w3c[slot].wait(g)
                        k.s_w3[slot].inc(g.dma_start(
                            out=k.wbuf[:, WCOL + slot * 1024:WCOL + (slot + 1) * 1024],
                            in_=k.w3t[i * 128:(i + 1) * 128, :]), 16)
                    k.sec("gpsimd", f_gp)

                    def f_sy2(sy, i=i, slot=slot):
                        k.s_w3[slot].wait(sy)
                        k.s_w3c[slot].inc(sy.dma_start(
                            out=k.c_w3[i],
                            in_=k.wbuf[:, WCOL + slot * 1024:WCOL + (slot + 1) * 1024]), 16)
                    k.sec("sync", f_sy2)
                else:
                    def f_sy2(sy, i=i, slot=slot):
                        if k._w3_free[slot] > 0:
                            k.pe.wait(sy, k._w3_free[slot])
                        k.s_w3c[slot].inc(sy.dma_start(
                            out=k.wbuf[:, WCOL + slot * 1024:WCOL + (slot + 1) * 1024],
                            in_=k.c_w3[i]), 16)
                    k.sec("sync", f_sy2)

                def f_pe(te, i=i, slot=slot, hp=hp, w3_ev=w3_ev):
                    k.s_w3[slot].wait(te)
                    k.s_w3c[slot].wait(te)
                    if i == 0:
                        k.ac.wait(te, gc_done)
                        if hp == 0:
                            k.dv.wait(te)
                        else:
                            k.dv.wait(te, w3_ev)
                    for tt_ in range(2):
                        for half in range(2):
                            ps = k.G[tt_ * 2 + half]
                            ins = te.matmul(
                                ps[:, :],
                                k.abf[:, F_GA + i * QT + (hp * 2 + tt_) * 128:F_GA + i * QT + (hp * 2 + tt_) * 128 + 128],
                                k.wbuf[:, WCOL + slot * 1024 + half * 512:WCOL + slot * 1024 + (half + 1) * 512],
                                start=(i == 0), stop=(i == FC - 1))
                    k.pe.inc(ins)
                k.sec("tensor", f_pe)
                k._w3_free[slot] = k.pe.n
            w3_pe = k.pe.n

            def f_dv(v, s=s, hp=hp, w3_pe=w3_pe):
                k.pe.wait(v, w3_pe)
                for tt_ in range(2):
                    t_glob = s * 4 + hp * 2 + tt_
                    for half in range(2):
                        ps = k.G[tt_ * 2 + half]
                        xsl = k.xres[:, t_glob * D + half * 512:t_glob * D + (half + 1) * 512]
                        v.tensor_tensor(k.fscr[:, ESC1:ESC1 + 512], ps[:, :],
                                        k.cb0[:, half * 512:(half + 1) * 512], ALU.mult)
                        ins = v.tensor_add(xsl, xsl, k.fscr[:, ESC1:ESC1 + 512])
            k.sec("vector", f_dv)
            w3_ev = k.dv.n

    _ffn_g_free = None
    _ffn_scl_free = None
    _w3_free = None

    def _ffn_wload(self, o, slot, specs, itiles, write_cache):
        k = self
        if write_cache:
            def f_gp(g, o=o, slot=slot):
                if k._wf_free[slot] > 0:
                    k.pe.wait(g, k._wf_free[slot])
                k.s_wfc[slot].wait(g)
                for wsrc, _, doff in specs:
                    for i8 in range(0, itiles, 8):
                        k.s_wf[slot].inc(g.dma_start(
                            out=k.wbuf[:, FW_SL[slot] + doff + i8 * 128:FW_SL[slot] + doff + (i8 + 8) * 128].rearrange(
                                "p (i c) -> p i c", c=128),
                            in_=wsrc[i8 * 128:(i8 + 8) * 128, o * 128:(o + 1) * 128].rearrange(
                                "(i p) c -> p i c", p=128)), 16)
            k.sec("gpsimd", f_gp)

            def f_sy(sy, o=o, slot=slot):
                k.s_wf[slot].wait(sy)
                for _, cache, doff in specs:
                    k.s_wfc[slot].inc(sy.dma_start(
                        out=cache[o],
                        in_=k.wbuf[:, FW_SL[slot] + doff:FW_SL[slot] + doff + itiles * 128]), 16)
            k.sec("sync", f_sy)
        else:
            def f_sy(sy, o=o, slot=slot):
                if k._wf_free[slot] > 0:
                    k.pe.wait(sy, k._wf_free[slot])
                for _, cache, doff in specs:
                    k.s_wfc[slot].inc(sy.dma_start(
                        out=k.wbuf[:, FW_SL[slot] + doff:FW_SL[slot] + doff + itiles * 128],
                        in_=cache[o]), 16)
            k.sec("sync", f_sy)

    _wf_free = None

    def emit_attn_b(self):
        k = self
        pe_barrier = k.pe.n
        self._bw_free = [pe_barrier, pe_barrier]
        self._bw2_free = [pe_barrier, pe_barrier]
        self._reload_cb([(k.cb0, k.n1w_r), (k.cb1, k.n2w_r)])

        k.tnrm = B_TN

        def f_sy(sy):
            k.dv.wait(sy)
            k.s_x15.inc(sy.dma_start(out=k.x2sp[15 * 128:16 * 128, :],
                                     in_=k.xres[:, 15 * D:16 * D]), 16)
            for t in range(15):
                k.s_x2w.inc(sy.dma_start(out=k.x2sp[t * 128:(t + 1) * 128, :],
                                         in_=k.xres[:, t * D:(t + 1) * D]), 16)
        k.sec("sync", f_sy)
        x2w_n = k.s_x2w.n

        self.ln_group_f32([(k.xres[:, 15 * D:16 * D], (k.s_x15, k.s_x15.n),
                            k.xres[:, 15 * D:16 * D])])

        self.ln_group([(k.xres[:, 15 * D:16 * D], None, (B_N1, 128, 0))],
                      rms=True, gamma=k.cb0[:])
        n1e_done = k.dv.n
        for g_ in range(GC):
            slot = g_ % 2
            self._bcol_load(k.wint, g_, slot, DC)

            def f_pe(te, g_=g_, slot=slot):
                k.s_wb[slot].wait(te)
                k.dv.wait(te)
                for d in range(DC):
                    ins = te.matmul(k.G[0][:, 0:128],
                                    k.wbuf[:, BW_SL[slot] + d * 128:BW_SL[slot] + (d + 1) * 128],
                                    k.abf[:, B_N1 + d * 128:B_N1 + (d + 1) * 128],
                                    start=(d == 0), stop=(d == DC - 1))
                k.pe.inc(ins)
                k._bw_free[slot] = k.pe.n
            k.sec("tensor", f_pe)
            u15_pe = k.pe.n

            def f_dv(v, g_=g_, u15_pe=u15_pe):
                k.pe.wait(v, u15_pe)
                v.tensor_copy(k.utail[:, g_ * 3:(g_ + 1) * 3], k.G[0][:, 125:128])
            k.sec("vector", f_dv)
        utail_done = k.dv.n

        def f_gp(g):
            k.dv.wait(g, utail_done)
            k.s_stage.inc(g.dma_start(out=k.cc_ut_in[:, :], in_=k.utail[:]), 16)
            k.s_stage.wait(g)
            k.s_cc.inc(g.collective_compute(
                "AllGather", ALU.bypass, replica_groups=[list(range(NCORES))],
                ins=[k.cc_ut_in[:, :]], outs=[k.cc_ut_out[:, :]]), 1)
            k.s_cc.wait(g)
            for j, r in enumerate((0, 2, 4, 6)):
                k.s_gth.inc(g.dma_start(out=k.utg[:, j * GC * 3:(j + 1) * GC * 3],
                                        in_=k.cc_ut_out[r * 128:(r + 1) * 128, :]), 16)
        k.sec("gpsimd", f_gp)
        gth_n = k.s_gth.n

        for t0 in range(0, 15, 4):
            tls = list(range(t0, min(t0 + 4, 15)))
            self.ln_group_f32([(k.xres[:, t * D:(t + 1) * D],
                                (k.s_x2w, x2w_n) if t0 == 0 else None,
                                k.xres[:, t * D:(t + 1) * D]) for t in tls])

        def f_dv(v):
            k.s_gth.wait(v, gth_n)
            v.tensor_scalar(k.urecv[:], k.utg[:, 0:GC * 3], k.csel[:, 0:1], None, ALU.mult)
            for j, r in ((1, 2), (2, 4), (3, 6)):
                v.scalar_tensor_tensor(k.urecv[:], k.utg[:, j * GC * 3:(j + 1) * GC * 3],
                                             k.csel[:, r:r + 1], k.urecv[:], ALU.mult, ALU.add)
        k.sec("vector", f_dv)
        urecv_done = k.dv.n

        def f_gp(g):
            k.pe.wait(g, pe_barrier)
            for g_ in range(GC):
                k.s_rows.inc(g.dma_start(
                    out=k.wbuf[:, BW_ROWS + g_ * D:BW_ROWS + (g_ + 1) * D],
                    in_=k.woutt[g_ * 128:(g_ + 1) * 128, :]), 16)
        k.sec("gpsimd", f_gp)
        wout_n = k.s_rows.n

        for q in range(4):
            self._attnb_quarter(q, urecv_done, wout_n)

        hcol_done = k.dv.n

        def f_gp(g):
            k.dv.wait(g, hcol_done)
            k.s_stage.inc(g.dma_start(out=k.cc_h_in[:, :], in_=k.hcol[:]), 16)
            k.s_stage.wait(g)
            k.s_cc.inc(g.collective_compute(
                "AllGather", ALU.bypass, replica_groups=[list(range(NCORES))],
                ins=[k.cc_h_in[:, :]], outs=[k.cc_h_out[:, :]]), 1)
            k.s_cc.wait(g)
            for j, r in enumerate((0, 2, 4, 6)):
                k.s_gth.inc(g.dma_start(out=k.hg[:, j * GC:(j + 1) * GC],
                                        in_=k.cc_h_out[r * 128:(r + 1) * 128, :]), 16)
        k.sec("gpsimd", f_gp)
        gth_n2 = k.s_gth.n

        def f_dv(v):
            k.s_gth.wait(v, gth_n2)
            v.tensor_scalar(k.hrecv[:], k.hg[:, 0:GC], k.csel[:, 0:1], None, ALU.mult)
            for j, r in ((1, 2), (2, 4), (3, 6)):
                v.scalar_tensor_tensor(k.hrecv[:], k.hg[:, j * GC:(j + 1) * GC],
                                             k.csel[:, r:r + 1], k.hrecv[:], ALU.mult, ALU.add)
        k.sec("vector", f_dv)
        hrecv_done = k.dv.n

        for q in range(4):
            def f_sy(sy, q=q):
                k.s_p2.wait(sy)
                k.pe.wait(sy)
                k.s_p2.inc(sy.dma_start(out=k.abf[:, B_UC:B_UC + GC * QT], in_=k.p2sp[q]), 16)
            k.sec("sync", f_sy)
            p2n = k.s_p2.n

            def f_dv(v, q=q, p2n=p2n):
                k.s_p2.wait(v, p2n)
                k.pe.wait(v)
                k.dv.wait(v, hrecv_done)
                for g_ in range(GC):
                    ins = v.tensor_scalar(k.abf[:, B_SIG + g_ * QT:B_SIG + (g_ + 1) * QT],
                                          k.abf[:, B_UC + g_ * QT:B_UC + (g_ + 1) * QT],
                                          k.hrecv[:, g_:g_ + 1], None, ALU.mult)
            k.sec("vector", f_dv)
            self._mix_mms(q, k.dv.n, wout_n)

        def f_gp(g):
            k.pe.wait(g)
            for g_ in range(GC):
                k.s_rows.inc(g.dma_start(
                    out=k.wbuf[:, BW_ROWS + g_ * D:BW_ROWS + (g_ + 1) * D],
                    in_=k.mdownt[g_ * 128:(g_ + 1) * 128, :]), 16)
        k.sec("gpsimd", f_gp)
        mdown_n = k.s_rows.n

        for q in range(4):
            self._mlp_quarter(q, mdown_n)

        self.emit_x2_add()


    def ln_tile_f32(self, src, dst, pre_wait=None):
        k = self

        def f_dv(v):
            if pre_wait is not None:
                pre_wait[0].wait(v, pre_wait[1])
            v.reduce_sum(k.stats[:, 0:1], src, axis=AX)
            v.scalar_tensor_tensor(k.fscr[:, SQSC:SQSC + D], src, 1.0, src,
                                            ALU.mult, ALU.mult, accum_out=k.stats[:, 1:2])
        k.sec("vector", f_dv)

        def f_dv2(v):
            v.tensor_scalar_mul(k.stats[:, 2:3], k.stats[:, 0:1], 1.0 / D)
            v.tensor_scalar_mul(k.stats[:, 3:4], k.stats[:, 1:2], 1.0 / D)
            v.tensor_tensor(k.stats[:, 5:6], k.stats[:, 2:3], k.stats[:, 2:3], ALU.mult)
            v.tensor_tensor(k.stats[:, 4:5], k.stats[:, 3:4],
                                     k.stats[:, 5:6], ALU.subtract)
        k.sec("vector", f_dv2)

        def f_ac(s):
            k.dv.wait(s)
            s.activation(k.stats[:, 6:7], k.stats[:, 4:5], AF.Ln, bias=EPS)
            s.activation(k.stats[:, 6:7], k.stats[:, 6:7], AF.Exp, scale=-0.5)
        k.sec("scalar", f_ac)

        def f_dv3(v):
            k.ac.wait(v)
            v.tensor_scalar(dst, src, k.stats[:, 2:3], k.stats[:, 6:7],
                                     ALU.subtract, ALU.mult)
        k.sec("vector", f_dv3)

    _bw_free = None

    def _bcol_load2col(self, wsrc, g0, slot, itiles):
        k = self

        def f_gp(g, g0=g0, slot=slot):
            if k._bw_free[slot] > 0:
                k.pe.wait(g, k._bw_free[slot])
            k.s_wb[slot].inc(g.dma_start(
                out=k.wbuf[:, BW_SL[slot]:BW_SL[slot] + itiles * 256].rearrange(
                    "p (i c) -> p i c", c=256),
                in_=wsrc[0:itiles * 128, g0 * 128:(g0 + 2) * 128].rearrange(
                    "(i p) c -> p i c", p=128)), 16)
        k.sec("gpsimd", f_gp)

    def _bcol_load(self, wsrc, g_, slot, itiles):
        k = self

        def f_gp(g, g_=g_, slot=slot):
            if k._bw_free[slot] > 0:
                k.pe.wait(g, k._bw_free[slot])
            k.s_wb[slot].inc(g.dma_start(
                out=k.wbuf[:, BW_SL[slot]:BW_SL[slot] + itiles * 128].rearrange(
                    "p (i c) -> p i c", c=128),
                in_=wsrc[0:itiles * 128, g_ * 128:(g_ + 1) * 128].rearrange(
                    "(i p) c -> p i c", p=128)), 16)
        k.sec("gpsimd", f_gp)

    def _attnb_quarter(self, q, urecv_done, wout_n):
        k = self
        k.tnrm = B_TN
        self.ln_group([(k.xres[:, (q * 4 + tl) * D:(q * 4 + tl + 1) * D], None,
                        (B_N1, QT, tl * 128)) for tl in range(4)],
                      rms=True, gamma=k.cb0[:])
        n1_done = k.dv.n

        for g_ in range(GC):
            slot = g_ % 2
            self._bcol_load(k.wgatet, g_, slot, DC)
            ps = k.G[slot]

            def f_pe(te, g_=g_, slot=slot, ps=ps):
                k.s_wb[slot].wait(te)
                k.dv.wait(te, n1_done)
                if k._bg_free[slot] > 0:
                    k.ac.wait(te, k._bg_free[slot])
                for d in range(DC):
                    ins = te.matmul(ps[:, :],
                                    k.wbuf[:, BW_SL[slot] + d * 128:BW_SL[slot] + (d + 1) * 128],
                                    k.abf[:, B_N1 + d * QT:B_N1 + (d + 1) * QT],
                                    start=(d == 0), stop=(d == DC - 1))
                k.pe.inc(ins)
                k._bw_free[slot] = k.pe.n
            k.sec("tensor", f_pe)
            mm_n = k.pe.n

            def f_ac(s_, g_=g_, ps=ps, mm_n=mm_n):
                k.pe.wait(s_, mm_n)
                s_.activation(k.abf[:, B_GG + g_ * QT:B_GG + (g_ + 1) * QT],
                                       ps[:, :], AF.Gelu)
            k.sec("scalar", f_ac)
            k._bg_free[slot] = k.ac.n
        gate_done = k.ac.n

        for g_ in range(GC):
            slot = (g_ // 2) % 2
            sub = g_ % 2
            if sub == 0:
                self._bcol_load2col(k.wint, g_, slot, DC)
            ps = k.G[2 + g_ % 2]

            def f_pe(te, g_=g_, slot=slot, sub=sub, ps=ps):
                k.s_wb[slot].wait(te)
                k.dv.wait(te, n1_done)
                if k._bu_free[g_ % 2] > 0:
                    k.dv.wait(te, k._bu_free[g_ % 2])
                for d in range(DC):
                    ins = te.matmul(ps[:, :],
                                    k.wbuf[:, BW_SL[slot] + d * 256 + sub * 128:BW_SL[slot] + d * 256 + sub * 128 + 128],
                                    k.abf[:, B_N1 + d * QT:B_N1 + (d + 1) * QT],
                                    start=(d == 0), stop=(d == DC - 1))
                k.pe.inc(ins)
                if sub == 1:
                    k._bw_free[slot] = k.pe.n
            k.sec("tensor", f_pe)
            mm_n = k.pe.n

            def f_dv(v, g_=g_, q=q, ps=ps, mm_n=mm_n):
                k.pe.wait(v, mm_n)
                if g_ == 0 and q > 0:
                    k.s_p2.wait(v)
                v.tensor_copy(k.fscr[:, UEXT + 3:UEXT + 3 + QT], ps[:, :])
            k.sec("vector", f_dv)
            k._bu_free[slot] = k.dv.n

            def f_gp(g, g_=g_, q=q):
                if q == 0:
                    g.tensor_copy(k.fscr[:, UEXT:UEXT + 3], k.urecv[:, g_ * 3:(g_ + 1) * 3])
                else:
                    g.tensor_copy(k.fscr[:, UEXT:UEXT + 3], k.uhalo[:, g_ * 3:(g_ + 1) * 3])
                g.tensor_copy(k.uhalo[:, g_ * 3:(g_ + 1) * 3], k.fscr[:, UEXT + QT:UEXT + QT + 3])
                g.tensor_scalar(k.fscr[:, S1:S1 + QT], k.fscr[:, UEXT + 3:UEXT + 3 + QT],
                                k.cconvw[:, g_ * 4 + 3:g_ * 4 + 4],
                                k.cconvb[:, g_:g_ + 1], ALU.mult, ALU.add)
                for kk in (2, 1):
                    g.tensor_scalar(k.fscr[:, ESC1:ESC1 + QT],
                                    k.fscr[:, UEXT + kk:UEXT + kk + QT],
                                    k.cconvw[:, g_ * 4 + kk:g_ * 4 + kk + 1], None, ALU.mult)
                    g.tensor_tensor(k.fscr[:, S1:S1 + QT], k.fscr[:, S1:S1 + QT],
                                    k.fscr[:, ESC1:ESC1 + QT], ALU.add)
                g.tensor_scalar(k.fscr[:, ESC1:ESC1 + QT], k.fscr[:, UEXT:UEXT + QT],
                                k.cconvw[:, g_ * 4:g_ * 4 + 1], None, ALU.mult)
                g.tensor_tensor(k.abf[:, B_UC + g_ * QT:B_UC + (g_ + 1) * QT],
                                k.fscr[:, S1:S1 + QT],
                                k.fscr[:, ESC1:ESC1 + QT], ALU.add)
            k.sec("vector", f_gp)
            k._bu_free[g_ % 2] = k.dv.n
        uc_done = k.dv.n

        for g_ in range(GC):
            slot = g_ % 2
            self._bcol_load(k.rgint, g_, slot, GC)
            self._bcol_load2(k.rggatet, g_, slot, GC)
            pa, pb = k.G[0 + 2 * slot], k.G[1 + 2 * slot]

            def f_pe(te, g_=g_, slot=slot, pa=pa, pb=pb):
                k.s_wb[slot].wait(te)
                k.dv.wait(te, uc_done)
                k.ac.wait(te, max(gate_done, k._brg_free[slot]))
                for h in range(GC):
                    te.matmul(pa[:, :], k.wbuf[:, BW_SL2[slot] + h * 128:BW_SL2[slot] + (h + 1) * 128],
                              k.abf[:, B_UC + h * QT:B_UC + (h + 1) * QT],
                              start=(h == 0), stop=(h == GC - 1))
                for h in range(GC):
                    ins = te.matmul(pb[:, :], k.wbuf[:, BW_SL2[slot] + 1536 + h * 128:BW_SL2[slot] + 1536 + (h + 1) * 128],
                                    k.abf[:, B_UC + h * QT:B_UC + (h + 1) * QT],
                                    start=(h == 0), stop=(h == GC - 1))
                k.pe.inc(ins)
                k._bw_free[slot] = k.pe.n
                k._bw2_free[slot] = k.pe.n
            k.sec("tensor", f_pe)
            mm_n = k.pe.n

            def f_ac(s_, g_=g_, pa=pa, pb=pb, mm_n=mm_n):
                k.pe.wait(s_, mm_n)
                s_.activation(k.abf[:, B_SIG + g_ * QT:B_SIG + (g_ + 1) * QT], pa[:, :],
                              AF.Sigmoid, bias=k.crginb[:, g_:g_ + 1])
                s_.activation(k.abf[:, B_SRG + g_ * QT:B_SRG + (g_ + 1) * QT], pb[:, :],
                                       AF.Sigmoid, bias=k.crggateb[:, g_:g_ + 1])
            k.sec("scalar", f_ac)
            k._brg_free[slot] = k.ac.n
        rg_done = k.ac.n
        rg_pe = k.pe.n

        ABANK = [k.G[0], k.G[1], k.G[2], k.G[3], k.PZ, k.PSA]
        SLOTS = [(S2, S3, S4, S5), (0, 512, 3072, 3584)]
        la_done = [0, 0]
        set_free = [0, 0]
        for hf in range(2):
            gs = list(range(hf * 6, hf * 6 + 6))
            for j, g_ in enumerate(gs):
                sl_la = SLOTS[j % 2][0]

                def f_gp(g, g_=g_, q=q, j=j, sl_la=sl_la, hf=hf):
                    if g_ == 0:
                        k.ac.wait(g, rg_done)
                        k.pe.wait(g, rg_pe)
                        if q > 0:
                            k.s_p2.wait(g)
                    if la_done[j % 2] > 0:
                        k.ac.wait(g, la_done[j % 2])
                    g.tensor_scalar(k.fscr[:, sl_la:sl_la + QT],
                                    k.abf[:, B_SRG + g_ * QT:B_SRG + (g_ + 1) * QT],
                                    k.clamp[:, g_:g_ + 1], None, ALU.mult)
                k.sec("vector", f_gp)

                def f_ac(s_, j=j, sl_la=sl_la, hf=hf):
                    k.dv.wait(s_)
                    if j == 0:
                        k.pe.wait(s_)
                        k.dv.wait(s_)
                    s_.activation(ABANK[j][:, 0:QT], k.fscr[:, sl_la:sl_la + QT], AF.Exp)
                k.sec("scalar", f_ac)
                la_done[j % 2] = k.ac.n

            for j, g_ in enumerate(gs):
                s_la, s_z, s_b, s_h = SLOTS[j % 2]

                def f_ac(s_, j=j, s_z=s_z):
                    if set_free[j % 2] > 0:
                        k.dv.wait(s_, set_free[j % 2])
                    s_.activation(k.fscr[:, s_z:s_z + QT], ABANK[j][:, 0:QT], AF.Square)
                k.sec("scalar", f_ac)

                def f_gp(g, j=j, s_z=s_z):
                    k.ac.wait(g)
                    g.tensor_scalar(k.fscr[:, s_z:s_z + QT], k.fscr[:, s_z:s_z + QT],
                                    -1.0, 1.0 + 1e-6, ALU.mult, ALU.add)
                k.sec("vector", f_gp)

                def f_ac2(s_, j=j, s_z=s_z, s_b=s_b):
                    k.dv.wait(s_)
                    s_.activation(k.fscr[:, s_b:s_b + QT], k.fscr[:, s_z:s_z + QT], AF.Sqrt)
                k.sec("scalar", f_ac2)

                def f_gp2(g, j=j, g_=g_, s_z=s_z, s_b=s_b):
                    k.ac.wait(g)
                    g.tensor_tensor(k.fscr[:, s_z:s_z + QT], k.fscr[:, s_b:s_b + QT],
                                    k.abf[:, B_SIG + g_ * QT:B_SIG + (g_ + 1) * QT], ALU.mult)
                    g.tensor_tensor(k.fscr[:, s_z:s_z + QT], k.fscr[:, s_z:s_z + QT],
                                    k.abf[:, B_UC + g_ * QT:B_UC + (g_ + 1) * QT], ALU.mult)
                k.sec("vector", f_gp2)

                def f_dv(v, j=j, g_=g_, s_z=s_z, s_b=s_b, s_h=s_h):
                    v.tensor_tensor_scan(k.fscr[:, s_h:s_h + QT], ABANK[j][:, 0:QT],
                                         k.fscr[:, s_z:s_z + QT], k.hcol[:, g_:g_ + 1],
                                         ALU.mult, ALU.add)
                    v.tensor_copy(k.hcol[:, g_:g_ + 1], k.fscr[:, s_h + QT - 1:s_h + QT])
                    v.tensor_tensor_scan(k.fscr[:, s_b:s_b + QT], ABANK[j][:, 0:QT],
                                         k.czero[:, 0:QT], k.pcol[:, g_:g_ + 1],
                                         ALU.mult, ALU.add)
                    v.tensor_copy(k.pcol[:, g_:g_ + 1], k.fscr[:, s_b + QT - 1:s_b + QT])
                k.sec("vector", f_dv)

                def f_gp3(g, j=j, g_=g_, s_b=s_b, s_h=s_h):
                    g.tensor_tensor(k.abf[:, B_SIG + g_ * QT:B_SIG + (g_ + 1) * QT],
                                    k.abf[:, B_GG + g_ * QT:B_GG + (g_ + 1) * QT],
                                    k.fscr[:, s_h:s_h + QT], ALU.mult)
                    g.tensor_tensor(k.abf[:, B_UC + g_ * QT:B_UC + (g_ + 1) * QT],
                                    k.abf[:, B_GG + g_ * QT:B_GG + (g_ + 1) * QT],
                                    k.fscr[:, s_b:s_b + QT], ALU.mult)
                k.sec("vector", f_gp3)
                set_free[j % 2] = k.dv.n
        scan_done = k.dv.n

        def f_sy(sy, q=q):
            k.dv.wait(sy, scan_done)
            k.s_p2.inc(sy.dma_start(out=k.p2sp[q], in_=k.abf[:, B_UC:B_UC + GC * QT]), 16)
        k.sec("sync", f_sy)

        self._mix_mms(q, scan_done, wout_n)

    _bg_free = None
    _bu_free = None
    _uext_free = 0
    _a2_vfree = 0
    _a2_lnfree = 0
    _brg_free = None
    _bw2_free = None

    def _bcol_load2(self, wsrc, g_, slot, itiles):
        k = self

        def f_gp(g, g_=g_, slot=slot):
            if k._bw2_free[slot] > 0:
                k.pe.wait(g, k._bw2_free[slot])
            k.s_wb[slot].inc(g.dma_start(
                out=k.wbuf[:, BW_SL2[slot] + 1536:BW_SL2[slot] + 1536 + itiles * 128].rearrange(
                    "p (i c) -> p i c", c=128),
                in_=wsrc[0:itiles * 128, g_ * 128:(g_ + 1) * 128].rearrange(
                    "(i p) c -> p i c", p=128)), 16)
        k.sec("gpsimd", f_gp)

    def _mix_mms(self, q, src_done, wout_n):
        k = self
        ev = [0, 0]
        for tl in range(4):
            t_glob = q * 4 + tl

            def f_pe(te, tl=tl, src_done=src_done, ev0=None):
                k.dv.wait(te, src_done)
                k.s_rows.wait(te, wout_n)
                if tl < 2:
                    k.dv.wait(te)
                elif ev[tl % 2] > 0:
                    k.dv.wait(te, ev[tl % 2])
                for half in range(2):
                    ps = k.G[tl % 2 * 2 + half]
                    for g_ in range(GC):
                        ins = te.matmul(ps[:, :],
                                        k.abf[:, B_SIG + g_ * QT + tl * 128:B_SIG + g_ * QT + tl * 128 + 128],
                                        k.wbuf[:, BW_ROWS + g_ * D + half * 512:BW_ROWS + g_ * D + (half + 1) * 512],
                                        start=(g_ == 0), stop=(g_ == GC - 1))
                k.pe.inc(ins)
            k.sec("tensor", f_pe)
            mm_n = k.pe.n

            def f_dv(v, t_glob=t_glob, tl=tl, mm_n=mm_n):
                k.pe.wait(v, mm_n)
                for half in range(2):
                    ps = k.G[tl % 2 * 2 + half]
                    xsl = k.xres[:, t_glob * D + half * 512:t_glob * D + (half + 1) * 512]
                    ins = v.tensor_add(xsl, xsl, ps[:, :])
            k.sec("vector", f_dv)
            ev[tl % 2] = k.dv.n

    def _mlp_quarter(self, q, mdown_n):
        k = self
        k.tnrm = B_TN
        self.ln_group([(k.xres[:, (q * 4 + tl) * D:(q * 4 + tl + 1) * D], None,
                        (B_N1, QT, tl * 128)) for tl in range(4)],
                      rms=True, gamma=k.cb1[:])
        n2_done = k.dv.n

        for g_ in range(GC):
            slot = g_ % 2
            self._bcol_load(k.mgatet, g_, slot, DC)
            self._bcol_load2(k.mupt, g_, slot, DC)
            pa, pb = k.G[0 + 2 * slot], k.G[1 + 2 * slot]

            def f_pe(te, g_=g_, slot=slot, pa=pa, pb=pb):
                k.s_wb[slot].wait(te)
                k.dv.wait(te, n2_done)
                if k._bmg_free[slot] > 0:
                    k.dv.wait(te, k._bmg_free[slot])
                for d in range(DC):
                    te.matmul(pa[:, :], k.wbuf[:, BW_SL[slot] + d * 128:BW_SL[slot] + (d + 1) * 128],
                              k.abf[:, B_N1 + d * QT:B_N1 + (d + 1) * QT],
                              start=(d == 0), stop=(d == DC - 1))
                for d in range(DC):
                    ins = te.matmul(pb[:, :], k.wbuf[:, BW_SL2[slot] + 1536 + d * 128:BW_SL2[slot] + 1536 + (d + 1) * 128],
                                    k.abf[:, B_N1 + d * QT:B_N1 + (d + 1) * QT],
                                    start=(d == 0), stop=(d == DC - 1))
                k.pe.inc(ins)
                k._bw_free[slot] = k.pe.n
                k._bw2_free[slot] = k.pe.n
            k.sec("tensor", f_pe)
            mm_n = k.pe.n

            def f_ac(s_, g_=g_, pa=pa, mm_n=mm_n):
                k.pe.wait(s_, mm_n)
                s_.activation(k.abf[:, B_GG + g_ * QT:B_GG + (g_ + 1) * QT],
                                       pa[:, :], AF.Gelu)
            k.sec("scalar", f_ac)

            def f_dv(v, g_=g_, pb=pb):
                k.ac.wait(v)
                v.tensor_tensor(k.abf[:, B_UC + g_ * QT:B_UC + (g_ + 1) * QT],
                                         pb[:, :], k.abf[:, B_GG + g_ * QT:B_GG + (g_ + 1) * QT],
                                         ALU.mult)
            k.sec("vector", f_dv)
            k._bmg_free[slot] = k.dv.n
        prd_done = k.dv.n

        dev = [0, 0]
        for tl in range(4):
            t_glob = q * 4 + tl

            def f_pe(te, tl=tl, prd_done=prd_done):
                k.dv.wait(te, prd_done)
                k.s_rows.wait(te, mdown_n)
                if tl < 2:
                    k.dv.wait(te)
                elif dev[tl % 2] > 0:
                    k.dv.wait(te, dev[tl % 2])
                for half in range(2):
                    ps = k.G[tl % 2 * 2 + half]
                    for g_ in range(GC):
                        ins = te.matmul(ps[:, :],
                                        k.abf[:, B_UC + g_ * QT + tl * 128:B_UC + g_ * QT + tl * 128 + 128],
                                        k.wbuf[:, BW_ROWS + g_ * D + half * 512:BW_ROWS + g_ * D + (half + 1) * 512],
                                        start=(g_ == 0), stop=(g_ == GC - 1))
                k.pe.inc(ins)
            k.sec("tensor", f_pe)
            mm_n = k.pe.n

            def f_dv(v, t_glob=t_glob, tl=tl, mm_n=mm_n):
                k.pe.wait(v, mm_n)
                for half in range(2):
                    ps = k.G[tl % 2 * 2 + half]
                    xsl = k.xres[:, t_glob * D + half * 512:t_glob * D + (half + 1) * 512]
                    ins = v.tensor_add(xsl, xsl, ps[:, :])
            k.sec("vector", f_dv)
            dev[tl % 2] = k.dv.n

    _bmg_free = None
    _x2add = None

    def emit_x2_add(self):
        k = self
        add_done = [0, 0]
        dv_base = k.dv.n
        for t in range(NT):
            def f_sy(sy, t=t):
                if t == 0:
                    k.dv.wait(sy, dv_base)
                if add_done[t % 2] > 0:
                    k.dv.wait(sy, add_done[t % 2])
                k.s_x2r[t % 2].inc(sy.dma_start(
                    out=k.fscr[:, XA + (t % 2) * 1024:XA + (t % 2) * 1024 + 1024],
                    in_=k.x2sp[t * 128:(t + 1) * 128, :]), 16)
            k.sec("sync", f_sy)
            x2n = k.s_x2r[t % 2].n

            def f_dv(v, t=t, x2n=x2n):
                k.s_x2r[t % 2].wait(v, x2n)
                xsl = k.xres[:, t * D:(t + 1) * D]
                v.tensor_add(xsl, xsl,
                                      k.fscr[:, XA + (t % 2) * 1024:XA + (t % 2) * 1024 + 1024])
            k.sec("vector", f_dv)
            add_done[t % 2] = k.dv.n

    def emit_store(self):
        k = self

        def f_sy(sy):
            k.dv.wait(sy)
            for t in range(NT):
                k.s_out.inc(sy.dma_start(out=k.out[t * 128:(t + 1) * 128, :],
                                         in_=k.xres[:, t * D:(t + 1) * D]), 16)
            k.s_out.wait(sy)
        k.sec("sync", f_sy)

    _x2gate = None


def _make_builder(phases):
    kb = KB(phases)
    kb._ffn_g_free = [0, 0]
    kb._ffn_scl_free = [0, 0]
    kb._w3_free = [0, 0]
    kb._wf_free = [0, 0]
    kb._bw_free = [0, 0]
    kb._bw2_free = [0, 0]
    kb._bg_free = [0, 0]
    kb._bu_free = [0, 0]
    kb._brg_free = [0, 0]
    kb._bmg_free = [0, 0]
    kb._x2gate = [0] * NT
    kb._x2add = [0] * NT
    return kb


def _shard_inputs(inputs):
    X = np.ascontiguousarray(np.asarray(inputs["L0_attn_a_x"], np.float32))

    def col_layout(v, chunks):
        return np.ascontiguousarray(np.asarray(v, np.float32).reshape(chunks, 128).T)

    common = dict(
        ident=np.eye(128, dtype=BF),
        triu=np.triu(np.ones((128, 128), dtype=np.float32)),
        wq=np.ascontiguousarray(np.asarray(inputs["L0_attn_a_Wq"], np.float32)),
        wk=np.ascontiguousarray(np.asarray(inputs["L0_attn_a_Wk"], np.float32)),
        wv=np.ascontiguousarray(np.asarray(inputs["L0_attn_a_Wv"], np.float32)),
        wo=np.ascontiguousarray(np.asarray(inputs["L0_attn_a_Wo"], np.float32)),
        bq=col_layout(inputs["L0_attn_a_bq"], DC),
        bk=col_layout(inputs["L0_attn_a_bk"], DC),
        bv_r=np.asarray(inputs["L0_attn_a_bv"], np.float32).reshape(1, D).copy(),
        bo_r=np.asarray(inputs["L0_attn_a_bo"], np.float32).reshape(1, D).copy(),
        w1t=np.ascontiguousarray(np.asarray(inputs["L0_ffn_w1_w"], np.float32).T),
        w2t=np.ascontiguousarray(np.asarray(inputs["L0_ffn_w2_w"], np.float32).T),
        h1t=np.ascontiguousarray(np.asarray(inputs["L0_ffn_h1_w"], np.float32).T),
        h2t=np.ascontiguousarray(np.asarray(inputs["L0_ffn_h2_w"], np.float32).T),
        w3t=np.ascontiguousarray(np.asarray(inputs["L0_ffn_w3_w"], np.float32).T),
        w1s=col_layout(inputs["L0_ffn_w1_s"], FC),
        w2s=col_layout(inputs["L0_ffn_w2_s"], FC),
        h1s=col_layout(inputs["L0_ffn_h1_s"], FC),
        h2s=col_layout(inputs["L0_ffn_h2_s"], FC),
        w3s_r=np.asarray(inputs["L0_ffn_w3_s"], np.float32).reshape(1, D).copy(),
        wint=np.ascontiguousarray(np.asarray(inputs["L0_attn_b_W_in"], np.float32).T),
        wgatet=np.ascontiguousarray(np.asarray(inputs["L0_attn_b_W_gate"], np.float32).T),
        rgint=np.ascontiguousarray(np.asarray(inputs["L0_attn_b_rg_in_w"], np.float32).T),
        rggatet=np.ascontiguousarray(np.asarray(inputs["L0_attn_b_rg_gate_w"], np.float32).T),
        woutt=np.ascontiguousarray(np.asarray(inputs["L0_attn_b_W_out"], np.float32).T),
        mgatet=np.ascontiguousarray(np.asarray(inputs["L0_attn_b_mlp_gate_w"], np.float32).T),
        mupt=np.ascontiguousarray(np.asarray(inputs["L0_attn_b_mlp_up_w"], np.float32).T),
        mdownt=np.ascontiguousarray(np.asarray(inputs["L0_attn_b_mlp_down_w"], np.float32).T),
        convw=np.ascontiguousarray(
            np.asarray(inputs["L0_attn_b_conv_w"], np.float32).reshape(GC, 128, 4)
            .transpose(1, 0, 2).reshape(128, GC * 4)),
        convb=col_layout(inputs["L0_attn_b_conv_b"], GC),
        rginb=col_layout(inputs["L0_attn_b_rg_in_b"], GC),
        rggateb=col_layout(inputs["L0_attn_b_rg_gate_b"], GC),
        lam=col_layout(inputs["L0_attn_b_forget_lambda"], GC),
        n1w_r=np.asarray(inputs["L0_attn_b_norm1_w"], np.float32).reshape(1, D).copy(),
        n2w_r=np.asarray(inputs["L0_attn_b_norm2_w"], np.float32).reshape(1, D).copy(),
    )
    in_maps = []
    for c in range(NCORES):
        b, half = c // 2, c % 2
        sel = np.zeros((128, 8), dtype=np.float32)
        if half == 1:
            sel[:, c - 1] = 1.0
        m = dict(common)
        m["xin"] = np.ascontiguousarray(X[b, half * T:(half + 1) * T])
        m["xprev"] = np.ascontiguousarray(X[b, 0:T])
        m["flag"] = np.full((128, 1), float(half), np.float32)
        m["sel"] = sel
        in_maps.append(m)
    return in_maps


_PHASES = ("ffn1", "attn_b", "ffn2")


def build_nc(phases=_PHASES):
    return _make_builder(phases).build()


def kernel(**inputs):
    inputs.get("L0_attn_a_mask")
    in_maps = _shard_inputs(inputs)
    nc = build_nc()
    res = run_bass_kernel_spmd(nc, in_maps, core_ids=list(range(NCORES)))
    outp = np.empty((4, 4096, 1024), dtype=np.float32)
    for c in range(NCORES):
        b, half = c // 2, c % 2
        outp[b, half * T:(half + 1) * T] = res.results[c]["out"]
    return outp
